# version 09
# baseline (speedup 1.0000x reference)
import math
from contextlib import ExitStack
import numpy as np
import ml_dtypes
import concourse.bass as bass
import concourse.mybir as mybir
from concourse.bass_utils import run_bass_kernel_spmd

F32 = mybir.dt.float32
BF16 = mybir.dt.bfloat16
I32 = mybir.dt.int32
I8 = mybir.dt.int8
AF = mybir.ActivationFunctionType
ALU = mybir.AluOpType
AX = mybir.AxisListType
BF16NP = ml_dtypes.bfloat16

N_CORES = 8
D = 1024
T = 2048
NT = T // 128
DC = D // 128

WHOLE = "__whole__"
SERIALIZE_DMA = True
DMA_LANES = 8
L2_BISECT = ""


class _Instr:
    __slots__ = ("eng", "fn", "reads", "writes", "dma", "idx", "deps",
                 "milestone", "val", "tag")

    def __init__(self, eng, fn, reads, writes, dma, idx, tag):
        self.eng = eng
        self.fn = fn
        self.reads = reads
        self.writes = writes
        self.dma = dma
        self.idx = idx
        self.deps = []
        self.milestone = False
        self.val = None
        self.tag = tag


def _regions(lst):
    out = []
    for it in lst:
        if isinstance(it, tuple) and len(it) == 2 and not isinstance(it[0], str):
            out.append((id(it[0]), it[1]))
        else:
            out.append((id(it), WHOLE))
    return out


class Prog:

    ENGINES = ("tensor", "vector", "scalar", "gpsimd", "sync")

    def __init__(self, nc):
        self.nc = nc
        self.instrs = {e: [] for e in self.ENGINES}
        self.order = []
        self.mem = {}
        self.n = 0
        self.freed = {"sbuf": [], "psum": []}

    def _buf_state(self, bid):
        st = self.mem.get(bid)
        if st is None:
            st = {"whole_w": None, "w": {}, "whole_r": [], "r": {},
                  "phantom": []}
            self.mem[bid] = st
        return st

    def note_alloc(self, buf, space):
        st = self._buf_state(id(buf))
        st["phantom"] = list(self.freed[space])

    def note_free(self, bufs, space):
        acc = self.freed[space]
        for buf in bufs:
            st = self.mem.get(id(buf))
            if st is None:
                continue
            if st["whole_w"] is not None:
                acc.append(st["whole_w"])
            acc.extend(st["w"].values())
            acc.extend(st["whole_r"])
            for lst in st["r"].values():
                acc.extend(lst)

    def transition(self, dsts, srcs):
        acc = []
        for buf in list(srcs) + list(dsts):
            st = self.mem.get(id(buf))
            if st is None:
                continue
            if st["whole_w"] is not None:
                acc.append(st["whole_w"])
            acc.extend(st["w"].values())
            acc.extend(st["whole_r"])
            for lst in st["r"].values():
                acc.extend(lst)
            acc.extend(st["phantom"])
            self.mem[id(buf)] = {"whole_w": None, "w": {}, "whole_r": [],
                                 "r": {}, "phantom": []}
        for dst in dsts:
            self.mem[id(dst)] = {"whole_w": None, "w": {}, "whole_r": [],
                                 "r": {}, "phantom": list(acc)}

    def emit(self, eng, fn, reads=(), writes=(), dma=False, tag=""):
        ins = _Instr(eng, fn, _regions(reads), _regions(writes), dma,
                     self.n, tag)
        self.n += 1
        deps = []
        for bid, key in ins.reads:
            st = self._buf_state(bid)
            if st["whole_w"] is not None:
                deps.append(st["whole_w"])
            if key == WHOLE:
                deps.extend(st["w"].values())
            elif key in st["w"]:
                deps.append(st["w"][key])
        for bid, key in ins.writes:
            st = self._buf_state(bid)
            deps.extend(st["phantom"])
            if st["whole_w"] is not None:
                deps.append(st["whole_w"])
            if key == WHOLE:
                deps.extend(st["w"].values())
            elif key in st["w"]:
                deps.append(st["w"][key])
            deps.extend(st["whole_r"])
            if key == WHOLE:
                for lst in st["r"].values():
                    deps.extend(lst)
            elif key in st["r"]:
                deps.extend(st["r"][key])
        ins.deps = []
        seen = set()
        for d in deps:
            if d is ins or id(d) in seen:
                continue
            if d.eng == "tensor" and eng == "tensor" and not d.dma and not dma:
                continue
            seen.add(id(d))
            ins.deps.append(d)
        for bid, key in ins.reads:
            st = self._buf_state(bid)
            if key == WHOLE:
                st["whole_r"].append(ins)
            else:
                st["r"].setdefault(key, []).append(ins)
        for bid, key in ins.writes:
            st = self._buf_state(bid)
            if key == WHOLE:
                st["whole_w"] = ins
                st["w"].clear()
                st["whole_r"] = []
                st["r"].clear()
                st["phantom"] = []
            else:
                st["w"][key] = ins
                if key in st["r"]:
                    st["r"][key] = []
        self.instrs[eng].append(ins)
        self.order.append(ins)
        return ins

    def dma(self, eng, out, in_, reads=(), writes=(), tag="", **kw):
        return self.emit(
            eng, lambda e: e.dma_start(out=out, in_=in_, **kw),
            reads=reads, writes=writes, dma=True, tag=tag)

    def finalize(self, stack):
        nc = self.nc
        for ins in self.order:
            for d in ins.deps:
                d.milestone = True
        for ins in self.order:
            if ins.dma:
                ins.milestone = True
        R = DMA_LANES
        sems = {}
        for eng in self.ENGINES:
            if any((not i.dma) and i.milestone for i in self.instrs[eng]):
                sems[(eng, False)] = stack.enter_context(
                    nc.semaphore(f"sc_{eng}"))
            if any(i.dma for i in self.instrs[eng]):
                for r in range(R):
                    sems[(eng, True, r)] = stack.enter_context(
                        nc.semaphore(f"sd_{eng}{r}"))
        counts = {}
        dmacnt = {e: 0 for e in self.ENGINES}
        for ins in self.order:
            if ins.dma:
                i = dmacnt[ins.eng]
                dmacnt[ins.eng] += 1
                ins.val = ((ins.eng, True, i % R), 16 * (i // R + 1))
                counts[ins.val[0]] = ins.val[1]
            elif ins.milestone:
                k = (ins.eng, False)
                counts[k] = counts.get(k, 0) + 1
                ins.val = (k, counts[k])
        blk = stack.enter_context(nc.Block())
        for eng in self.ENGINES:
            ilist = self.instrs[eng]
            if not ilist:
                continue

            def body(engobj, ilist=ilist, eng=eng):
                waited = {}

                def wait(key, val):
                    if waited.get(key, 0) < val:
                        engobj.wait_ge(sems[key], val)
                        waited[key] = val

                ndma = 0
                nms = 0
                for ins in ilist:
                    for d in ins.deps:
                        wait(*d.val)
                    if ins.dma:
                        lane = ndma % R
                        prev = ndma // R
                        if prev > 0:
                            wait((eng, True, lane), 16 * prev)
                        ndma += 1
                    if (ins.milestone and not ins.dma and eng != "tensor"
                            and nms > 0):
                        wait((eng, False), nms)
                    if ins.milestone and not ins.dma:
                        nms += 1
                    inst = ins.fn(engobj)
                    if ins.milestone or ins.dma:
                        key = ins.val[0]
                        inst.then_inc(sems[key], 16 if ins.dma else 1)
                for r in range(R):
                    k = (eng, True, r)
                    if k in counts:
                        wait(k, counts[k])

            getattr(blk, eng)(body)
        return blk



class Ctx:
    def __init__(self, nc, p, stack):
        self.nc = nc
        self.p = p
        self.stack = stack
        self.ident = self.sb("c_ident", [128, 128], BF16)
        self.ones_row = self.sb("c_ones_row", [1, 128], F32)
        p.emit("gpsimd", lambda e: e.memset(self.ident[:], 0.0),
               writes=[self.ident])
        p.emit("gpsimd", lambda e: e.affine_select(
            out=self.ident[:], in_=self.ident[:],
            compare_op=ALU.not_equal, fill=1.0, base=0,
            pattern=[[-1, 128]], channel_multiplier=1),
            reads=[self.ident], writes=[self.ident])
        p.emit("gpsimd", lambda e: e.memset(self.ones_row[:], 1.0),
               writes=[self.ones_row])
        self._const_cols = {}

    def ident32(self):
        if getattr(self, "_id32", None) is None:
            t = self.sb("c_ident32", [128, 128], F32)
            self.p.emit("gpsimd", lambda e: e.memset(t[:], 0.0), writes=[t])
            self.p.emit("gpsimd", lambda e: e.affine_select(
                out=t[:], in_=t[:], compare_op=ALU.not_equal, fill=1.0,
                base=0, pattern=[[-1, 128]], channel_multiplier=1),
                reads=[t], writes=[t])
            self._id32 = t
        return self._id32

    def const_col(self, value):
        value = float(value)
        t = self._const_cols.get(value)
        if t is None:
            t = self.sb(f"c_col_{len(self._const_cols)}", [128, 1], F32)
            self.p.emit("gpsimd", lambda e: e.memset(t[:], value),
                        writes=[t])
            self._const_cols[value] = t
        return t

    def sb(self, name, shape, dt, stack=None):
        side = "left" if stack is None else "right"
        t = (stack or self.stack).enter_context(
            self.nc.sbuf_tensor(name, shape, dt, side=side))
        self.p.note_alloc(t, "sbuf")
        if stack is not None:
            stack.callback(lambda: self.p.note_free([t], "sbuf"))
        return t

    def _ps(self, name, shape, dt, stack):
        side = "left" if stack is None else "right"
        t = (stack or self.stack).enter_context(
            self.nc.psum_tensor(name, shape, dt, side=side))
        self.p.note_alloc(t, "psum")
        if stack is not None:
            stack.callback(lambda: self.p.note_free([t], "psum"))
        return t

    def ps_f32(self, name, stack=None):
        return self._ps(name, [128, 512], F32, stack)

    def ps_bf16(self, name, stack=None):
        return self._ps(name, [128, 1024], BF16, stack)

    def broadcast_row(self, row_sb, F, name, psum_pool, scale=1.0,
                      stack=None, row_reads=()):
        p = self.p
        bc = self.sb(name, [128, F], F32, stack=stack)
        for j0 in range(0, F, 512):
            w = min(512, F - j0)
            bank = psum_pool[(j0 // 512) % len(psum_pool)]
            p.emit("tensor", lambda e, j0=j0, w=w, bank=bank: e.matmul(
                bank[:, 0:w], self.ones_row[:], row_sb[0:1, j0:j0 + w],
                start=True, stop=True),
                reads=[self.ones_row] + list(row_reads), writes=[(bank, "s")],
                tag=f"{name}_mm")
            p.emit("vector", lambda e, j0=j0, w=w, bank=bank: e.tensor_copy(
                bc[:, j0:j0 + w], bank[:, 0:w]),
                reads=[(bank, "s")], writes=[(bc, f"c{j0}")],
                tag=f"{name}_cp")
        return bc



def ln_stats(ctx, src, src_tiles, src_keyf, stats, eps, prefix, scratch_stack):
    p = ctx.p
    n = len(src_tiles)
    bn = ctx.sb(f"{prefix}_bn", [128, 12], F32, stack=scratch_stack)
    mv = ctx.sb(f"{prefix}_mv", [128, 2 * n], F32, stack=scratch_stack)
    std = ctx.sb(f"{prefix}_std", [128, n], F32, stack=scratch_stack)
    for j, i in enumerate(src_tiles):
        sk = (src, src_keyf(i))
        p.emit("vector", lambda e, i=i: e.bn_stats(
            bn[:, 0:6], src[:, i * D:i * D + 512]),
            reads=[sk], writes=[(bn, 0)], tag=f"{prefix}_bns{i}a")
        p.emit("vector", lambda e, i=i: e.bn_stats(
            bn[:, 6:12], src[:, i * D + 512:(i + 1) * D]),
            reads=[sk], writes=[(bn, 1)], tag=f"{prefix}_bns{i}b")
        p.emit("vector", lambda e, j=j: e.bn_aggr(
            mv[:, 2 * j:2 * j + 2], bn[:, 0:12]),
            reads=[(bn, 0), (bn, 1)], writes=[(mv, j)],
            tag=f"{prefix}_bna{j}")
    mvreads = [(mv, j) for j in range(n)]
    epsc = ctx.const_col(eps)
    p.emit("scalar", lambda e: e.activation(
        out=std[:, 0:n], in_=mv[:, 1:2 * n:2], func=AF.Sqrt, bias=epsc[:]),
        reads=mvreads + [epsc], writes=[std], tag=f"{prefix}_sqrt")
    for j in range(n):
        p.emit("vector", lambda e, j=j: e.reciprocal(
            stats[:, 2 * j:2 * j + 1], std[:, j:j + 1]),
            reads=[std], writes=[(stats, 2 * j)], tag=f"{prefix}_rcp{j}")
        p.emit("vector", lambda e, j=j: e.scalar_tensor_tensor(
            out=stats[:, 2 * j + 1:2 * j + 2], in0=mv[:, 2 * j:2 * j + 1],
            scalar=-1.0, in1=stats[:, 2 * j:2 * j + 1],
            op0=ALU.mult, op1=ALU.mult),
            reads=[(mv, j), (stats, 2 * j)], writes=[(stats, 2 * j + 1)],
            tag=f"{prefix}_nmr{j}")


def ln_apply_bf16(ctx, src, src_tiles, src_keyf, stats, dst, dst_tiles,
                  prefix):
    p = ctx.p
    for j, (i, o) in enumerate(zip(src_tiles, dst_tiles)):
        p.emit("scalar", lambda e, i=i, o=o, j=j: e.activation(
            out=dst[:, o * D:(o + 1) * D], in_=src[:, i * D:(i + 1) * D],
            func=AF.Identity, scale=stats[:, 2 * j:2 * j + 1],
            bias=stats[:, 2 * j + 1:2 * j + 2]),
            reads=[(src, src_keyf(i)), (stats, 2 * j), (stats, 2 * j + 1)],
            writes=[(dst, f"t{o}")], tag=f"{prefix}_ln{o}")


def transpose_tm_to_fm(ctx, src_tm, src_tiles, dst_fm, dst_cols0, ntok_fm,
                       prefix, psum_pool, eng="vector"):
    p = ctx.p
    nb = len(psum_pool)
    bsel = 0
    n = len(src_tiles)
    for c in range(DC):
        for j0 in range(0, n, 4):
            grp = list(range(j0, min(j0 + 4, n)))
            bank = psum_pool[bsel % nb]
            bsel += 1
            for slot, j in enumerate(grp):
                i = src_tiles[j]
                p.emit("tensor", lambda e, i=i, c=c, bank=bank, slot=slot:
                       e.transpose(
                           bank[:, slot * 128:slot * 128 + 128],
                           src_tm[:, i * D + c * 128:i * D + (c + 1) * 128],
                           ctx.ident[:]),
                       reads=[(src_tm, f"t{i}"), ctx.ident],
                       writes=[(bank, "s")],
                       tag=f"{prefix}_tr{i}_{c}")
            w = len(grp) * 128
            p.emit(eng, lambda e, j0=j0, c=c, bank=bank, w=w:
                   e.tensor_copy(
                       dst_fm[:, c * ntok_fm + dst_cols0 + j0 * 128:
                              c * ntok_fm + dst_cols0 + j0 * 128 + w],
                       bank[:, 0:w]),
                   reads=[(bank, "s")],
                   writes=[(dst_fm, f"c{c}g{dst_cols0 + j0 * 128}")],
                   tag=f"{prefix}_tre{c}_{j0}")


def ln_prologue(ctx, x_src, x_tiles, x_keyf, yb_fm, fm_cols0, ntok_fm,
                prefix, psum_pool, scope, want_tm=False):
    n = len(x_tiles)
    stats = ctx.sb(f"{prefix}_st1", [128, 2 * n], F32, stack=scope)
    ln_stats(ctx, x_src, x_tiles, x_keyf, stats, 1e-5, f"{prefix}_s1", scope)
    yb_tm = ctx.sb(f"{prefix}_ybtm", [128, n * D], BF16, stack=scope)
    ln_apply_bf16(ctx, x_src, x_tiles, x_keyf, stats, yb_tm, list(range(n)),
                  prefix)
    transpose_tm_to_fm(ctx, yb_tm, list(range(n)), yb_fm, fm_cols0, ntok_fm,
                       prefix, psum_pool)
    return yb_tm if want_tm else None


def sandwich_epilogue(ctx, z_sb, z_tiles, x_buf, x_tiles, x_keyf, prefix,
                      scope):
    p = ctx.p
    n = len(z_tiles)
    stats = ctx.sb(f"{prefix}_st2", [128, 2 * n], F32, stack=scope)
    ln_stats(ctx, z_sb, z_tiles, lambda i: f"t{i}", stats, 1e-5,
             f"{prefix}_s2", scope)
    for j, (zi, xi) in enumerate(zip(z_tiles, x_tiles)):
        p.emit("vector", lambda e, xi=xi, j=j: e.tensor_scalar(
            out=x_buf[:, xi * D:(xi + 1) * D],
            in0=x_buf[:, xi * D:(xi + 1) * D],
            scalar1=stats[:, 2 * j + 1:2 * j + 2], scalar2=None, op0=ALU.add),
            reads=[(x_buf, x_keyf(xi)), (stats, 2 * j + 1)],
            writes=[(x_buf, x_keyf(xi))], tag=f"{prefix}_radd{xi}")
        p.emit("vector", lambda e, zi=zi, xi=xi, j=j: e.scalar_tensor_tensor(
            out=x_buf[:, xi * D:(xi + 1) * D], in0=z_sb[:, zi * D:(zi + 1) * D],
            scalar=stats[:, 2 * j:2 * j + 1],
            in1=x_buf[:, xi * D:(xi + 1) * D],
            op0=ALU.mult, op1=ALU.add),
            reads=[(z_sb, f"t{zi}"), (stats, 2 * j), (x_buf, x_keyf(xi))],
            writes=[(x_buf, x_keyf(xi))], tag=f"{prefix}_rfma{xi}")


RG_PAIRS = [[0, 1], [2, 3], [4, 5], [6, 7]]


def emit_exchange_ag(ctx, src_tm, ntiles, exin_t, exout_t, name):
    p = ctx.p
    er = exin_t.ap().rearrange("(n p) d -> n p d", p=128)
    for i in range(ntiles):
        p.dma("sync", er[i], src_tm[:, i * D:(i + 1) * D],
              reads=[(src_tm, f"t{i}")], writes=[(exin_t, i)],
              tag=f"{name}_o{i}")
    p.emit("gpsimd", lambda e: e.collective_compute(
        "AllGather", ALU.bypass, replica_groups=RG_PAIRS,
        ins=[exin_t.ap().opt()], outs=[exout_t.ap().opt()]),
        reads=[(exin_t, i) for i in range(ntiles)], writes=[exout_t],
        tag=f"{name}_cc")


def mm_accum(p, bank, bank_key, pairs, n0, n1, tag="", accum_more=False,
             m=128):
    last = len(pairs) - 1
    for k, (lt, rh, rr) in enumerate(pairs):
        st = (k == 0) and not accum_more
        p.emit("tensor", lambda e, lt=lt, rh=rh, k=k, st=st: e.matmul(
            bank[0:m, n0:n1], lt, rh, start=st, stop=(k == last)),
            reads=rr, writes=[(bank, bank_key)], tag=f"{tag}_k{k}")



class GateScratch:
    def __init__(self, ctx, E, prefix, scope):
        self.E = E
        self.nm = ctx.sb(f"{prefix}_gsnm", [128, 1], F32, stack=scope)
        self.e = ctx.sb(f"{prefix}_gse", [128, E], F32, stack=scope)
        self.m1 = ctx.sb(f"{prefix}_gsm1", [128, E], F32, stack=scope)
        self.e2 = ctx.sb(f"{prefix}_gse2", [128, E], F32, stack=scope)
        self.m2 = ctx.sb(f"{prefix}_gsm2", [128, 1], F32, stack=scope)
        self.w = ctx.sb(f"{prefix}_gsw", [128, E], F32, stack=scope)
        self.d = ctx.sb(f"{prefix}_gsd", [128, 1], F32, stack=scope)
        self.d2 = ctx.sb(f"{prefix}_gsd2", [128, 1], F32, stack=scope)
        self.rc = ctx.sb(f"{prefix}_gsrc", [128, 1], F32, stack=scope)


def emit_gating(ctx, sc, logits, lkey, lcol, ew, ewkey, ecol, variant,
                prefix, n_true=None, scale_out=1.0):
    p = ctx.p
    E = sc.E
    L = logits[:, lcol:lcol + E]
    lr = (logits, lkey)
    p.emit("vector", lambda e: e.reduce_max(sc.nm[:], L, axis=AX.X,
                                            negate=True),
           reads=[lr], writes=[sc.nm], tag=f"{prefix}_gmax")
    p.emit("scalar", lambda e: e.activation(out=sc.e[:], in_=L, func=AF.Exp,
                                            bias=sc.nm[:]),
           reads=[lr, sc.nm], writes=[sc.e], tag=f"{prefix}_gexp")
    if variant == "eps":
        p.emit("vector", lambda e: e.tensor_reduce(
            sc.d[:], sc.e[:], axis=AX.X, op=ALU.add),
            reads=[sc.e], writes=[sc.d], tag=f"{prefix}_gZ")
        p.emit("vector", lambda e: e.reciprocal(sc.d2[:], sc.d[:]),
               reads=[sc.d], writes=[sc.d2], tag=f"{prefix}_gZr")
        p.emit("vector", lambda e: e.tensor_scalar(
            out=sc.e[:], in0=sc.e[:], scalar1=sc.d2[:], scalar2=None,
            op0=ALU.mult),
            reads=[sc.e, sc.d2], writes=[sc.e], tag=f"{prefix}_gp")
        top1 = sc.d2
        p.emit("vector", lambda e: e.tensor_scalar(
            out=sc.m1[:], in0=sc.e[:], scalar1=top1[:], scalar2=None,
            op0=ALU.is_ge),
            reads=[sc.e, top1], writes=[sc.m1], tag=f"{prefix}_gm1")
    else:
        p.emit("vector", lambda e: e.tensor_single_scalar(
            out=sc.m1[:], in_=sc.e[:], scalar=1.0, op=ALU.is_ge),
            reads=[sc.e], writes=[sc.m1], tag=f"{prefix}_gm1")
    p.emit("vector", lambda e: e.tensor_mul(sc.e2[:], sc.e[:], sc.m1[:]),
           reads=[sc.e, sc.m1], writes=[sc.e2], tag=f"{prefix}_ge2a")
    p.emit("vector", lambda e: e.tensor_sub(sc.e2[:], sc.e[:], sc.e2[:]),
           reads=[sc.e, sc.e2], writes=[sc.e2], tag=f"{prefix}_ge2b")
    p.emit("vector", lambda e: e.reduce_max(sc.m2[:], sc.e2[:], axis=AX.X),
           reads=[sc.e2], writes=[sc.m2], tag=f"{prefix}_gm2")
    p.emit("vector", lambda e: e.tensor_scalar(
        out=sc.w[:], in0=sc.e[:], scalar1=sc.m2[:], scalar2=None,
        op0=ALU.is_ge),
        reads=[sc.e, sc.m2], writes=[sc.w], tag=f"{prefix}_gsel")
    p.emit("vector", lambda e: e.tensor_mul(sc.w[:], sc.w[:], sc.e[:]),
           reads=[sc.w, sc.e], writes=[sc.w], tag=f"{prefix}_gw")
    nt = n_true if n_true is not None else E
    p.emit("vector", lambda e: e.tensor_reduce(
        sc.d[:], sc.w[:, 0:nt], axis=AX.X, op=ALU.add),
        reads=[sc.w], writes=[sc.d], tag=f"{prefix}_gden")
    if variant == "null8":
        p.emit("vector", lambda e: e.tensor_single_scalar(
            out=sc.d2[:], in_=sc.d[:], scalar=0.0, op=ALU.is_equal),
            reads=[sc.d], writes=[sc.d2], tag=f"{prefix}_gz")
        p.emit("vector", lambda e: e.tensor_add(sc.d2[:], sc.d2[:], sc.d[:]),
               reads=[sc.d2, sc.d], writes=[sc.d2], tag=f"{prefix}_gz2")
    elif variant == "eps":
        p.emit("vector", lambda e: e.tensor_single_scalar(
            out=sc.d2[:], in_=sc.d[:], scalar=1e-6, op=ALU.add),
            reads=[sc.d], writes=[sc.d2], tag=f"{prefix}_geps")
    else:
        p.emit("vector", lambda e: e.tensor_copy(sc.d2[:], sc.d[:]),
               reads=[sc.d], writes=[sc.d2], tag=f"{prefix}_gcp")
    p.emit("vector", lambda e: e.reciprocal(sc.rc[:], sc.d2[:]),
           reads=[sc.d2], writes=[sc.rc], tag=f"{prefix}_grc")
    if scale_out != 1.0:
        p.emit("vector", lambda e: e.tensor_scalar(
            out=ew[:, ecol:ecol + nt], in0=sc.w[:, 0:nt], scalar1=sc.rc[:],
            scalar2=float(scale_out), op0=ALU.mult, op1=ALU.mult),
            reads=[sc.w, sc.rc], writes=[(ew, ewkey)], tag=f"{prefix}_gew")
    else:
        p.emit("vector", lambda e: e.tensor_scalar(
            out=ew[:, ecol:ecol + nt], in0=sc.w[:, 0:nt], scalar1=sc.rc[:],
            scalar2=None, op0=ALU.mult),
            reads=[sc.w, sc.rc], writes=[(ew, ewkey)], tag=f"{prefix}_gew")



def emit_l1_ffn(ctx, W, yb_fm, x_buf, x_tiles, x_keyf, prefix="f1"):
    nc, p = ctx.nc, ctx.p
    H = 3072
    HC = H // 128
    TG = 512
    SCALE = float(np.float32(1.0) / np.float32(127.0))
    wgr = W["wgT"].rearrange("(c p) h -> c p h", p=128)
    wur = W["wuT"].rearrange("(c p) h -> c p h", p=128)
    wdr = W["wdT"].rearrange("(c p) d -> c p d", p=128)

    with ExitStack() as sc:
        ps = [ctx.ps_f32(f"{prefix}_ps{i}", stack=sc) for i in range(6)]
        wd = ctx.sb(f"{prefix}_wd", [128, (HC // 2) * D], BF16, stack=sc)
        HH = H // 2
        wg = ctx.sb(f"{prefix}_wg", [128, DC * HH], BF16, stack=sc)
        wu = ctx.sb(f"{prefix}_wu", [128, DC * HH], BF16, stack=sc)
        hbuf = ctx.sb(f"{prefix}_h", [128, (HC // 2) * TG], BF16, stack=sc)
        sg = ctx.sb(f"{prefix}_sg", [128, TG], BF16, stack=sc)
        z_sb = ctx.sb(f"{prefix}_z", [128, 4 * D], F32, stack=sc)
        def load_cvt(dst, dkey, src_ap, width, tag):
            p.dma("gpsimd", dst, src_ap, writes=[dkey], tag=f"{tag}_l")
        for g in range(T // TG):
            t0 = g * TG
            for hh in range(2):
                h0 = hh * HH
                for c in range(DC):
                    load_cvt(wg[:, c * HH:(c + 1) * HH], (wg, c),
                             wgr[c][:, h0:h0 + HH], HH,
                             f"{prefix}_lwg{g}{hh}{c}")
                for c in range(DC):
                    load_cvt(wu[:, c * HH:(c + 1) * HH], (wu, c),
                             wur[c][:, h0:h0 + HH], HH,
                             f"{prefix}_lwu{g}{hh}{c}")
                for cl in range(HC // 2):
                    load_cvt(wd[:, cl * D:(cl + 1) * D], (wd, cl),
                             wdr[hh * (HC // 2) + cl], D,
                             f"{prefix}_lwd{g}{hh}{cl}")
                for cl in range(HC // 2):
                    c = hh * (HC // 2) + cl
                    bgk = ps[(2 * cl) % 4]
                    buk = ps[(2 * cl + 1) % 4]
                    pairs_g = [
                        (wg[:, dc * HH + cl * 128:dc * HH + cl * 128 + 128],
                         yb_fm[:, dc * T + t0:dc * T + t0 + TG],
                         [(wg, dc), (yb_fm, WHOLE)])
                        for dc in range(DC)]
                    pairs_u = [
                        (wu[:, dc * HH + cl * 128:dc * HH + cl * 128 + 128],
                         yb_fm[:, dc * T + t0:dc * T + t0 + TG],
                         [(wu, dc), (yb_fm, WHOLE)])
                        for dc in range(DC)]
                    mm_accum(p, bgk, "s", pairs_g, 0, TG,
                             tag=f"{prefix}_mg{g}{c}")
                    mm_accum(p, buk, "s", pairs_u, 0, TG,
                             tag=f"{prefix}_mu{g}{c}")
                    p.emit("scalar", lambda e, bgk=bgk: e.activation(
                        out=sg[:, 0:TG], in_=bgk[:, 0:TG], func=AF.Silu,
                        scale=SCALE),
                        reads=[(bgk, "s")], writes=[sg],
                        tag=f"{prefix}_silu{g}{c}")
                    p.emit("vector", lambda e, buk=buk, cl=cl:
                           e.scalar_tensor_tensor(
                               out=hbuf[:, cl * TG:(cl + 1) * TG],
                               in0=buk[:, 0:TG], scalar=SCALE,
                               in1=sg[:, 0:TG], op0=ALU.mult, op1=ALU.mult),
                           reads=[(buk, "s"), sg], writes=[(hbuf, cl)],
                           tag=f"{prefix}_hm{g}{c}")
                for s in range(TG // 128):
                    zt = s
                    for half in range(2):
                        bank = ps[4 + half]
                        pairs = [
                            (hbuf[:, cl * TG + s * 128:cl * TG + (s + 1) * 128],
                             wd[:, cl * D + half * 512:
                                cl * D + half * 512 + 512],
                             [(hbuf, cl), (wd, cl)])
                            for cl in range(HC // 2)]
                        mm_accum(p, bank, "s", pairs, 0, 512,
                                 tag=f"{prefix}_md{g}{hh}{s}{half}")
                        if hh == 0:
                            p.emit("scalar",
                                   lambda e, bank=bank, zt=zt, half=half:
                                   e.activation(
                                       out=z_sb[:, zt * D + half * 512:
                                                zt * D + half * 512 + 512],
                                       in_=bank[:, 0:512], func=AF.Copy,
                                       scale=SCALE),
                                   reads=[(bank, "s")],
                                   writes=[(z_sb, f"t{zt}")],
                                   tag=f"{prefix}_ev{g}{hh}{s}{half}")
                        else:
                            p.emit("vector",
                                   lambda e, bank=bank, zt=zt, half=half:
                                   e.scalar_tensor_tensor(
                                       out=z_sb[:, zt * D + half * 512:
                                                zt * D + half * 512 + 512],
                                       in0=bank[:, 0:512], scalar=SCALE,
                                       in1=z_sb[:, zt * D + half * 512:
                                                zt * D + half * 512 + 512],
                                       op0=ALU.mult, op1=ALU.add),
                                   reads=[(bank, "s"), (z_sb, f"t{zt}")],
                                   writes=[(z_sb, f"t{zt}")],
                                   tag=f"{prefix}_ev{g}{hh}{s}{half}")
            with ExitStack() as esc:
                sandwich_epilogue(ctx, z_sb, [0, 1, 2, 3], x_buf,
                                  x_tiles[g * 4:(g + 1) * 4], x_keyf,
                                  f"{prefix}_e{g}", esc)



def emit_l0_ffn(ctx, W, yb_fm, x_buf, x_tiles, x_keyf, prefix="f0"):
    nc, p = ctx.nc, ctx.p
    with ExitStack() as sc:
        ps = [ctx.ps_f32(f"{prefix}_ps{i}", stack=sc) for i in range(6)]
        bw = ctx.sb(f"{prefix}_bw", [128, DC * D], BF16, stack=sc)
        gw = ctx.sb(f"{prefix}_gw", [128, DC * 10], BF16, stack=sc)
        As = ctx.sb(f"{prefix}_As", [128, DC * 128], BF16, stack=sc)
        Bs = ctx.sb(f"{prefix}_Bs", [128, D], BF16, stack=sc)
        bwr = W["base_wT"].rearrange("(c p) o -> c p o", p=128)
        gwr = W["gate_wT"].rearrange("(c p) o -> c p o", p=128)
        asr = W["A_stackT"].rearrange("(c p) o -> c p o", p=128)
        for c in range(DC):
            p.dma("gpsimd", bw[:, c * D:(c + 1) * D], bwr[c],
                  writes=[(bw, c)], tag=f"{prefix}_lbw{c}")
            p.dma("gpsimd", gw[:, c * 10:(c + 1) * 10], gwr[c],
                  writes=[(gw, c)], tag=f"{prefix}_lgw{c}")
            p.dma("gpsimd", As[:, c * 128:(c + 1) * 128], asr[c],
                  writes=[(As, c)], tag=f"{prefix}_lAs{c}")
        p.dma("gpsimd", Bs[:], W["B_stackT"], writes=[Bs],
              tag=f"{prefix}_lBs")
        brow = ctx.sb(f"{prefix}_brow", [1, D], F32, stack=sc)
        p.dma("sync", brow[:], W["base_b"], writes=[brow],
              tag=f"{prefix}_lbb")
        b_bc = ctx.broadcast_row(brow, D, f"{prefix}_bbc", [ps[3]],
                                 stack=sc, row_reads=[brow])
        IND = ctx.sb(f"{prefix}_ind", [8, 128], F32, stack=sc)
        p.emit("gpsimd", lambda e: e.memset(IND[:], 1.0), writes=[IND])
        p.emit("gpsimd", lambda e: e.affine_select(
            out=IND[:], in_=IND[:], compare_op=ALU.is_ge, fill=0.0,
            base=0, pattern=[[1, 128]], channel_multiplier=-16),
            reads=[IND], writes=[IND])
        p.emit("gpsimd", lambda e: e.affine_select(
            out=IND[:], in_=IND[:], compare_op=ALU.is_ge, fill=0.0,
            base=15, pattern=[[-1, 128]], channel_multiplier=16),
            reads=[IND], writes=[IND])

        logit = ctx.sb(f"{prefix}_lg", [128, NT * 10], F32, stack=sc)
        for i in range(NT):
            pairs = [(yb_fm[:, dc * T + i * 128:dc * T + (i + 1) * 128],
                      gw[:, dc * 10:(dc + 1) * 10],
                      [(yb_fm, WHOLE), (gw, dc)]) for dc in range(DC)]
            mm_accum(p, ps[0], "s", pairs, i * 10, (i + 1) * 10,
                     tag=f"{prefix}_mlg{i}")
        p.emit("vector", lambda e: e.tensor_copy(
            logit[:], ps[0][:, 0:NT * 10]),
            reads=[(ps[0], "s")], writes=[logit], tag=f"{prefix}_elg")
        ew_all = ctx.sb(f"{prefix}_ew", [128, NT * 8], F32, stack=sc)
        gs = GateScratch(ctx, 10, prefix, sc)
        for i in range(NT):
            emit_gating(ctx, gs, logit, WHOLE, i * 10, ew_all, f"t{i}", i * 8,
                        "null8", f"{prefix}_g{i}", n_true=8, scale_out=2.0)
        ew_em = ctx.sb(f"{prefix}_ewem", [8, T], F32, stack=sc)
        id32 = ctx.ident32()
        for i in range(NT):
            bank = ps[1 + (i // 4) % 2]
            slot = i % 4
            p.emit("tensor", lambda e, i=i, bank=bank, slot=slot: e.transpose(
                bank[0:8, slot * 128:slot * 128 + 128],
                ew_all[:, i * 8:(i + 1) * 8], id32[:]),
                reads=[(ew_all, f"t{i}"), id32], writes=[(bank, "s")],
                tag=f"{prefix}_trew{i}")
            if slot == 3:
                p.emit("vector", lambda e, i=i, bank=bank: e.tensor_copy(
                    ew_em[0:8, (i - 3) * 128:(i + 1) * 128],
                    bank[0:8, 0:512]),
                    reads=[(bank, "s")], writes=[(ew_em, i // 4)],
                    tag=f"{prefix}_eew{i}")
        Zb = ctx.sb(f"{prefix}_Zb", [128, T], BF16, stack=sc)
        EWb = ctx.sb(f"{prefix}_EWb", [128, T], BF16, stack=sc)
        for g in range(T // 512):
            t0 = g * 512
            p.emit("tensor", lambda e, t0=t0: e.matmul(
                ps[3][:, 0:512], IND[:], ew_em[0:8, t0:t0 + 512],
                start=True, stop=True),
                reads=[IND, (ew_em, g)], writes=[(ps[3], "s")],
                tag=f"{prefix}_mbc{g}")
            p.emit("vector", lambda e, t0=t0: e.tensor_copy(
                EWb[:, t0:t0 + 512], ps[3][:, 0:512]),
                reads=[(ps[3], "s")], writes=[(EWb, g)],
                tag=f"{prefix}_ebc{g}")
            pairs = [(As[:, dc * 128:(dc + 1) * 128],
                      yb_fm[:, dc * T + t0:dc * T + t0 + 512],
                      [(As, dc), (yb_fm, WHOLE)]) for dc in range(DC)]
            mm_accum(p, ps[4 + g % 2], "s", pairs, 0, 512,
                     tag=f"{prefix}_mZ{g}")
            p.emit("vector", lambda e, g=g, t0=t0: e.tensor_copy(
                Zb[:, t0:t0 + 512], ps[4 + g % 2][:, 0:512]),
                reads=[(ps[4 + g % 2], "s")], writes=[(Zb, g)],
                tag=f"{prefix}_eZ{g}")
            p.emit("vector", lambda e, g=g, t0=t0: e.tensor_mul(
                Zb[:, t0:t0 + 512], Zb[:, t0:t0 + 512], EWb[:, t0:t0 + 512]),
                reads=[(Zb, g), (EWb, g)], writes=[(Zb, g)],
                tag=f"{prefix}_mZp{g}")
        z_sb = ctx.sb(f"{prefix}_z", [128, 4 * D], F32, stack=sc)
        for i in range(NT):
            g = (i * 128) // 512
            j = i % 4
            for half in range(2):
                bank = ps[2 + (2 * i + half) % 4]
                pairs = [(yb_fm[:, dc * T + i * 128:dc * T + (i + 1) * 128],
                          bw[:, dc * D + half * 512:dc * D + half * 512 + 512],
                          [(yb_fm, WHOLE), (bw, dc)]) for dc in range(DC)]
                pairs.append((Zb[:, i * 128:(i + 1) * 128],
                              Bs[:, half * 512:half * 512 + 512],
                              [(Zb, g), (Bs, WHOLE)]))
                mm_accum(p, bank, "s", pairs, 0, 512,
                         tag=f"{prefix}_mz{i}h{half}")
                p.emit("vector", lambda e, bank=bank, j=j, half=half:
                       e.tensor_add(
                           z_sb[:, j * D + half * 512:j * D + half * 512 + 512],
                           bank[:, 0:512], b_bc[:, half * 512:half * 512 + 512]),
                       reads=[(bank, "s"), (b_bc, f"c{half * 512}")],
                       writes=[(z_sb, f"t{j}")], tag=f"{prefix}_ez{i}h{half}")
            if j == 3:
                with ExitStack() as esc:
                    sandwich_epilogue(ctx, z_sb, [0, 1, 2, 3], x_buf,
                                      x_tiles[i - 3:i + 1], x_keyf,
                                      f"{prefix}_e{i // 4}", esc)



def emit_l2_ffn(ctx, W, yb_fm, x_buf, x_tiles, x_keyf, prefix="f2"):
    nc, p = ctx.nc, ctx.p
    E = 6
    with ExitStack() as sc:
        ps = [ctx.ps_f32(f"{prefix}_ps{i}", stack=sc) for i in range(6)]
        bw = ctx.sb(f"{prefix}_bw", [128, DC * D], BF16, stack=sc)
        sA = ctx.sb(f"{prefix}_sA", [128, DC * 16], BF16, stack=sc)
        sB = ctx.sb(f"{prefix}_sB", [16, D], BF16, stack=sc)
        As = ctx.sb(f"{prefix}_As", [128, DC * 96], BF16, stack=sc)
        Bs = ctx.sb(f"{prefix}_Bs", [96, D], BF16, stack=sc)
        rw = ctx.sb(f"{prefix}_rw", [128, DC * E], BF16, stack=sc)
        bwr = W["base_WT"].rearrange("(c p) o -> c p o", p=128)
        sar = W["sA_T"].rearrange("(c p) o -> c p o", p=128)
        asr = W["A_stackT"].rearrange("(c p) o -> c p o", p=128)
        rwr = W["rw_T"].rearrange("(c p) o -> c p o", p=128)
        for c in range(DC):
            p.dma("gpsimd", bw[:, c * D:(c + 1) * D], bwr[c],
                  writes=[(bw, c)], tag=f"{prefix}_lbw{c}")
            p.dma("gpsimd", sA[:, c * 16:(c + 1) * 16], sar[c],
                  writes=[(sA, c)], tag=f"{prefix}_lsa{c}")
            p.dma("gpsimd", As[:, c * 96:(c + 1) * 96], asr[c],
                  writes=[(As, c)], tag=f"{prefix}_las{c}")
            p.dma("gpsimd", rw[:, c * E:(c + 1) * E], rwr[c],
                  writes=[(rw, c)], tag=f"{prefix}_lrw{c}")
        p.dma("gpsimd", sB[:], W["sB_T"], writes=[sB], tag=f"{prefix}_lsb")
        p.dma("gpsimd", Bs[:], W["B_stackT"], writes=[Bs], tag=f"{prefix}_lbs")
        brow = ctx.sb(f"{prefix}_brow", [1, D], F32, stack=sc)
        p.dma("sync", brow[:], W["base_b"], writes=[brow], tag=f"{prefix}_lbb")
        b_bc = ctx.broadcast_row(brow, D, f"{prefix}_bbc", [ps[3]],
                                 stack=sc, row_reads=[brow])
        rbrow = ctx.sb(f"{prefix}_rbrow", [1, E], F32, stack=sc)
        p.dma("sync", rbrow[:], W["router_b"], writes=[rbrow],
              tag=f"{prefix}_lrb")
        rb_bc = ctx.broadcast_row(rbrow, E, f"{prefix}_rbbc", [ps[3]],
                                  stack=sc, row_reads=[rbrow])
        IND = ctx.sb(f"{prefix}_ind", [6, 96], F32, stack=sc)
        p.emit("gpsimd", lambda e: e.memset(IND[:], 1.0), writes=[IND])
        p.emit("gpsimd", lambda e: e.affine_select(
            out=IND[:], in_=IND[:], compare_op=ALU.is_ge, fill=0.0,
            base=0, pattern=[[1, 96]], channel_multiplier=-16),
            reads=[IND], writes=[IND])
        p.emit("gpsimd", lambda e: e.affine_select(
            out=IND[:], in_=IND[:], compare_op=ALU.is_ge, fill=0.0,
            base=15, pattern=[[-1, 96]], channel_multiplier=16),
            reads=[IND], writes=[IND])
        s1b = ctx.sb(f"{prefix}_s1", [16, T], BF16, stack=sc)
        for g in range(T // 512):
            t0 = g * 512
            pairs = [(sA[:, dc * 16:(dc + 1) * 16],
                      yb_fm[:, dc * T + t0:dc * T + t0 + 512],
                      [(sA, dc), (yb_fm, WHOLE)]) for dc in range(DC)]
            mm_accum(p, ps[0], "s", pairs, 0, 512, tag=f"{prefix}_ms1{g}", m=16)
            p.emit("vector", lambda e, t0=t0: e.tensor_copy(
                s1b[0:16, t0:t0 + 512], ps[0][0:16, 0:512]),
                reads=[(ps[0], "s")], writes=[(s1b, g)],
                tag=f"{prefix}_es1{g}")
        logit = ctx.sb(f"{prefix}_lg", [128, NT * E], F32, stack=sc)
        for i in range(NT):
            pairs = [(yb_fm[:, dc * T + i * 128:dc * T + (i + 1) * 128],
                      rw[:, dc * E:(dc + 1) * E],
                      [(yb_fm, WHOLE), (rw, dc)]) for dc in range(DC)]
            mm_accum(p, ps[0], "s", pairs, i * E, (i + 1) * E,
                     tag=f"{prefix}_mlg{i}")
        for i in range(NT):
            p.emit("vector", lambda e, i=i: e.tensor_add(
                logit[:, i * E:(i + 1) * E], ps[0][:, i * E:(i + 1) * E],
                rb_bc[:, 0:E]),
                reads=[(ps[0], "s"), (rb_bc, "c0")], writes=[(logit, i)],
                tag=f"{prefix}_elg{i}")
        sw_all = ctx.sb(f"{prefix}_sw", [128, NT * E], F32, stack=sc)
        gs = GateScratch(ctx, E, prefix, sc)
        for i in range(NT):
            emit_gating(ctx, gs, logit, i, i * E, sw_all, f"t{i}", i * E,
                        "eps", f"{prefix}_g{i}")
        ew_em = ctx.sb(f"{prefix}_ewem", [6, T], F32, stack=sc)
        id32 = ctx.ident32()
        for i in range(NT):
            bank = ps[1 + (i // 4) % 2]
            slot = i % 4
            p.emit("tensor", lambda e, i=i, bank=bank, slot=slot: e.transpose(
                bank[0:E, slot * 128:slot * 128 + 128],
                sw_all[:, i * E:(i + 1) * E], id32[:]),
                reads=[(sw_all, f"t{i}"), id32], writes=[(bank, "s")],
                tag=f"{prefix}_trew{i}")
            if slot == 3:
                p.emit("vector", lambda e, i=i, bank=bank: e.tensor_copy(
                    ew_em[0:E, (i - 3) * 128:(i + 1) * 128],
                    bank[0:E, 0:512]),
                    reads=[(bank, "s")], writes=[(ew_em, i // 4)],
                    tag=f"{prefix}_eew{i}")
        Zb = ctx.sb(f"{prefix}_Zb", [96, T], BF16, stack=sc)
        EWb = ctx.sb(f"{prefix}_EWb", [96, T], BF16, stack=sc)
        for g in range(T // 512):
            t0 = g * 512
            p.emit("tensor", lambda e, t0=t0, g=g: e.matmul(
                ps[3][0:96, 0:512], IND[:], ew_em[0:E, t0:t0 + 512],
                start=True, stop=True),
                reads=[IND, (ew_em, g)], writes=[(ps[3], "s")],
                tag=f"{prefix}_mbc{g}")
            p.emit("vector", lambda e, t0=t0, g=g: e.tensor_copy(
                EWb[0:96, t0:t0 + 512], ps[3][0:96, 0:512]),
                reads=[(ps[3], "s")], writes=[(EWb, g)],
                tag=f"{prefix}_ebc{g}")
            pairs = [(As[:, dc * 96:(dc + 1) * 96],
                      yb_fm[:, dc * T + t0:dc * T + t0 + 512],
                      [(As, dc), (yb_fm, WHOLE)]) for dc in range(DC)]
            mm_accum(p, ps[4 + g % 2], "s", pairs, 0, 512,
                     tag=f"{prefix}_mZ{g}", m=96)
            p.emit("vector", lambda e, g=g, t0=t0: e.tensor_copy(
                Zb[0:96, t0:t0 + 512], ps[4 + g % 2][0:96, 0:512]),
                reads=[(ps[4 + g % 2], "s")], writes=[(Zb, g)],
                tag=f"{prefix}_eZ{g}")
            p.emit("vector", lambda e, g=g, t0=t0: e.tensor_mul(
                Zb[0:96, t0:t0 + 512], Zb[0:96, t0:t0 + 512],
                EWb[0:96, t0:t0 + 512]),
                reads=[(Zb, g), (EWb, g)], writes=[(Zb, g)],
                tag=f"{prefix}_mZp{g}")
        z_sb = ctx.sb(f"{prefix}_z", [128, 4 * D], F32, stack=sc)
        for i in range(NT):
            g = (i * 128) // 512
            j = i % 4
            for half in range(2):
                bank = ps[2 + (2 * i + half) % 4]
                pairs = [(yb_fm[:, dc * T + i * 128:dc * T + (i + 1) * 128],
                          bw[:, dc * D + half * 512:dc * D + half * 512 + 512],
                          [(yb_fm, WHOLE), (bw, dc)]) for dc in range(DC)]
                pairs.append((s1b[0:16, i * 128:(i + 1) * 128],
                              sB[0:16, half * 512:half * 512 + 512],
                              [(s1b, g), (sB, WHOLE)]))
                pairs.append((Zb[0:96, i * 128:(i + 1) * 128],
                              Bs[0:96, half * 512:half * 512 + 512],
                              [(Zb, g), (Bs, WHOLE)]))
                mm_accum(p, bank, "s", pairs, 0, 512,
                         tag=f"{prefix}_mz{i}h{half}")
                p.emit("vector", lambda e, bank=bank, j=j, half=half:
                       e.tensor_add(
                           z_sb[:, j * D + half * 512:j * D + half * 512 + 512],
                           bank[:, 0:512],
                           b_bc[:, half * 512:half * 512 + 512]),
                       reads=[(bank, "s"), (b_bc, f"c{half * 512}")],
                       writes=[(z_sb, f"t{j}")], tag=f"{prefix}_ez{i}h{half}")
            if j == 3:
                with ExitStack() as esc:
                    sandwich_epilogue(ctx, z_sb, [0, 1, 2, 3], x_buf,
                                      x_tiles[i - 3:i + 1], x_keyf,
                                      f"{prefix}_e{i // 4}", esc)



def emit_l3_ffn(ctx, W, yb_fm, acc, prefix="f3"):
    nc, p = ctx.nc, ctx.p
    E = 16
    with ExitStack() as sc:
        ps = [ctx.ps_f32(f"{prefix}_ps{i}", stack=sc) for i in range(8)]
        gwt = ctx.sb(f"{prefix}_gwt", [128, DC * E], BF16, stack=sc)
        gwr = W["gate_wT"].rearrange("(c p) o -> c p o", p=128)
        for c in range(DC):
            p.dma("gpsimd", gwt[:, c * E:(c + 1) * E], gwr[c],
                  writes=[(gwt, c)], tag=f"{prefix}_lgw{c}")
        logit = ctx.sb(f"{prefix}_lg", [128, NT * E], F32, stack=sc)
        for i in range(NT):
            pairs = [(yb_fm[:, dc * T + i * 128:dc * T + (i + 1) * 128],
                      gwt[:, dc * E:(dc + 1) * E],
                      [(yb_fm, WHOLE), (gwt, dc)]) for dc in range(DC)]
            mm_accum(p, ps[0], "s", pairs, i * E, (i + 1) * E,
                     tag=f"{prefix}_mlg{i}")
        p.emit("vector", lambda e: e.tensor_copy(
            logit[:], ps[0][:, 0:NT * E]),
            reads=[(ps[0], "s")], writes=[logit], tag=f"{prefix}_elg")
        comb = ctx.sb(f"{prefix}_comb", [128, NT * E], F32, stack=sc)
        gs = GateScratch(ctx, E, prefix, sc)
        for i in range(NT):
            emit_gating(ctx, gs, logit, WHOLE, i * E, comb, f"t{i}", i * E,
                        "plain", f"{prefix}_g{i}")
        sggt = ctx.sb(f"{prefix}_sggt", [128, DC], BF16, stack=sc)
        sggr = W["sggT"].rearrange("(c p) o -> c p o", p=128)
        for c in range(DC):
            p.dma("gpsimd", sggt[:, c:c + 1], sggr[c],
                  writes=[(sggt, c)], tag=f"{prefix}_lsgg{c}")
        sgc = ctx.sb(f"{prefix}_sgc", [128, NT], F32, stack=sc)
        for i in range(NT):
            pairs = [(yb_fm[:, dc * T + i * 128:dc * T + (i + 1) * 128],
                      sggt[:, dc:dc + 1],
                      [(yb_fm, WHOLE), (sggt, dc)]) for dc in range(DC)]
            mm_accum(p, ps[0], "s", pairs, i, i + 1, tag=f"{prefix}_msg{i}")
        p.emit("scalar", lambda e: e.activation(
            out=sgc[:], in_=ps[0][:, 0:NT], func=AF.Sigmoid),
            reads=[(ps[0], "s")], writes=[sgc], tag=f"{prefix}_sig")

        wgb2 = [ctx.sb(f"{prefix}_wgb{v}", [128, DC * D], BF16, stack=sc)
                for v in range(2)]
        w1b2 = [ctx.sb(f"{prefix}_w1b{v}", [128, DC * D], BF16, stack=sc)
                for v in range(2)]
        w2b = ctx.sb(f"{prefix}_w2b", [128, DC * D], BF16, stack=sc)
        silu_sb = ctx.sb(f"{prefix}_silu", [128, 1024], BF16, stack=sc)
        ub2 = [ctx.sb(f"{prefix}_ub{v}", [128, 512], BF16, stack=sc)
               for v in range(2)]
        hbuf = ctx.sb(f"{prefix}_h", [128, DC * (T // 2)], BF16, stack=sc)

        def swiglu_pass(widx, wg_r, w1_r, w2_r, scale_keyf, first, tagp):
            wgb = wgb2[widx % 2]
            w1b = w1b2[widx % 2]
            for c in range(DC):
                p.dma("gpsimd", wgb[:, c * D:(c + 1) * D], wg_r(c),
                      writes=[(wgb, c)], tag=f"{tagp}_lwg{c}")
            for c in range(DC):
                p.dma("gpsimd", w1b[:, c * D:(c + 1) * D], w1_r(c),
                      writes=[(w1b, c)], tag=f"{tagp}_lw1{c}")
            for c in range(DC):
                p.dma("gpsimd", w2b[:, c * D:(c + 1) * D], w2_r(c),
                      writes=[(w2b, c)], tag=f"{tagp}_lw2{c}")
            for th in range(2):
                t00 = th * (T // 2)
                for hc in range(DC):
                    for gg in range(2):
                        t0 = t00 + gg * 512
                        g = (t0 // 512)
                        bg = ps[(2 * gg) % 4]
                        bu = ps[(2 * gg + 1) % 4]
                        pg = [(wgb[:, dc * D + hc * 128:
                                   dc * D + hc * 128 + 128],
                               yb_fm[:, dc * T + t0:dc * T + t0 + 512],
                               [(wgb, dc), (yb_fm, WHOLE)])
                              for dc in range(DC)]
                        pu = [(w1b[:, dc * D + hc * 128:
                                   dc * D + hc * 128 + 128],
                               yb_fm[:, dc * T + t0:dc * T + t0 + 512],
                               [(w1b, dc), (yb_fm, WHOLE)])
                              for dc in range(DC)]
                        mm_accum(p, bg, "s", pg, 0, 512,
                                 tag=f"{tagp}_mg{hc}{g}")
                        mm_accum(p, bu, "s", pu, 0, 512,
                                 tag=f"{tagp}_mu{hc}{g}")
                        p.emit("scalar", lambda e, bg=bg, t0=t0, t00=t00:
                               e.activation(
                                   out=silu_sb[:, t0 - t00:t0 - t00 + 512],
                                   in_=bg[:, 0:512], func=AF.Silu),
                               reads=[(bg, "s")], writes=[(silu_sb, gg)],
                               tag=f"{tagp}_si{hc}{g}")
                        ub = ub2[gg % 2]
                        p.emit("scalar", lambda e, bu=bu, ub=ub: e.activation(
                            out=ub[:, 0:512], in_=bu[:, 0:512],
                            func=AF.Identity),
                            reads=[(bu, "s")], writes=[ub],
                            tag=f"{tagp}_ue{hc}{g}")
                        p.emit("vector", lambda e, hc=hc, t0=t0, t00=t00,
                               gg=gg, ub=ub: e.tensor_mul(
                                   hbuf[:, hc * (T // 2) + t0 - t00:
                                        hc * (T // 2) + t0 - t00 + 512],
                                   silu_sb[:, t0 - t00:t0 - t00 + 512],
                                   ub[:, 0:512]),
                               reads=[(silu_sb, gg), ub],
                               writes=[(hbuf, f"{hc}_{gg}")],
                               tag=f"{tagp}_hm{hc}{g}")
                for i in range(th * (NT // 2), (th + 1) * (NT // 2)):
                    il = i - th * (NT // 2)
                    for half in range(2):
                        bank = ps[4 + (2 * i + half) % 4]
                        pairs = [
                            (hbuf[:, hc * (T // 2) + il * 128:
                                  hc * (T // 2) + (il + 1) * 128],
                             w2b[:, hc * D + half * 512:
                                 hc * D + half * 512 + 512],
                             [(hbuf, f"{hc}_{(il * 128) // 512}"), (w2b, hc)])
                            for hc in range(DC)]
                        mm_accum(p, bank, "s", pairs, 0, 512,
                                 tag=f"{tagp}_mv{i}h{half}")
                        scol = scale_keyf(i)
                        if first:
                            p.emit("vector",
                                   lambda e, bank=bank, i=i, half=half,
                                   scol=scol: e.tensor_scalar(
                                       out=acc[:, i * D + half * 512:
                                               i * D + half * 512 + 512],
                                       in0=bank[:, 0:512], scalar1=scol[0],
                                       scalar2=None, op0=ALU.mult),
                                   reads=[(bank, "s"), scol[1]],
                                   writes=[(acc, f"t{i}")],
                                   tag=f"{tagp}_va{i}h{half}")
                        else:
                            p.emit("vector",
                                   lambda e, bank=bank, i=i, half=half,
                                   scol=scol: e.scalar_tensor_tensor(
                                       out=acc[:, i * D + half * 512:
                                               i * D + half * 512 + 512],
                                       in0=bank[:, 0:512], scalar=scol[0],
                                       in1=acc[:, i * D + half * 512:
                                               i * D + half * 512 + 512],
                                       op0=ALU.mult, op1=ALU.add),
                                   reads=[(bank, "s"), scol[1],
                                          (acc, f"t{i}")],
                                   writes=[(acc, f"t{i}")],
                                   tag=f"{tagp}_va{i}h{half}")

        wgr3 = W["wgT"].rearrange("e (c p) h -> e c p h", p=128)
        w1r3 = W["w1T"].rearrange("e (c p) h -> e c p h", p=128)
        w2r3 = W["w2T"].rearrange("e (c p) h -> e c p h", p=128)
        for ex in range(E):
            swiglu_pass(
                ex,
                lambda c, ex=ex: wgr3[ex, c],
                lambda c, ex=ex: w1r3[ex, c],
                lambda c, ex=ex: w2r3[ex, c],
                lambda i, ex=ex: (comb[:, i * E + ex:i * E + ex + 1],
                                  (comb, f"t{i}")),
                first=(ex == 0), tagp=f"{prefix}_x{ex}")
        sgwr = W["sgwT"].rearrange("(c p) h -> c p h", p=128)
        s1wr = W["s1wT"].rearrange("(c p) h -> c p h", p=128)
        s2wr = W["s2wT"].rearrange("(hh c p) o -> hh c p o", p=128, c=DC)
        for hh in range(2):
            swiglu_pass(
                16 + hh,
                lambda c, hh=hh: sgwr[c][:, hh * D:(hh + 1) * D],
                lambda c, hh=hh: s1wr[c][:, hh * D:(hh + 1) * D],
                lambda c, hh=hh: s2wr[hh, c],
                lambda i: (sgc[:, i:i + 1], sgc),
                first=False, tagp=f"{prefix}_s{hh}")


def emit_l3_ffn_block(ctx, W, yb_fm, x_buf, x_tiles, x_keyf, prefix="f3"):
    acc = ctx.sb(f"{prefix}_acc", [128, NT * D], F32)
    emit_l3_ffn(ctx, W, yb_fm, acc, prefix)
    with ExitStack() as esc:
        sandwich_epilogue(ctx, acc, list(range(NT)), x_buf, x_tiles,
                          x_keyf, f"{prefix}_e", esc)



def row_to_cols(ctx, row_sb, n, dst, prefix, bank, row_reads=(), scale=None):
    p = ctx.p
    id32 = ctx.ident32()
    for j in range(n):
        p.emit("tensor", lambda e, j=j: e.transpose(
            bank[0:128, j:j + 1], row_sb[0:1, j * 128:(j + 1) * 128],
            id32[0:1, 0:1]),
            reads=list(row_reads) + [id32], writes=[(bank, "s")],
            tag=f"{prefix}_r2c{j}")
    if scale is None:
        p.emit("vector", lambda e: e.tensor_copy(dst[:, 0:n], bank[:, 0:n]),
               reads=[(bank, "s")], writes=[dst], tag=f"{prefix}_r2ce")
    else:
        p.emit("vector", lambda e: e.tensor_scalar_mul(
            dst[:, 0:n], bank[:, 0:n], float(scale)),
            reads=[(bank, "s")], writes=[dst], tag=f"{prefix}_r2ce")


def emit_l0_attn(ctx, W, x_tm, x_nbr_dram, prefix="a0", scope=None,
                 co=None, exch=None):
    nc, p = ctx.nc, ctx.p
    H, DX = 8, 64
    QK = 512

    stats1 = ctx.sb(f"{prefix}_st1", [128, 2 * NT], F32, stack=scope)
    with ExitStack() as s0:
        ln_stats(ctx, x_tm, list(range(NT)), lambda i: f"t{i}", stats1,
                 1e-5, f"{prefix}_s1", s0)
    out_fm = ctx.sb(f"{prefix}_ofm", [128, 4 * T], BF16, stack=scope)
    sbig = ExitStack()
    with sbig:
        yb_fm = ctx.sb(f"{prefix}_ybfm", [128, DC * T], BF16, stack=sbig)
        with ExitStack() as s1:
            yb_tm = ctx.sb(f"{prefix}_ybtm", [128, NT * D], BF16, stack=s1)
            ln_apply_bf16(ctx, x_tm, list(range(NT)), lambda i: f"t{i}",
                          stats1, yb_tm, list(range(NT)), prefix)
            if exch is not None:
                emit_exchange_ag(ctx, yb_tm, NT, exch[0], exch[1],
                                 f"{prefix}_ex")
            trps = [ctx.ps_bf16(f"{prefix}_trp{i}", stack=s1)
                    for i in range(2)]
            transpose_tm_to_fm(ctx, yb_tm, list(range(NT)), yb_fm, 0, T,
                               prefix, trps)

        wq = ctx.sb(f"{prefix}_wq", [128, DC * 1536], BF16, stack=sbig)
        wqr = W["w_qkv"].rearrange("(c p) o -> c p o", p=128)
        for c in range(DC):
            p.dma("gpsimd", wq[:, c * 1536:(c + 1) * 1536], wqr[c],
                  writes=[(wq, c)], tag=f"{prefix}_lwq{c}")
        v_fm = ctx.sb(f"{prefix}_vfm", [128, 4 * T], BF16, stack=sbig)
        S_acc = ctx.sb(f"{prefix}_Sacc", [64, H * 64], F32, stack=sbig)

        s2 = sbig.enter_context(ExitStack())
        ps = [ctx.ps_f32(f"{prefix}_ps{i}", stack=s2) for i in range(6)]
        brow = ctx.sb(f"{prefix}_bqkv", [1, 1536], F32, stack=s2)
        p.dma("sync", brow[:], W["b_qkv"], writes=[brow],
              tag=f"{prefix}_lbq")
        bqk_bc = ctx.broadcast_row(brow, 1024, f"{prefix}_bqkbc", [ps[5]],
                                  stack=s2, row_reads=[brow])
        bv_col = ctx.sb(f"{prefix}_bvc", [128, 4], F32, stack=s2)
        row_to_cols(ctx, brow[0:1, 1024:1536], 4,
                    bv_col, f"{prefix}_bv", ps[5], row_reads=[brow])
        for c in range(4):
            for g in range(4):
                t0 = g * 512
                bank = ps[g % 2]
                pairs = [(wq[:, dc * 1536 + 1024 + c * 128:
                             dc * 1536 + 1024 + (c + 1) * 128],
                          yb_fm[:, dc * T + t0:dc * T + t0 + 512],
                          [(wq, dc), (yb_fm, WHOLE)]) for dc in range(DC)]
                mm_accum(p, bank, "s", pairs, 0, 512,
                         tag=f"{prefix}_mv{c}{g}")
                p.emit("scalar", lambda e, bank=bank, c=c, t0=t0:
                       e.activation(
                           out=v_fm[:, c * T + t0:c * T + t0 + 512],
                           in_=bank[:, 0:512], func=AF.Identity,
                           bias=bv_col[:, c:c + 1]),
                       reads=[(bank, "s"), bv_col],
                       writes=[(v_fm, f"{c}_{g}")], tag=f"{prefix}_ev{c}{g}")
        qg2 = [ctx.sb(f"{prefix}_qg{v}", [128, 2 * QK], BF16, stack=s2)
               for v in range(2)]
        kg2 = [ctx.sb(f"{prefix}_kg{v}", [128, 2 * QK], BF16, stack=s2)
               for v in range(2)]
        xng = ctx.sb(f"{prefix}_xng", [128, 2 * D], F32, stack=s2)
        ybg_tm = ctx.sb(f"{prefix}_ybgtm", [128, 2 * D], BF16, stack=s2)
        ybg_fm = ctx.sb(f"{prefix}_ybgfm", [128, DC * 256], BF16, stack=s2)
        stg = ctx.sb(f"{prefix}_stg", [128, 4], F32, stack=s2)
        xnr = (x_nbr_dram.rearrange("(n p) d -> n p d", p=128)
               if x_nbr_dram is not None else None)
        cor = (co.ap().rearrange("(n p) d -> n p d", p=128)
               if co is not None else None)
        for tg in range(NT):
            qg = qg2[tg % 2]
            kg = kg2[tg % 2]
            own = tg < 8
            if co is not None:
                with ExitStack() as sg_:
                    for j in range(2):
                        p.dma("sync", ybg_tm[:, j * D:(j + 1) * D],
                              cor[2 * tg + j], reads=[co],
                              writes=[(ybg_tm, f"t{j}")],
                              tag=f"{prefix}_lco{tg}{j}")
                    trps2 = [ctx.ps_bf16(f"{prefix}_t2{tg}_{v}", stack=sg_)
                             for v in range(2)]
                    transpose_tm_to_fm(ctx, ybg_tm, [0, 1], ybg_fm, 0, 256,
                                       f"{prefix}_tn{tg}", trps2)
                src_fm, fmoff, ntokf = ybg_fm, 0, 256
                fmreads = [(ybg_fm, WHOLE)]
            elif own:
                src_fm, fmoff, ntokf = yb_fm, tg * 256, T
                fmreads = [(yb_fm, WHOLE)]
            else:
                j0 = (tg - 8) * 2
                with ExitStack() as sg_:
                    for j in range(2):
                        p.dma("sync", xng[:, j * D:(j + 1) * D], xnr[j0 + j],
                              writes=[(xng, f"t{j}")], tag=f"{prefix}_lxn")
                    ln_stats(ctx, xng, [0, 1], lambda i: f"t{i}", stg,
                             1e-5, f"{prefix}_sn{tg}", sg_)
                    ln_apply_bf16(ctx, xng, [0, 1], lambda i: f"t{i}", stg,
                                  ybg_tm, [0, 1], f"{prefix}_an{tg}")
                    trps2 = [ctx.ps_bf16(f"{prefix}_t2{tg}_{v}", stack=sg_)
                             for v in range(2)]
                    transpose_tm_to_fm(ctx, ybg_tm, [0, 1], ybg_fm, 0, 256,
                                       f"{prefix}_tn{tg}", trps2)
                src_fm, fmoff, ntokf = ybg_fm, 0, 256
                fmreads = [(ybg_fm, WHOLE)]
            for j in range(2):
                bq = ps[2 * ((2 * tg + j) % 2)]
                bk = ps[2 * ((2 * tg + j) % 2) + 1]
                pq = [(src_fm[:, dc * ntokf + fmoff + j * 128:
                              dc * ntokf + fmoff + (j + 1) * 128],
                       wq[:, dc * 1536:dc * 1536 + 512],
                       fmreads + [(wq, dc)]) for dc in range(DC)]
                pk = [(src_fm[:, dc * ntokf + fmoff + j * 128:
                              dc * ntokf + fmoff + (j + 1) * 128],
                       wq[:, dc * 1536 + 512:dc * 1536 + 1024],
                       fmreads + [(wq, dc)]) for dc in range(DC)]
                mm_accum(p, bq, "s", pq, 0, 512, tag=f"{prefix}_mq{tg}{j}")
                mm_accum(p, bk, "s", pk, 0, 512, tag=f"{prefix}_mk{tg}{j}")
                p.emit("vector", lambda e, bq=bq, j=j: e.tensor_add(
                    qg[:, j * QK:(j + 1) * QK], bq[:, 0:512],
                    bqk_bc[:, 0:512]),
                    reads=[(bq, "s"), (bqk_bc, "c0")], writes=[(qg, j)],
                    tag=f"{prefix}_eq{tg}{j}")
                p.emit("vector", lambda e, bk=bk, j=j: e.tensor_add(
                    kg[:, j * QK:(j + 1) * QK], bk[:, 0:512],
                    bqk_bc[:, 512:1024]),
                    reads=[(bk, "s"), (bqk_bc, "c512")], writes=[(kg, j)],
                    tag=f"{prefix}_ek{tg}{j}")
            gb_ = ps[4 + tg % 2]
            for h in range(H):
                for j in range(2):
                    p.emit("tensor", lambda e, h=h, j=j, gb_=gb_: e.matmul(
                        gb_[0:64, h * 64:(h + 1) * 64],
                        qg[:, j * QK + h * 64:j * QK + (h + 1) * 64],
                        kg[:, j * QK + h * 64:j * QK + (h + 1) * 64],
                        start=(j == 0), stop=(j == 1)),
                        reads=[(qg, j), (kg, j)], writes=[(gb_, "s")],
                        tag=f"{prefix}_mS{tg}h{h}")
            if tg == 0:
                p.emit("vector", lambda e, gb_=gb_: e.tensor_copy(
                    S_acc[0:64, :], gb_[0:64, 0:H * 64]),
                    reads=[(gb_, "s")], writes=[S_acc],
                    tag=f"{prefix}_eS{tg}")
            else:
                p.emit("vector", lambda e, gb_=gb_: e.tensor_add(
                    S_acc[0:64, :], S_acc[0:64, :], gb_[0:64, 0:H * 64]),
                    reads=[S_acc, (gb_, "s")], writes=[S_acc],
                    tag=f"{prefix}_eS{tg}")
        attn_sb = ctx.sb(f"{prefix}_attn", [128, 4 * 64], BF16, stack=s2)
        nmx = ctx.sb(f"{prefix}_nmx", [64, 2], F32, stack=s2)
        esc_ = ctx.sb(f"{prefix}_esc", [64, 64], F32, stack=s2)
        for h in range(H):
            Sh = S_acc[0:64, h * 64:(h + 1) * 64]
            p.emit("vector", lambda e, Sh=Sh: e.reduce_max(
                nmx[0:64, 0:1], Sh, axis=AX.X, negate=True),
                reads=[S_acc], writes=[(nmx, 0)], tag=f"{prefix}_smx{h}")
            p.emit("vector", lambda e: e.tensor_scalar_mul(
                nmx[0:64, 1:2], nmx[0:64, 0:1], 0.125),
                reads=[(nmx, 0)], writes=[(nmx, 1)], tag=f"{prefix}_sms{h}")
            p.emit("scalar", lambda e, Sh=Sh: e.activation(
                out=esc_[0:64, :], in_=Sh, func=AF.Exp,
                bias=nmx[0:64, 1:2], scale=0.125),
                reads=[S_acc, (nmx, 1)], writes=[esc_],
                tag=f"{prefix}_sex{h}")
            p.emit("vector", lambda e: e.tensor_reduce(
                nmx[0:64, 0:1], esc_[0:64, :], axis=AX.X, op=ALU.add),
                reads=[esc_], writes=[(nmx, 0)], tag=f"{prefix}_ssum{h}")
            p.emit("vector", lambda e: e.reciprocal(
                nmx[0:64, 1:2], nmx[0:64, 0:1]),
                reads=[(nmx, 0)], writes=[(nmx, 1)], tag=f"{prefix}_srcp{h}")
            r0 = 64 * (h % 2)
            p.emit("vector", lambda e, h=h, r0=r0: e.tensor_scalar(
                out=attn_sb[r0:r0 + 64, (h // 2) * 64:(h // 2 + 1) * 64],
                in0=esc_[0:64, :], scalar1=nmx[0:64, 1:2], scalar2=None,
                op0=ALU.mult),
                reads=[esc_, (nmx, 1)], writes=[(attn_sb, h)],
                tag=f"{prefix}_snrm{h}")
        for pp in range(4):
            for g in range(4):
                t0 = g * 512
                ba, bb = ps[0], ps[1]
                p.emit("tensor", lambda e, pp=pp, t0=t0, ba=ba: e.matmul(
                    ba[0:64, 0:512],
                    attn_sb[0:64, pp * 64:(pp + 1) * 64],
                    v_fm[0:64, pp * T + t0:pp * T + t0 + 512],
                    start=True, stop=True, tile_position=(0, 0)),
                    reads=[(attn_sb, 2 * pp), (v_fm, f"{pp}_{g}")],
                    writes=[(ba, "s")], tag=f"{prefix}_mpa{pp}{g}")
                p.emit("tensor", lambda e, pp=pp, t0=t0, bb=bb: e.matmul(
                    bb[0:64, 0:512],
                    attn_sb[64:128, pp * 64:(pp + 1) * 64],
                    v_fm[64:128, pp * T + t0:pp * T + t0 + 512],
                    start=True, stop=True, tile_position=(64, 0)),
                    reads=[(attn_sb, 2 * pp + 1), (v_fm, f"{pp}_{g}")],
                    writes=[(bb, "s")], tag=f"{prefix}_mpb{pp}{g}")
                p.emit("vector", lambda e, pp=pp, t0=t0, ba=ba: e.tensor_copy(
                    out_fm[0:64, pp * T + t0:pp * T + t0 + 512],
                    ba[0:64, 0:512]),
                    reads=[(ba, "s")], writes=[(out_fm, f"{pp}_{g}")],
                    tag=f"{prefix}_epa{pp}{g}")
                p.emit("vector", lambda e, pp=pp, t0=t0, bb=bb: e.tensor_copy(
                    out_fm[64:128, pp * T + t0:pp * T + t0 + 512],
                    bb[0:64, 0:512]),
                    reads=[(bb, "s")], writes=[(out_fm, f"{pp}_{g}")],
                    tag=f"{prefix}_epb{pp}{g}")

    with ExitStack() as s3:
        qs = [ctx.ps_f32(f"{prefix}_qs{i}", stack=s3) for i in range(4)]
        wfc = ctx.sb(f"{prefix}_wfc", [128, 4 * D], BF16, stack=s3)
        wfr = W["w_fc"].rearrange("(c p) o -> c p o", p=128)
        for c in range(4):
            p.dma("gpsimd", wfc[:, c * D:(c + 1) * D], wfr[c],
                  writes=[(wfc, c)], tag=f"{prefix}_lwf{c}")
        rows = {}
        for nm in ["b_fc", "gamma", "beta"]:
            r = ctx.sb(f"{prefix}_{nm}r", [1, D], F32, stack=s3)
            p.dma("sync", r[:], W[nm], writes=[r], tag=f"{prefix}_l{nm}")
            rows[nm] = ctx.broadcast_row(r, D, f"{prefix}_{nm}bc", [qs[3]],
                                         stack=s3, row_reads=[r])
        z_sb = ctx.sb(f"{prefix}_z", [128, 4 * D], F32, stack=s3)
        ybuf = ctx.sb(f"{prefix}_yb2", [128, D], F32, stack=s3)
        ubuf = ctx.sb(f"{prefix}_ub2", [128, D], F32, stack=s3)
        sti = ctx.sb(f"{prefix}_sti", [128, 2 * 4], F32, stack=s3)
        for i in range(NT):
            j = i % 4
            p.emit("scalar", lambda e, i=i: e.activation(
                out=ybuf[:], in_=x_tm[:, i * D:(i + 1) * D],
                func=AF.Identity, scale=stats1[:, 2 * i:2 * i + 1],
                bias=stats1[:, 2 * i + 1:2 * i + 2]),
                reads=[(x_tm, f"t{i}"), (stats1, 2 * i), (stats1, 2 * i + 1)],
                writes=[ybuf], tag=f"{prefix}_y{i}")
            for half in range(2):
                bank = qs[half]
                pairs = [(out_fm[:, cc * T + i * 128:cc * T + (i + 1) * 128],
                          wfc[:, cc * D + half * 512:cc * D + half * 512 + 512],
                          [(out_fm, f"{cc}_{(i * 128) // 512}"), (wfc, cc)])
                         for cc in range(4)]
                mm_accum(p, bank, "s", pairs, 0, 512,
                         tag=f"{prefix}_mz{i}h{half}")
                sl = slice(j * D + half * 512, j * D + half * 512 + 512)
                p.emit("vector", lambda e, bank=bank, sl=sl, half=half:
                       e.tensor_add(z_sb[:, sl], bank[:, 0:512],
                                    rows["b_fc"][:, half * 512:
                                                 half * 512 + 512]),
                       reads=[(bank, "s"), (rows["b_fc"], f"c{half * 512}")],
                       writes=[(z_sb, f"t{j}")], tag=f"{prefix}_ez{i}{half}")
            p.emit("vector", lambda e, j=j: e.tensor_add(
                z_sb[:, j * D:(j + 1) * D], z_sb[:, j * D:(j + 1) * D],
                ybuf[:]),
                reads=[(z_sb, f"t{j}"), ybuf], writes=[(z_sb, f"t{j}")],
                tag=f"{prefix}_ezy{i}")
            with ExitStack() as s4:
                ln_stats(ctx, z_sb, [j], lambda t: f"t{t}", sti, 1e-6,
                         f"{prefix}_si{i}", s4)
            p.emit("scalar", lambda e, j=j: e.activation(
                out=ubuf[:], in_=z_sb[:, j * D:(j + 1) * D],
                func=AF.Identity, scale=sti[:, 0:1], bias=sti[:, 1:2]),
                reads=[(z_sb, f"t{j}"), (sti, 0), (sti, 1)],
                writes=[ubuf], tag=f"{prefix}_u{i}")
            p.emit("vector", lambda e: e.tensor_mul(
                ubuf[:], ubuf[:], rows["gamma"][:, 0:D]),
                reads=[ubuf, (rows["gamma"], "c0"), (rows["gamma"], "c512")],
                writes=[ubuf], tag=f"{prefix}_ug{i}")
            p.emit("vector", lambda e, j=j: e.tensor_add(
                z_sb[:, j * D:(j + 1) * D], ubuf[:], rows["beta"][:, 0:D]),
                reads=[ubuf, (rows["beta"], "c0"), (rows["beta"], "c512")],
                writes=[(z_sb, f"t{j}")], tag=f"{prefix}_ub{i}")
            if j == 3:
                with ExitStack() as esc:
                    sandwich_epilogue(ctx, z_sb, [0, 1, 2, 3], x_tm,
                                      [i - 3, i - 2, i - 1, i],
                                      lambda t: f"t{t}",
                                      f"{prefix}_e{i // 4}", esc)



def emit_l1_attn(ctx, W, x_tm, x_nbr_dram, prefix="a1", scope=None,
                 co=None, exch=None):
    nc, p = ctx.nc, ctx.p
    NK = 2 * T
    NKT = NK // 128
    y_all = ctx.sb(f"{prefix}_yall", [128, NKT * D], BF16, stack=scope)
    right = ctx.sb(f"{prefix}_right", [128, 2 * NK], BF16, stack=scope)
    left = ctx.sb(f"{prefix}_left", [128, 2 * T], BF16, stack=scope)
    if co is not None:
        with ExitStack() as s0:
            ybo = ctx.sb(f"{prefix}_ybo", [128, NT * D], BF16, stack=s0)
            st = ctx.sb(f"{prefix}_st", [128, 2 * NT], F32, stack=s0)
            ln_stats(ctx, x_tm, list(range(NT)), lambda i: f"t{i}", st, 1e-5,
                     f"{prefix}_s0", s0)
            ln_apply_bf16(ctx, x_tm, list(range(NT)), lambda i: f"t{i}", st,
                          ybo, list(range(NT)), prefix)
            emit_exchange_ag(ctx, ybo, NT, exch[0], exch[1], f"{prefix}_ex")
            p.dma("sync", y_all.rearrange("p (n d) -> p n d", d=D),
                  co.ap().rearrange("(n p) d -> p n d", p=128),
                  reads=[co], writes=[(y_all, f"t{t}") for t in range(NKT)],
                  tag=f"{prefix}_lyall")
            ps0 = [ctx.ps_f32(f"{prefix}_lp{i}", stack=s0) for i in range(2)]
            trp0 = [ctx.ps_bf16(f"{prefix}_ltp", stack=s0)]
            cl = ctx.sb(f"{prefix}_cl0", [128, DC * 256], BF16, stack=s0)
            clr = W["cl"].rearrange("(c p) o -> c p o", p=128)
            for c in range(DC):
                p.dma("gpsimd", cl[:, c * 256:(c + 1) * 256], clr[c],
                      writes=[(cl, c)], tag=f"{prefix}_lcl{c}")
            ybg0 = ctx.sb(f"{prefix}_ybg0", [128, DC * 128], BF16, stack=s0)
            for t in range(NT):
                for c in range(DC):
                    slot = c % 4
                    p.emit("tensor", lambda e, t=t, c=c, slot=slot:
                           e.transpose(
                               trp0[0][:, slot * 128:slot * 128 + 128],
                               ybo[:, t * D + c * 128:t * D + (c + 1) * 128],
                               ctx.ident[:]),
                           reads=[(ybo, f"t{t}"), ctx.ident],
                           writes=[(trp0[0], "s")],
                           tag=f"{prefix}_lty{t}{c}")
                    if slot == 3:
                        p.emit("vector", lambda e, c=c: e.tensor_copy(
                            ybg0[:, (c - 3) * 128:(c + 1) * 128],
                            trp0[0][:, 0:512]),
                            reads=[(trp0[0], "s")], writes=[(ybg0, c // 4)],
                            tag=f"{prefix}_ltye{t}{c}")
                for kc in range(2):
                    pairs = [(cl[:, dc * 256 + kc * 128:
                                 dc * 256 + kc * 128 + 128],
                              ybg0[:, dc * 128:(dc + 1) * 128],
                              [(cl, dc), (ybg0, dc // 4)])
                             for dc in range(DC)]
                    mm_accum(p, ps0[kc], "s", pairs, 0, 128,
                             tag=f"{prefix}_ml{t}{kc}")
                    p.emit("vector", lambda e, t=t, kc=kc: e.tensor_copy(
                        left[:, kc * T + t * 128:kc * T + (t + 1) * 128],
                        ps0[kc][:, 0:128]),
                        reads=[(ps0[kc], "s")],
                        writes=[(left, f"{kc}_{t}")],
                        tag=f"{prefix}_el{t}{kc}")
    else:
        with ExitStack() as s0:
            st = ctx.sb(f"{prefix}_st", [128, 2 * NT], F32, stack=s0)
            ln_stats(ctx, x_tm, list(range(NT)), lambda i: f"t{i}", st, 1e-5,
                     f"{prefix}_s0", s0)
            ln_apply_bf16(ctx, x_tm, list(range(NT)), lambda i: f"t{i}", st,
                          y_all, list(range(NT)), prefix)
            xng = ctx.sb(f"{prefix}_xng", [128, 4 * D], F32, stack=s0)
            stg = ctx.sb(f"{prefix}_stg", [128, 8], F32, stack=s0)
            xnr = x_nbr_dram.rearrange("(n p) d -> n p d", p=128)
            for g in range(4):
                with ExitStack() as sg_:
                    for j in range(4):
                        p.dma("sync", xng[:, j * D:(j + 1) * D],
                              xnr[g * 4 + j],
                              writes=[(xng, f"t{j}")],
                              tag=f"{prefix}_lxn{g}{j}")
                    ln_stats(ctx, xng, [0, 1, 2, 3], lambda i: f"t{i}", stg,
                             1e-5, f"{prefix}_sn{g}", sg_)
                    ln_apply_bf16(ctx, xng, [0, 1, 2, 3], lambda i: f"t{i}",
                                  stg, y_all,
                                  [16 + g * 4 + j for j in range(4)],
                                  f"{prefix}_an{g}")
    with ExitStack() as s1:
        ps = [ctx.ps_f32(f"{prefix}_ps{i}", stack=s1) for i in range(4)]
        trp = [ctx.ps_bf16(f"{prefix}_trp", stack=s1)]
        cl = ctx.sb(f"{prefix}_cls", [128, DC * 256], BF16, stack=s1)
        cr = ctx.sb(f"{prefix}_crs", [128, DC * 256], BF16, stack=s1)
        clr = W["cl"].rearrange("(c p) o -> c p o", p=128)
        crr = W["cr"].rearrange("(c p) o -> c p o", p=128)
        for c in range(DC):
            p.dma("gpsimd", cl[:, c * 256:(c + 1) * 256], clr[c],
                  writes=[(cl, c)], tag=f"{prefix}_lcl{c}")
            p.dma("gpsimd", cr[:, c * 256:(c + 1) * 256], crr[c],
                  writes=[(cr, c)], tag=f"{prefix}_lcr{c}")
        ybg = ctx.sb(f"{prefix}_ybg", [128, DC * 128], BF16, stack=s1)
        for t in range(NKT):
            for c in range(DC):
                slot = c % 4
                bank = trp[0]
                p.emit("tensor", lambda e, t=t, c=c, bank=bank, slot=slot:
                       e.transpose(bank[:, slot * 128:slot * 128 + 128],
                                   y_all[:, t * D + c * 128:
                                         t * D + (c + 1) * 128],
                                   ctx.ident[:]),
                       reads=[(y_all, f"t{t}"), ctx.ident],
                       writes=[(bank, "s")], tag=f"{prefix}_ty{t}{c}")
                if slot == 3:
                    p.emit("vector", lambda e, c=c, bank=bank: e.tensor_copy(
                        ybg[:, (c - 3) * 128:(c + 1) * 128], bank[:, 0:512]),
                        reads=[(bank, "s")], writes=[(ybg, c // 4)],
                        tag=f"{prefix}_tye{t}{c}")
            for kc in range(2):
                pairs = [(cr[:, dc * 256 + kc * 128:dc * 256 + kc * 128 + 128],
                          ybg[:, dc * 128:(dc + 1) * 128],
                          [(cr, dc), (ybg, dc // 4)]) for dc in range(DC)]
                mm_accum(p, ps[kc], "s", pairs, 0, 128,
                         tag=f"{prefix}_mr{t}{kc}")
                p.emit("vector", lambda e, t=t, kc=kc: e.tensor_copy(
                    right[:, kc * NK + t * 128:kc * NK + (t + 1) * 128],
                    ps[kc][:, 0:128]),
                    reads=[(ps[kc], "s")], writes=[(right, f"{kc}_{t}")],
                    tag=f"{prefix}_er{t}{kc}")
                if t < NT and co is None:
                    pairs = [(cl[:, dc * 256 + kc * 128:
                                 dc * 256 + kc * 128 + 128],
                              ybg[:, dc * 128:(dc + 1) * 128],
                              [(cl, dc), (ybg, dc // 4)]) for dc in range(DC)]
                    mm_accum(p, ps[2 + kc], "s", pairs, 0, 128,
                             tag=f"{prefix}_ml{t}{kc}")
                    p.emit("vector", lambda e, t=t, kc=kc: e.tensor_copy(
                        left[:, kc * T + t * 128:kc * T + (t + 1) * 128],
                        ps[2 + kc][:, 0:128]),
                        reads=[(ps[2 + kc], "s")],
                        writes=[(left, f"{kc}_{t}")],
                        tag=f"{prefix}_el{t}{kc}")
    with ExitStack() as s2:
        qs2 = [ctx.ps_f32(f"{prefix}_qs{i}", stack=s2) for i in range(4)]
        trp = [ctx.ps_bf16(f"{prefix}_qtp{i}", stack=s2) for i in range(2)]
        thr = ctx.sb(f"{prefix}_thrs", [1, 1], F32, stack=s2)
        gbr = ctx.sb(f"{prefix}_gbrs", [1, 1], F32, stack=s2)
        p.dma("sync", thr[:], W["thr"], writes=[thr], tag=f"{prefix}_lth")
        p.dma("sync", gbr[:], W["gb"], writes=[gbr], tag=f"{prefix}_lgb")
        cols = ctx.sb(f"{prefix}_cols", [128, 2], F32, stack=s2)
        p.emit("tensor", lambda e: e.matmul(
            qs2[0][:, 0:1], ctx.ones_row[:], thr[:], start=True, stop=True),
            reads=[ctx.ones_row, thr], writes=[(qs2[0], "s")],
            tag=f"{prefix}_bth")
        p.emit("vector", lambda e: e.tensor_scalar_mul(
            cols[:, 0:1], qs2[0][:, 0:1], 32.0),
            reads=[(qs2[0], "s")], writes=[(cols, 0)], tag=f"{prefix}_eth")
        p.emit("tensor", lambda e: e.matmul(
            qs2[0][:, 0:1], ctx.ones_row[:], gbr[:], start=True, stop=True),
            reads=[ctx.ones_row, gbr], writes=[(qs2[0], "s")],
            tag=f"{prefix}_bgb")
        p.emit("vector", lambda e: e.tensor_copy(cols[:, 1:2], qs2[0][:, 0:1]),
               reads=[(qs2[0], "s")], writes=[(cols, 1)], tag=f"{prefix}_egb")
        Ssb2 = [ctx.sb(f"{prefix}_S{v}", [128, NK], F32, stack=s2)
                for v in range(2)]
        e_sb = ctx.sb(f"{prefix}_eb", [128, NK], BF16, stack=s2)
        Pt = ctx.sb(f"{prefix}_Pt", [128, NK], BF16, stack=s2)
        mscr = ctx.sb(f"{prefix}_msc", [128, 4], F32, stack=s2)
        z_sb = ctx.sb(f"{prefix}_zz", [128, D], F32, stack=s2)

        def emit_S(i):
            S_sb = Ssb2[i % 2]
            for jg in range(NK // 512):
                bank = qs2[jg % 2]
                pairs = [(left[:, kc * T + i * 128:kc * T + (i + 1) * 128],
                          right[:, kc * NK + jg * 512:kc * NK + jg * 512 + 512],
                          [(left, f"{kc}_{i}")] +
                          [(right, f"{kc}_{tt}") for tt in
                           range(jg * 4, jg * 4 + 4)])
                         for kc in range(2)]
                mm_accum(p, bank, "s", pairs, 0, 512,
                         tag=f"{prefix}_mS{i}_{jg}")
                p.emit("vector", lambda e, bank=bank, jg=jg, S_sb=S_sb:
                       e.tensor_copy(
                    S_sb[:, jg * 512:(jg + 1) * 512], bank[:, 0:512]),
                    reads=[(bank, "s")], writes=[(S_sb, jg)],
                    tag=f"{prefix}_eS{i}_{jg}")

        emit_S(0)
        for i in range(NT):
            S_sb = Ssb2[i % 2]
            sreads = [(S_sb, jg) for jg in range(8)]
            p.emit("vector", lambda e, S_sb=S_sb: e.reduce_max(
                mscr[:, 0:1], S_sb[:], axis=AX.X, negate=False),
                reads=sreads, writes=[(mscr, 0)], tag=f"{prefix}_mx{i}")
            if i + 1 < NT:
                emit_S(i + 1)
            p.emit("vector", lambda e: e.scalar_tensor_tensor(
                out=mscr[:, 1:2], in0=mscr[:, 0:1], scalar=1.0 / 32.0,
                in1=cols[:, 1:2], op0=ALU.mult, op1=ALU.add),
                reads=[(mscr, 0), (cols, 1)], writes=[(mscr, 1)],
                tag=f"{prefix}_b1{i}")
            p.emit("vector", lambda e: e.tensor_sub(
                mscr[:, 2:3], cols[:, 1:2], mscr[:, 1:2]),
                reads=[(cols, 1), (mscr, 1)], writes=[(mscr, 2)],
                tag=f"{prefix}_b2{i}")
            p.emit("scalar", lambda e, S_sb=S_sb: e.activation(
                out=e_sb[:], in_=S_sb[:], func=AF.Exp, scale=1.0 / 32.0,
                bias=mscr[:, 2:3], accum_out=mscr[:, 3:4]),
                reads=sreads + [(mscr, 2)], writes=[e_sb, (mscr, 3)],
                tag=f"{prefix}_ex{i}")
            p.emit("vector", lambda e: e.reciprocal(
                mscr[:, 1:2], mscr[:, 3:4]),
                reads=[(mscr, 3)], writes=[(mscr, 1)], tag=f"{prefix}_rc{i}")
            p.emit("vector", lambda e, S_sb=S_sb: e.scalar_tensor_tensor(
                out=e_sb[:], in0=S_sb[:], scalar=cols[:, 0:1], in1=e_sb[:],
                op0=ALU.is_gt, op1=ALU.mult),
                reads=[e_sb, (cols, 0)] + sreads, writes=[e_sb],
                tag=f"{prefix}_P{i}")
            for jc in range(NKT):
                bank = trp[(jc // 4) % 2]
                slot = jc % 4
                p.emit("tensor", lambda e, jc=jc, bank=bank, slot=slot:
                       e.transpose(bank[:, slot * 128:slot * 128 + 128],
                                   e_sb[:, jc * 128:(jc + 1) * 128],
                                   ctx.ident[:]),
                       reads=[e_sb, ctx.ident], writes=[(bank, "s")],
                       tag=f"{prefix}_tP{i}_{jc}")
                if slot == 3:
                    p.emit("scalar", lambda e, jc=jc, bank=bank:
                           e.activation(
                               out=Pt[:, (jc - 3) * 128:(jc + 1) * 128],
                               in_=bank[:, 0:512], func=AF.Copy, scale=1.0),
                           reads=[(bank, "s")], writes=[(Pt, jc // 4)],
                           tag=f"{prefix}_tPe{i}_{jc}")
            for half in range(2):
                bank = qs2[2 + half]
                pairs = [(Pt[:, jc * 128:(jc + 1) * 128],
                          y_all[:, jc * D + half * 512:
                                jc * D + half * 512 + 512],
                          [(Pt, jc // 4), (y_all, f"t{jc}")])
                         for jc in range(NKT)]
                mm_accum(p, bank, "s", pairs, 0, 512,
                         tag=f"{prefix}_mpv{i}h{half}")
                p.emit("scalar", lambda e, bank=bank, half=half:
                       e.activation(
                           out=z_sb[:, half * 512:half * 512 + 512],
                           in_=bank[:, 0:512], func=AF.Identity,
                           scale=mscr[:, 1:2]),
                       reads=[(bank, "s"), (mscr, 1)],
                       writes=[(z_sb, f"t0")],
                       tag=f"{prefix}_epv{i}h{half}")
            with ExitStack() as esc:
                sandwich_epilogue(ctx, z_sb, [0], x_tm, [i],
                                  lambda t: f"t{t}", f"{prefix}_e{i}", esc)



def emit_l3_attn_core(ctx, W, x1_dram, x2_dram, oat_dram, prefix="a3"):
    nc, p = ctx.nc, ctx.p
    NB = 2 * T
    NBT = NB // 128
    HPC = 8
    AUG = 65
    q_fm = ctx.sb(f"{prefix}_qfm", [128, 4 * NB], BF16)
    k_fm = ctx.sb(f"{prefix}_kfm", [128, 4 * NB], BF16)
    v_tm = ctx.sb(f"{prefix}_vtm", [128, NBT * HPC * AUG], BF16)
    out_tm = ctx.sb(f"{prefix}_otm", [128, NBT * 512], BF16)
    p.emit("gpsimd", lambda e: e.memset(v_tm[:, 64::AUG], 1.0),
           writes=[(v_tm, "aug")], tag=f"{prefix}_aug")

    with ExitStack() as s1:
        ps = [ctx.ps_f32(f"{prefix}_ps{i}", stack=s1) for i in range(4)]
        trp = [ctx.ps_bf16(f"{prefix}_trp", stack=s1)]
        wq = ctx.sb(f"{prefix}_wq", [128, DC * 512], BF16, stack=s1)
        wk = ctx.sb(f"{prefix}_wk", [128, DC * 512], BF16, stack=s1)
        wv = ctx.sb(f"{prefix}_wv", [128, DC * 512], BF16, stack=s1)
        for nm, buf in [("wq", wq), ("wk", wk), ("wv", wv)]:
            r = W[nm].rearrange("(c p) o -> c p o", p=128)
            for c in range(DC):
                p.dma("gpsimd", buf[:, c * 512:(c + 1) * 512], r[c],
                      writes=[(buf, c)], tag=f"{prefix}_l{nm}{c}")
        bqr = ctx.sb(f"{prefix}_bqr", [1, 512], F32, stack=s1)
        bkr = ctx.sb(f"{prefix}_bkr", [1, 512], F32, stack=s1)
        bvr = ctx.sb(f"{prefix}_bvr", [1, 512], F32, stack=s1)
        for nm, buf in [("bq", bqr), ("bk", bkr), ("bv", bvr)]:
            p.dma("sync", buf[:], W[nm], writes=[buf], tag=f"{prefix}_l{nm}")
        bq_col = ctx.sb(f"{prefix}_bqc", [128, 4], F32, stack=s1)
        bk_col = ctx.sb(f"{prefix}_bkc", [128, 4], F32, stack=s1)
        row_to_cols(ctx, bqr[0:1, :], 4, bq_col, f"{prefix}_bqc", ps[3],
                    row_reads=[bqr], scale=0.125)
        row_to_cols(ctx, bkr[0:1, :], 4, bk_col, f"{prefix}_bkc", ps[3],
                    row_reads=[bkr])
        bv_bc = ctx.broadcast_row(bvr, 512, f"{prefix}_bvbc", [ps[3]],
                                  stack=s1, row_reads=[bvr])
        xg = ctx.sb(f"{prefix}_xg", [128, 4 * D], F32, stack=s1)
        stg = ctx.sb(f"{prefix}_stg", [128, 8], F32, stack=s1)
        ybg_tm = ctx.sb(f"{prefix}_ybgt", [128, 4 * D], BF16, stack=s1)
        ybg = ctx.sb(f"{prefix}_ybg", [128, DC * 512], BF16, stack=s1)
        x1r = x1_dram.rearrange("(n p) d -> n p d", p=128)
        x2r = x2_dram.rearrange("(n p) d -> n p d", p=128)
        for g in range(8):
            xsrc = x1r if g < 4 else x2r
            with ExitStack() as sg_:
                for j in range(4):
                    p.dma("sync", xg[:, j * D:(j + 1) * D],
                          xsrc[(g % 4) * 4 + j],
                          writes=[(xg, f"t{j}")], tag=f"{prefix}_lx{g}{j}")
                ln_stats(ctx, xg, [0, 1, 2, 3], lambda i: f"t{i}", stg, 1e-5,
                         f"{prefix}_sg{g}", sg_)
                ln_apply_bf16(ctx, xg, [0, 1, 2, 3], lambda i: f"t{i}", stg,
                              ybg_tm, [0, 1, 2, 3], f"{prefix}_ag{g}")
                transpose_tm_to_fm(ctx, ybg_tm, [0, 1, 2, 3], ybg, 0, 512,
                                   f"{prefix}_tg{g}", trp)
            t0 = g * 512
            for c in range(4):
                for nmb, wb, bc, dst in [("q", wq, bq_col, q_fm),
                                         ("k", wk, bk_col, k_fm)]:
                    bank = ps[0 if nmb == "q" else 1]
                    pairs = [(wb[:, dc * 512 + c * 128:dc * 512 + c * 128 + 128],
                              ybg[:, dc * 512:(dc + 1) * 512],
                              [(wb, dc), (ybg, WHOLE)]) for dc in range(DC)]
                    mm_accum(p, bank, "s", pairs, 0, 512,
                             tag=f"{prefix}_m{nmb}{g}{c}")
                    p.emit("scalar", lambda e, bank=bank, c=c, t0=t0, dst=dst,
                           bc=bc: e.activation(
                               out=dst[:, c * NB + t0:c * NB + t0 + 512],
                               in_=bank[:, 0:512], func=AF.Identity,
                               bias=bc[:, c:c + 1]),
                           reads=[(bank, "s"), bc],
                           writes=[(dst, f"{c}_{g}")],
                           tag=f"{prefix}_e{nmb}{g}{c}")
            for j in range(4):
                ti = g * 4 + j
                bank = ps[2]
                pairs = [(ybg[:, dc * 512 + j * 128:dc * 512 + (j + 1) * 128],
                          wv[:, dc * 512:(dc + 1) * 512],
                          [(ybg, WHOLE), (wv, dc)]) for dc in range(DC)]
                mm_accum(p, bank, "s", pairs, 0, 512,
                         tag=f"{prefix}_mv{g}{j}")
                vview = v_tm[:, ti * HPC * AUG:(ti + 1) * HPC * AUG]
                vv3 = vview.rearrange("p (h ea) -> p h ea", ea=AUG)
                p.emit("vector", lambda e, bank=bank, vv3=vv3: e.tensor_add(
                    vv3[:, :, 0:64],
                    bank[:, 0:512].rearrange("p (h ea) -> p h ea", ea=64),
                    bv_bc[:, 0:512].rearrange("p (h ea) -> p h ea", ea=64)),
                    reads=[(bank, "s"), (bv_bc, "c0")],
                    writes=[(v_tm, f"t{ti}")], tag=f"{prefix}_ev{g}{j}")
    masks = ctx.sb(f"{prefix}_msk", [128, 4 * 512], BF16)
    for kk in range(4):
        sl = masks[:, kk * 512:(kk + 1) * 512]
        p.emit("gpsimd", lambda e, sl=sl: e.memset(sl, 1.0),
               writes=[(masks, kk)], tag=f"{prefix}_mm{kk}")
        p.emit("gpsimd", lambda e, sl=sl, kk=kk: e.affine_select(
            out=sl, in_=sl, compare_op=ALU.is_ge, fill=0.0,
            base=-128 * kk, pattern=[[1, 512]], channel_multiplier=-1),
            reads=[(masks, kk)], writes=[(masks, kk)], tag=f"{prefix}_ma{kk}")
    with ExitStack() as s2:
        ps = [ctx.ps_f32(f"{prefix}_as{i}", stack=s2) for i in range(8)]
        id32 = ctx.ident32()
        PAs = [ctx.sb(f"{prefix}_PA{v}", [128, 512], BF16, stack=s2)
               for v in range(2)]
        PBs = [ctx.sb(f"{prefix}_PB{v}", [128, 512], BF16, stack=s2)
               for v in range(2)]
        nsb = ctx.sb(f"{prefix}_nsb", [128, 1024], F32, stack=s2)
        rcol = ctx.sb(f"{prefix}_rcol", [128, 2], F32, stack=s2)
        for qt in range(8):
            for pc in range(4):
                nA, nB = ps[4 + 2 * (pc % 2)], ps[5 + 2 * (pc % 2)]
                kmax = 4 * qt + 4
                for kc in range(kmax):
                    sA, sB = ps[2 * (kc % 2)], ps[2 * (kc % 2) + 1]
                    PA, PB = PAs[kc % 2], PBs[kc % 2]
                    p.emit("tensor", lambda e, pc=pc, kc=kc, qt=qt, sA=sA:
                           e.matmul(
                               sA[:, 0:512],
                               k_fm[0:64, pc * NB + kc * 128:
                                    pc * NB + (kc + 1) * 128],
                               q_fm[0:64, pc * NB + qt * 512:
                                    pc * NB + qt * 512 + 512],
                               start=True, stop=True, tile_position=(0, 0)),
                           reads=[(k_fm, f"{pc}_{kc // 4}"),
                                  (q_fm, f"{pc}_{qt}")],
                           writes=[(sA, "s")], tag=f"{prefix}_sA{qt}{pc}{kc}")
                    p.emit("tensor", lambda e, pc=pc, kc=kc, qt=qt, sB=sB:
                           e.matmul(
                               sB[:, 0:512],
                               k_fm[64:128, pc * NB + kc * 128:
                                    pc * NB + (kc + 1) * 128],
                               q_fm[64:128, pc * NB + qt * 512:
                                    pc * NB + qt * 512 + 512],
                               start=True, stop=True, tile_position=(64, 0)),
                           reads=[(k_fm, f"{pc}_{kc // 4}"),
                                  (q_fm, f"{pc}_{qt}")],
                           writes=[(sB, "s")], tag=f"{prefix}_sB{qt}{pc}{kc}")
                    for P, sbank in [(PA, sA), (PB, sB)]:
                        p.emit("scalar", lambda e, P=P, sbank=sbank:
                               e.activation(out=P[:], in_=sbank[:, 0:512],
                                            func=AF.Exp, scale=0.125),
                               reads=[(sbank, "s")], writes=[P],
                               tag=f"{prefix}_ex{qt}{pc}{kc}")
                    kk = kc - 4 * qt
                    if kk >= 0:
                        for P in (PA, PB):
                            p.emit("vector", lambda e, P=P, kk=kk:
                                   e.tensor_mul(
                                       P[:], P[:],
                                       masks[:, kk * 512:(kk + 1) * 512]),
                                   reads=[P, (masks, kk)], writes=[P],
                                   tag=f"{prefix}_mk{qt}{pc}{kc}")
                    for hh, (P, nb) in enumerate([(PA, nA), (PB, nB)]):
                        h = 2 * pc + hh
                        p.emit("tensor", lambda e, kc=kc, h=h, P=P, nb=nb,
                               st=(kc == 0), sp=(kc == kmax - 1): e.matmul(
                                   nb[0:AUG, 0:512],
                                   v_tm[:, kc * HPC * AUG + h * AUG:
                                        kc * HPC * AUG + (h + 1) * AUG],
                                   P[:], start=st, stop=sp),
                               reads=[(v_tm, f"t{kc}"), (v_tm, "aug"), P],
                               writes=[(nb, "s")],
                               tag=f"{prefix}_pv{qt}{pc}{kc}{hh}")
                for hh, nb in [(0, nA), (1, nB)]:
                    p.emit("scalar", lambda e, hh=hh, nb=nb: e.activation(
                        out=nsb[0:AUG, hh * 512:(hh + 1) * 512],
                        in_=nb[0:AUG, 0:512], func=AF.Copy, scale=1.0),
                        reads=[(nb, "s")], writes=[(nsb, hh)],
                        tag=f"{prefix}_ne{qt}{pc}{hh}")
                for hh in range(2):
                    h = 2 * pc + hh
                    for s in range(4):
                        bank = ps[2 + (s % 2)]
                        p.emit("tensor", lambda e, hh=hh, s=s, bank=bank:
                               e.transpose(
                                   bank[:, 0:AUG],
                                   nsb[0:AUG, hh * 512 + s * 128:
                                       hh * 512 + (s + 1) * 128],
                                   id32[0:AUG, 0:AUG]),
                               reads=[(nsb, hh), id32], writes=[(bank, "s")],
                               tag=f"{prefix}_nt{qt}{pc}{hh}{s}")
                        p.emit("vector", lambda e, bank=bank, hh=hh:
                               e.reciprocal(rcol[:, hh:hh + 1],
                                            bank[:, 64:65]),
                               reads=[(bank, "s")], writes=[(rcol, hh)],
                               tag=f"{prefix}_nr{qt}{pc}{hh}{s}")
                        ti = qt * 4 + s
                        p.emit("vector", lambda e, bank=bank, hh=hh, h=h,
                               ti=ti: e.tensor_scalar(
                                   out=out_tm[:, ti * 512 + h * 64:
                                              ti * 512 + (h + 1) * 64],
                                   in0=bank[:, 0:64],
                                   scalar1=rcol[:, hh:hh + 1], scalar2=None,
                                   op0=ALU.mult),
                               reads=[(bank, "s"), (rcol, hh)],
                               writes=[(out_tm, f"t{ti}h{h}")],
                               tag=f"{prefix}_no{qt}{pc}{hh}{s}")
    oar = oat_dram.rearrange("(n p) d -> n p d", p=128)
    for ti in range(NBT):
        p.dma("sync", oar[ti], out_tm[:, ti * 512:(ti + 1) * 512],
              reads=[(out_tm, f"t{ti}h{h}") for h in range(HPC)],
              tag=f"{prefix}_so{ti}")


def emit_l2_attn(ctx, W, yb_fm, out_fm, scope, prefix="a2", q_fm=None,
                 k_fm=None):
    nc, p = ctx.nc, ctx.p
    H, Dh, AUG = 16, 64, 65
    SEGS = ((512, 1), (1024, 2), (2048, 4))
    if L2_BISECT == "nodil":
        SEGS = ((512, 1),)
    if q_fm is None:
        q_fm = ctx.sb(f"{prefix}_qfm", [128, DC * T], BF16, stack=scope)
        k_fm = ctx.sb(f"{prefix}_kfm", [128, DC * T], BF16, stack=scope)
    v_fm = ctx.sb(f"{prefix}_vfm", [128, DC * T], BF16, stack=scope)
    for mi, (secname, dst, scale) in enumerate(
            [("q", q_fm, 0.125), ("k", k_fm, 1.0), ("v", v_fm, 1.0)]):
        with ExitStack() as sw:
            ps = [ctx.ps_f32(f"{prefix}_w{mi}p{i}", stack=sw)
                  for i in range(8)]
            weff = ctx.sb(f"{prefix}_weff{mi}", [128, DC * D], BF16, stack=sw)
            wsec = ctx.sb(f"{prefix}_wsec{mi}", [128, DC * D], BF16, stack=sw)
            wr = W[secname + "w"].rearrange("(c p) o -> c p o", p=128)
            for c in range(DC):
                p.dma("gpsimd", wsec[:, c * D:(c + 1) * D], wr[c],
                      writes=[(wsec, c)], tag=f"{prefix}_lw{mi}{c}")
            sw1 = sw.enter_context(ExitStack())
            cat2 = [ctx.sb(f"{prefix}_cat{mi}{v}", [128, D], BF16, stack=sw1)
                    for v in range(4)]
            catr = W["caT"].rearrange("(s c p) o -> s c p o", p=128, c=DC)
            for ph in range(2):
                combos = [(dc, half) for dc in range(DC) for half in range(2)
                          ][ph * 8:(ph + 1) * 8]
                for mc in range(DC):
                    cat = cat2[(ph * DC + mc) % 4]
                    p.dma("gpsimd", cat[:], catr[mi, mc],
                          writes=[cat], tag=f"{prefix}_lc{mi}{ph}{mc}")
                    for bi, (dc, half) in enumerate(combos):
                        p.emit("tensor", lambda e, bank=ps[bi], dc=dc,
                               half=half, mc=mc, cat=cat, wsec=wsec:
                               e.matmul(
                                   bank[:, 0:512],
                                   cat[:, dc * 128:(dc + 1) * 128],
                                   wsec[:, mc * D + half * 512:
                                        mc * D + half * 512 + 512],
                                   start=(mc == 0), stop=(mc == DC - 1)),
                               reads=[cat, (wsec, mc)],
                               writes=[(ps[bi], "s")],
                               tag=f"{prefix}_pm{mi}{ph}{mc}{bi}")
                for bi, (dc, half) in enumerate(combos):
                    p.emit("scalar", lambda e, bank=ps[bi], dc=dc,
                           half=half, weff=weff, scale=scale: e.activation(
                               out=weff[:, dc * D + half * 512:
                                        dc * D + half * 512 + 512],
                               in_=bank[:, 0:512], func=AF.Identity,
                               scale=float(scale)),
                           reads=[(ps[bi], "s")], writes=[(weff, dc)],
                           tag=f"{prefix}_pe{mi}{ph}{bi}")
            sw1.close()
            beffc = ctx.sb(f"{prefix}_beffc{mi}", [128, DC], F32, stack=sw)
            sw2 = sw.enter_context(ExitStack())
            cabr = ctx.sb(f"{prefix}_cabr{mi}", [1, D], F32, stack=sw2)
            p.dma("sync", cabr[:], W["cab"][0:1, mi * D:(mi + 1) * D],
                  writes=[cabr], tag=f"{prefix}_lcb{mi}")
            bsr = ctx.sb(f"{prefix}_bsr{mi}", [1, D], F32, stack=sw2)
            p.dma("sync", bsr[:], W[secname + "b"], writes=[bsr],
                  tag=f"{prefix}_lbs{mi}")
            bcol = ctx.sb(f"{prefix}_bcol{mi}", [128, DC], BF16, stack=sw2)
            row_to_cols(ctx, cabr[0:1, :], DC, bcol, f"{prefix}_bc{mi}",
                        ps[6], row_reads=[cabr])
            brow = ctx.sb(f"{prefix}_brow{mi}", [1, D], F32, stack=sw2)
            for half in range(2):
                bb = ps[6 + half]
                for mc in range(DC):
                    p.emit("tensor", lambda e, half=half, mc=mc,
                           bb=bb, bcol=bcol, wsec=wsec: e.matmul(
                        bb[0:1, 0:512],
                        bcol[:, mc:mc + 1],
                        wsec[:, mc * D + half * 512:mc * D + half * 512 + 512],
                        start=(mc == 0), stop=(mc == DC - 1)),
                        reads=[bcol, (wsec, mc)], writes=[(bb, "s")],
                        tag=f"{prefix}_pb{mi}{half}{mc}")
                p.emit("vector", lambda e, bb=bb, half=half, brow=brow,
                       bsr=bsr: e.tensor_add(
                    brow[0:1, half * 512:half * 512 + 512], bb[0:1, 0:512],
                    bsr[0:1, half * 512:half * 512 + 512]),
                    reads=[(bb, "s"), bsr], writes=[(brow, half)],
                    tag=f"{prefix}_pbe{mi}{half}")
            if scale != 1.0:
                p.emit("vector", lambda e, brow=brow, scale=scale:
                       e.tensor_scalar_mul(
                    brow[:], brow[:], float(scale)),
                    reads=[brow], writes=[brow], tag=f"{prefix}_pbs{mi}")
            row_to_cols(ctx, brow[0:1, :], DC, beffc, f"{prefix}_be{mi}",
                        ps[6], row_reads=[brow])
            sw2.close()
            for c in range(DC):
                for g in range(4):
                    bank = ps[g % 2]
                    pairs = [(weff[:, dc * D + c * 128:dc * D + c * 128 + 128],
                              yb_fm[:, dc * T + g * 512:dc * T + g * 512 + 512],
                              [(weff, dc), (yb_fm, WHOLE)])
                             for dc in range(DC)]
                    mm_accum(p, bank, "s", pairs, 0, 512,
                             tag=f"{prefix}_mp{mi}{c}{g}")
                    p.emit("scalar", lambda e, bank=bank, c=c, g=g, dst=dst,
                           beffc=beffc: e.activation(
                               out=dst[:, c * T + g * 512:c * T + g * 512 + 512],
                               in_=bank[:, 0:512], func=AF.Identity,
                               bias=beffc[:, c:c + 1]),
                           reads=[(bank, "s"), beffc],
                           writes=[(dst, f"{c}_{g}")],
                           tag=f"{prefix}_ep{mi}{c}{g}")
    masks = ctx.sb(f"{prefix}_msk", [128, 4 * 1024], BF16, stack=scope)
    for kk in range(4):
        for half in range(2):
            sl = masks[:, kk * 1024 + half * 512:kk * 1024 + half * 512 + 512]
            p.emit("gpsimd", lambda e, sl=sl: e.memset(sl, 1.0),
                   writes=[(masks, kk)], tag=f"{prefix}_mm{kk}{half}")
            p.emit("gpsimd", lambda e, sl=sl, kk=kk: e.affine_select(
                out=sl, in_=sl, compare_op=ALU.is_ge, fill=0.0,
                base=-128 * kk, pattern=[[1, 512]], channel_multiplier=-1),
                reads=[(masks, kk)], writes=[(masks, kk)],
                tag=f"{prefix}_ma{kk}{half}")
    lvl = 99
    if L2_BISECT.startswith("cut"):
        lvl = int(L2_BISECT[3])
    if "nodil" in L2_BISECT:
        SEGS = ((512, 1),)
    if L2_BISECT == "noattn":
        p.emit("gpsimd", lambda e: e.memset(out_fm[:], 0.125),
               writes=[(out_fm, f"{pc_}_{g_}") for pc_ in range(8)
                       for g_ in range(4)], tag=f"{prefix}_cut")
        return
    with ExitStack() as sa:
        ps = [ctx.ps_f32(f"{prefix}_as{i}", stack=sa) for i in range(6)]
        trb = ctx.ps_bf16(f"{prefix}_atr", stack=sa)
        id32 = ctx.ident32()
        qg = ctx.sb(f"{prefix}_qg", [128, 512], BF16, stack=sa)
        kg = ctx.sb(f"{prefix}_kg", [128, 512], BF16, stack=sa)
        vg = ctx.sb(f"{prefix}_vg", [128, 512], BF16, stack=sa)
        vgB = ctx.sb(f"{prefix}_vgB", [64, 512], BF16, stack=sa)
        vt = ctx.sb(f"{prefix}_vt", [128, 8 * AUG], BF16, stack=sa)
        p.emit("gpsimd", lambda e: e.memset(vt[:, 64::AUG], 1.0),
               writes=[(vt, "aug")], tag=f"{prefix}_va")
        PAs = [ctx.sb(f"{prefix}_PA{v}", [128, 512], BF16, stack=sa)
               for v in range(2)]
        PBs = [ctx.sb(f"{prefix}_PB{v}", [128, 512], BF16, stack=sa)
               for v in range(2)]
        nacc = ctx.sb(f"{prefix}_nacc", [128, 2 * T], F32, stack=sa)
        rcp = ctx.sb(f"{prefix}_rcp", [128, 16], F32, stack=sa)
        rrow = ctx.sb(f"{prefix}_rrow", [1, T], F32, stack=sa)
        for pc in range(8):
            for br, (w, r) in enumerate(SEGS):
                nseg = T // w
                for seg in range(nseg):
                    seg0 = seg * w
                    for gbuf, src in [(qg, q_fm), (kg, k_fm), (vg, v_fm)]:
                        if r == 1:
                            gb, gcol, grd = (src, pc * T + seg0,
                                             [(src, f"{pc}_{seg0 // 512}")])
                        else:
                            for hh in range(2):
                                off = (2 * pc + hh) % r
                                srd = [(src, f"{pc}_{t}") for t in range(4)]
                                p.emit("vector",
                                       lambda e, gbuf=gbuf, src=src, hh=hh,
                                       off=off, pc=pc, seg0=seg0, w=w, r=r:
                                       e.tensor_copy(
                                           gbuf[hh * 64:hh * 64 + 64, :],
                                           src[hh * 64:hh * 64 + 64,
                                               pc * T + seg0 + off:
                                               pc * T + seg0 + w:r]),
                                       reads=srd, writes=[(gbuf, hh)],
                                       tag=f"{prefix}_g{pc}{br}{seg}")
                            gb, gcol, grd = gbuf, 0, [(gbuf, 0), (gbuf, 1)]
                        if gbuf is qg:
                            qb_, qcol, qrd = gb, gcol, grd
                        elif gbuf is kg:
                            kb_, kcol, krd = gb, gcol, grd
                        else:
                            vb_, vcol, vrd = gb, gcol, grd
                    p.emit("vector", lambda e, vb_=vb_, vcol=vcol:
                           e.tensor_copy(vgB[0:64, :],
                                         vb_[64:128, vcol:vcol + 512]),
                           reads=vrd, writes=[vgB],
                           tag=f"{prefix}_vgb{pc}{br}{seg}")
                    for s in range(4):
                        for hh in range(2):
                            vsrc = (vb_[0:64, vcol + s * 128:
                                        vcol + (s + 1) * 128]
                                    if hh == 0 else
                                    vgB[0:64, s * 128:(s + 1) * 128])
                            vsrd = vrd if hh == 0 else [vgB]
                            p.emit("tensor", lambda e, vsrc=vsrc, s=s, hh=hh:
                                   e.transpose(
                                       trb[:, (2 * s + hh) * 128:
                                           (2 * s + hh) * 128 + 64],
                                       vsrc, ctx.ident[0:64, 0:64]),
                                   reads=vsrd + [ctx.ident],
                                   writes=[(trb, "s")],
                                   tag=f"{prefix}_vt{pc}{br}{seg}{s}{hh}")
                    p.emit("vector", lambda e: e.tensor_copy(
                        vt.rearrange("p (n ea) -> p n ea", ea=AUG)[:, :, 0:64],
                        trb.rearrange("p (n x) -> p n x", x=128)[:, 0:8, 0:64]),
                        reads=[(trb, "s")], writes=[(vt, "d")],
                        tag=f"{prefix}_vte{pc}{br}{seg}")
                    nA, nB = ps[4], ps[5]

                    def emit_qk2(s, kb_=kb_, kcol=kcol, qb_=qb_, qcol=qcol,
                                 pc=pc, br=br, seg=seg):
                        N = 512 - 128 * s
                        for hh, sbank in [(0, ps[2 * (s % 2)]),
                                          (1, ps[2 * (s % 2) + 1])]:
                            p.emit("tensor", lambda e, hh=hh, s=s, N=N,
                                   kb_=kb_, kcol=kcol, qb_=qb_, qcol=qcol,
                                   sbank=sbank: e.matmul(
                                       sbank[:, 0:N],
                                       kb_[hh * 64:hh * 64 + 64,
                                           kcol + s * 128:kcol + (s + 1) * 128],
                                       qb_[hh * 64:hh * 64 + 64,
                                           qcol + s * 128:qcol + 512],
                                       start=True, stop=True,
                                       tile_position=(64 * hh, 0)),
                                   reads=krd + qrd, writes=[(sbank, "s")],
                                   tag=f"{prefix}_qk{pc}{br}{seg}{s}{hh}")

                    if lvl >= 2:
                        emit_qk2(0)
                    for s in range(4):
                        N = 512 - 128 * s
                        if lvl < 2:
                            continue
                        if s + 1 < 4:
                            emit_qk2(s + 1)
                        for hh, (sbank, P, nb) in enumerate(
                                [(ps[2 * (s % 2)], PAs[s % 2], nA),
                                 (ps[2 * (s % 2) + 1], PBs[s % 2], nB)]):
                            p.emit("scalar", lambda e, P=P, sbank=sbank, N=N:
                                   e.activation(out=P[:, 0:N],
                                                in_=sbank[:, 0:N],
                                                func=AF.Exp),
                                   reads=[(sbank, "s")], writes=[P],
                                   tag=f"{prefix}_ex{pc}{br}{seg}{s}{hh}")
                            p.emit("vector", lambda e, P=P, s=s, N=N:
                                   e.tensor_mul(
                                       P[:, 0:N], P[:, 0:N],
                                       masks[:, s * 512 + 128 * s:
                                             s * 512 + 512]),
                                   reads=[P, (masks, s)], writes=[P],
                                   tag=f"{prefix}_msk{pc}{br}{seg}{s}{hh}")
                            if lvl < 3:
                                continue
                            p.emit("tensor", lambda e, P=P, nb=nb, s=s, N=N,
                                   hh=hh: e.matmul(
                                       nb[0:AUG, 128 * s:512],
                                       vt[:, (2 * s + hh) * AUG:
                                          (2 * s + hh + 1) * AUG],
                                       P[:, 0:N], start=(s == 0),
                                       stop=(s == 3)),
                                   reads=[(vt, "d"), (vt, "aug"), P],
                                   writes=[(nb, "s")],
                                   tag=f"{prefix}_pv{pc}{br}{seg}{s}{hh}")
                    for hh, nb in ([] if lvl < 3 else [(0, nA), (1, nB)]):
                        off = (2 * pc + hh) % r
                        if r == 1:
                            p.emit("vector", lambda e, hh=hh, nb=nb,
                                   seg0=seg0: e.tensor_copy(
                                       nacc[0:AUG, hh * T + seg0:
                                            hh * T + seg0 + 512],
                                       nb[0:AUG, 0:512]),
                                   reads=[(nb, "s")], writes=[(nacc, hh)],
                                   tag=f"{prefix}_na{pc}{br}{seg}{hh}")
                        else:
                            p.emit("vector", lambda e, hh=hh, nb=nb,
                                   seg0=seg0, off=off, w=w, r=r:
                                   e.tensor_add(
                                       nacc[0:AUG, hh * T + seg0 + off:
                                            hh * T + seg0 + w:r],
                                       nacc[0:AUG, hh * T + seg0 + off:
                                            hh * T + seg0 + w:r],
                                       nb[0:AUG, 0:512]),
                                   reads=[(nb, "s"), (nacc, hh)],
                                   writes=[(nacc, hh)],
                                   tag=f"{prefix}_na{pc}{br}{seg}{hh}")
            for hh in (range(2) if lvl >= 4 else []):
                for s in range(NT):
                    p.emit("tensor", lambda e, hh=hh, s=s: e.transpose(
                        ps[2][0:128, s:s + 1],
                        nacc[64:65, hh * T + s * 128:hh * T + (s + 1) * 128],
                        id32[64:65, 64:65]),
                        reads=[(nacc, hh), id32], writes=[(ps[2], "s")],
                        tag=f"{prefix}_dt{pc}{hh}{s}")
                p.emit("vector", lambda e, hh=hh: e.reciprocal(
                    rcp[:, 0:NT], ps[2][:, 0:NT]),
                    reads=[(ps[2], "s")], writes=[rcp],
                    tag=f"{prefix}_dr{pc}{hh}")
                for s in range(NT):
                    p.emit("tensor", lambda e, s=s: e.transpose(
                        ps[3][0:1, (s % 4) * 128:(s % 4) * 128 + 128],
                        rcp[:, s:s + 1], id32[:]),
                        reads=[rcp, id32], writes=[(ps[3], "s")],
                        tag=f"{prefix}_rt{pc}{hh}{s}")
                    if s % 4 == 3:
                        p.emit("vector", lambda e, s=s: e.tensor_copy(
                            rrow[0:1, (s - 3) * 128:(s + 1) * 128],
                            ps[3][0:1, 0:512]),
                            reads=[(ps[3], "s")], writes=[(rrow, s // 4)],
                            tag=f"{prefix}_rre{pc}{hh}{s}")
                for g in range(4):
                    p.emit("tensor", lambda e, g=g: e.matmul(
                        ps[2][:, 0:512], ctx.ones_row[:],
                        rrow[0:1, g * 512:(g + 1) * 512],
                        start=True, stop=True),
                        reads=[ctx.ones_row, (rrow, g)],
                        writes=[(ps[2], "s")], tag=f"{prefix}_rb{pc}{hh}{g}")
                    p.emit("vector", lambda e, hh=hh, g=g, pc=pc:
                           e.tensor_mul(
                               out_fm[hh * 64:hh * 64 + 64,
                                      pc * T + g * 512:pc * T + (g + 1) * 512],
                               nacc[0:64, hh * T + g * 512:
                                    hh * T + (g + 1) * 512],
                               ps[2][0:64, 0:512]),
                           reads=[(nacc, hh), (ps[2], "s")],
                           writes=[(out_fm, f"{pc}_{g}")],
                           tag=f"{prefix}_ou{pc}{hh}{g}")
        if lvl < 4:
            p.emit("gpsimd", lambda e: e.memset(out_fm[:], 0.125),
                   writes=[(out_fm, f"{pc_}_{g_}") for pc_ in range(8)
                           for g_ in range(4)], tag=f"{prefix}_cut2")


def emit_l3_attn_core_fused(ctx, W, co, exin, exout, qk, scope, prefix="a3"):
    nc, p = ctx.nc, ctx.p
    NB = 2 * T
    NBT = NB // 128
    HPC = 8
    AUG = 65
    QOFF, KOFF = 0, 4 * NB
    v_tm = ctx.sb(f"{prefix}_vtm", [128, NBT * HPC * AUG], BF16, stack=scope)
    out_tm = ctx.sb(f"{prefix}_otm", [128, NBT * 512], BF16, stack=scope)
    p.emit("gpsimd", lambda e: e.memset(v_tm[:, 64::AUG], 1.0),
           writes=[(v_tm, "aug")], tag=f"{prefix}_aug")

    with ExitStack() as s1:
        ps = [ctx.ps_f32(f"{prefix}_ps{i}", stack=s1) for i in range(4)]
        trp = [ctx.ps_bf16(f"{prefix}_trp", stack=s1)]
        wq = ctx.sb(f"{prefix}_wqs", [128, DC * 512], BF16, stack=s1)
        wk = ctx.sb(f"{prefix}_wks", [128, DC * 512], BF16, stack=s1)
        wv = ctx.sb(f"{prefix}_wvs", [128, DC * 512], BF16, stack=s1)
        for nm, buf in [("wq", wq), ("wk", wk), ("wv", wv)]:
            r = W[nm].rearrange("(c p) o -> c p o", p=128)
            for c in range(DC):
                p.dma("gpsimd", buf[:, c * 512:(c + 1) * 512], r[c],
                      writes=[(buf, c)], tag=f"{prefix}_l{nm}{c}")
        bqr = ctx.sb(f"{prefix}_bqrs", [1, 512], F32, stack=s1)
        bkr = ctx.sb(f"{prefix}_bkrs", [1, 512], F32, stack=s1)
        bvr = ctx.sb(f"{prefix}_bvrs", [1, 512], F32, stack=s1)
        for nm, buf in [("bq", bqr), ("bk", bkr), ("bv", bvr)]:
            p.dma("sync", buf[:], W[nm], writes=[buf], tag=f"{prefix}_l{nm}")
        bq_col = ctx.sb(f"{prefix}_bqcs", [128, 4], F32, stack=s1)
        bk_col = ctx.sb(f"{prefix}_bkcs", [128, 4], F32, stack=s1)
        row_to_cols(ctx, bqr[0:1, :], 4, bq_col, f"{prefix}_bqcs", ps[3],
                    row_reads=[bqr], scale=0.125)
        row_to_cols(ctx, bkr[0:1, :], 4, bk_col, f"{prefix}_bkcs", ps[3],
                    row_reads=[bkr])
        bv_bc = ctx.broadcast_row(bvr, 512, f"{prefix}_bvbcs", [ps[3]],
                                  stack=s1, row_reads=[bvr])
        ybg_tm = ctx.sb(f"{prefix}_ybgts", [128, 4 * D], BF16, stack=s1)
        ybg = ctx.sb(f"{prefix}_ybgs", [128, DC * 512], BF16, stack=s1)
        cor = co.ap().rearrange("(n p) d -> n p d", p=128)
        for g in range(8):
            with ExitStack() as sg_:
                for j in range(4):
                    p.dma("sync", ybg_tm[:, j * D:(j + 1) * D],
                          cor[g * 4 + j], reads=[co],
                          writes=[(ybg_tm, f"t{j}")],
                          tag=f"{prefix}_lx{g}{j}")
                transpose_tm_to_fm(ctx, ybg_tm, [0, 1, 2, 3], ybg, 0, 512,
                                   f"{prefix}_tg{g}", trp)
            t0 = g * 512
            for c in range(4):
                for nmb, wb, bc, dof in [("q", wq, bq_col, QOFF),
                                         ("k", wk, bk_col, KOFF)]:
                    bank = ps[0 if nmb == "q" else 1]
                    pairs = [(wb[:, dc * 512 + c * 128:dc * 512 + c * 128 + 128],
                              ybg[:, dc * 512:(dc + 1) * 512],
                              [(wb, dc), (ybg, WHOLE)]) for dc in range(DC)]
                    mm_accum(p, bank, "s", pairs, 0, 512,
                             tag=f"{prefix}_m{nmb}{g}{c}")
                    p.emit("scalar", lambda e, bank=bank, c=c, t0=t0, dof=dof,
                           bc=bc: e.activation(
                               out=qk[:, dof + c * NB + t0:
                                      dof + c * NB + t0 + 512],
                               in_=bank[:, 0:512], func=AF.Identity,
                               bias=bc[:, c:c + 1]),
                           reads=[(bank, "s"), bc],
                           writes=[(qk, f"{nmb}{c}_{g}")],
                           tag=f"{prefix}_e{nmb}{g}{c}")
            for j in range(4):
                ti = g * 4 + j
                bank = ps[2]
                pairs = [(ybg[:, dc * 512 + j * 128:dc * 512 + (j + 1) * 128],
                          wv[:, dc * 512:(dc + 1) * 512],
                          [(ybg, WHOLE), (wv, dc)]) for dc in range(DC)]
                mm_accum(p, bank, "s", pairs, 0, 512,
                         tag=f"{prefix}_mv{g}{j}")
                vview = v_tm[:, ti * HPC * AUG:(ti + 1) * HPC * AUG]
                vv3 = vview.rearrange("p (h ea) -> p h ea", ea=AUG)
                p.emit("vector", lambda e, bank=bank, vv3=vv3: e.tensor_add(
                    vv3[:, :, 0:64],
                    bank[:, 0:512].rearrange("p (h ea) -> p h ea", ea=64),
                    bv_bc[:, 0:512].rearrange("p (h ea) -> p h ea", ea=64)),
                    reads=[(bank, "s"), (bv_bc, "c0")],
                    writes=[(v_tm, f"t{ti}")], tag=f"{prefix}_ev{g}{j}")
    masks = ctx.sb(f"{prefix}_msk", [128, 4 * 1024], BF16, stack=scope)
    for kk in range(4):
        for half in range(2):
            sl = masks[:, kk * 1024 + half * 512:kk * 1024 + half * 512 + 512]
            p.emit("gpsimd", lambda e, sl=sl: e.memset(sl, 1.0),
                   writes=[(masks, kk)], tag=f"{prefix}_mm{kk}{half}")
            p.emit("gpsimd", lambda e, sl=sl, kk=kk: e.affine_select(
                out=sl, in_=sl, compare_op=ALU.is_ge, fill=0.0,
                base=-128 * kk, pattern=[[1, 512]], channel_multiplier=-1),
                reads=[(masks, kk)], writes=[(masks, kk)],
                tag=f"{prefix}_ma{kk}{half}")
    with ExitStack() as s2:
        qkp = [ctx._ps(f"{prefix}_qk{v}", [128, 1024], F32, s2)
               for v in range(2)]
        nab = [ctx._ps(f"{prefix}_nab{v}", [128, 1024], F32, s2)
               for v in range(2)]
        id32 = ctx.ident32()
        PABs = [ctx.sb(f"{prefix}_PAB{v}", [128, 1024], BF16, stack=s2)
                for v in range(2)]
        nsb = ctx.sb(f"{prefix}_nsb", [128, 1024], F32, stack=s2)
        rcol = ctx.sb(f"{prefix}_rcol", [128, 2], F32, stack=s2)
        for qt in range(8):
            for pc in range(4):
                nT = nab[pc % 2]
                kmax = 4 * qt + 4

                def emit_qk(kc, pc=pc, qt=qt):
                    sT = qkp[kc % 2]
                    for hh in range(2):
                        p.emit("tensor", lambda e, pc=pc, kc=kc, qt=qt,
                               sT=sT, hh=hh: e.matmul(
                                   sT[:, hh * 512:hh * 512 + 512],
                                   qk[hh * 64:hh * 64 + 64,
                                      KOFF + pc * NB + kc * 128:
                                      KOFF + pc * NB + (kc + 1) * 128],
                                   qk[hh * 64:hh * 64 + 64,
                                      QOFF + pc * NB + qt * 512:
                                      QOFF + pc * NB + qt * 512 + 512],
                                   start=True, stop=True,
                                   tile_position=(64 * hh, 0)),
                               reads=[(qk, f"k{pc}_{kc // 4}"),
                                      (qk, f"q{pc}_{qt}")],
                               writes=[(sT, "AB"[hh])],
                               tag=f"{prefix}_s{qt}{pc}{kc}{hh}")

                emit_qk(0)
                for kc in range(kmax):
                    sT = qkp[kc % 2]
                    PAB = PABs[kc % 2]
                    if kc + 1 < kmax:
                        emit_qk(kc + 1)
                    p.emit("scalar", lambda e, PAB=PAB, sT=sT:
                           e.activation(out=PAB[:], in_=sT[:, 0:1024],
                                        func=AF.Exp, scale=0.125),
                           reads=[(sT, "A"), (sT, "B")], writes=[PAB],
                           tag=f"{prefix}_ex{qt}{pc}{kc}")
                    kk = kc - 4 * qt
                    if kk >= 0:
                        p.emit("vector", lambda e, PAB=PAB, kk=kk:
                               e.tensor_mul(
                                   PAB[:], PAB[:],
                                   masks[:, kk * 1024:(kk + 1) * 1024]),
                               reads=[PAB, (masks, kk)], writes=[PAB],
                               tag=f"{prefix}_mk{qt}{pc}{kc}")
                    for hh in range(2):
                        h = 2 * pc + hh
                        p.emit("tensor", lambda e, kc=kc, h=h, PAB=PAB,
                               nT=nT, hh=hh,
                               st=(kc == 0), sp=(kc == kmax - 1): e.matmul(
                                   nT[0:AUG, hh * 512:hh * 512 + 512],
                                   v_tm[:, kc * HPC * AUG + h * AUG:
                                        kc * HPC * AUG + (h + 1) * AUG],
                                   PAB[:, hh * 512:hh * 512 + 512],
                                   start=st, stop=sp),
                               reads=[(v_tm, f"t{kc}"), (v_tm, "aug"), PAB],
                               writes=[(nT, "AB"[hh])],
                               tag=f"{prefix}_pv{qt}{pc}{kc}{hh}")
                p.emit("scalar", lambda e, nT=nT: e.activation(
                    out=nsb[0:AUG, 0:1024],
                    in_=nT[0:AUG, 0:1024], func=AF.Copy, scale=1.0),
                    reads=[(nT, "A"), (nT, "B")],
                    writes=[(nsb, 0), (nsb, 1)],
                    tag=f"{prefix}_ne{qt}{pc}")
                for hh in range(2):
                    h = 2 * pc + hh
                    for s in range(4):
                        b0 = (s % 2) * 512
                        bk = "AB"[s % 2]
                        bank = qkp[1]
                        p.emit("tensor", lambda e, hh=hh, s=s, bank=bank,
                               b0=b0: e.transpose(
                                   bank[:, b0:b0 + AUG],
                                   nsb[0:AUG, hh * 512 + s * 128:
                                       hh * 512 + (s + 1) * 128],
                                   id32[0:AUG, 0:AUG]),
                               reads=[(nsb, hh), id32],
                               writes=[(bank, bk)],
                               tag=f"{prefix}_nt{qt}{pc}{hh}{s}")
                        p.emit("vector", lambda e, bank=bank, hh=hh, b0=b0:
                               e.reciprocal(rcol[:, hh:hh + 1],
                                            bank[:, b0 + 64:b0 + 65]),
                               reads=[(bank, bk)], writes=[(rcol, hh)],
                               tag=f"{prefix}_nr{qt}{pc}{hh}{s}")
                        ti = qt * 4 + s
                        p.emit("vector", lambda e, bank=bank, hh=hh, h=h,
                               ti=ti, b0=b0: e.tensor_scalar(
                                   out=out_tm[:, ti * 512 + h * 64:
                                              ti * 512 + (h + 1) * 64],
                                   in0=bank[:, b0:b0 + 64],
                                   scalar1=rcol[:, hh:hh + 1], scalar2=None,
                                   op0=ALU.mult),
                               reads=[(bank, bk), (rcol, hh)],
                               writes=[(out_tm, f"t{ti}h{h}")],
                               tag=f"{prefix}_no{qt}{pc}{hh}{s}")
    er = exin.ap().rearrange("(n p) d -> n p d", p=128)
    for ti in range(NBT):
        p.dma("sync", er[ti], out_tm[:, ti * 512:(ti + 1) * 512],
              reads=[(out_tm, f"t{ti}h{h}") for h in range(2 * HPC)],
              writes=[(exin, ti)], tag=f"{prefix}_so{ti}")
    p.emit("gpsimd", lambda e: e.collective_compute(
        "AllGather", ALU.bypass, replica_groups=RG_PAIRS,
        ins=[exin.ap().opt()], outs=[exout.ap().opt()]),
        reads=[(exin, i) for i in range(NBT)], writes=[exout],
        tag=f"{prefix}_cc")


def emit_l3_attn_proj_fused(ctx, W, x_buf, co2, mcol_d, m2col_d, scope,
                            prefix="a3b"):
    nc, p = ctx.nc, ctx.p
    with ExitStack() as sc:
        ps = [ctx.ps_f32(f"{prefix}_ps{i}", stack=sc) for i in range(4)]
        trp = [ctx.ps_bf16(f"{prefix}_trp{i}", stack=sc) for i in range(2)]
        ow = ctx.sb(f"{prefix}_ow", [128, DC * D], BF16, stack=sc)
        owr = W["o_w"].rearrange("(c p) o -> c p o", p=128)
        for c in range(DC):
            p.dma("gpsimd", ow[:, c * D:(c + 1) * D], owr[c],
                  writes=[(ow, c)], tag=f"{prefix}_low{c}")
        obr = ctx.sb(f"{prefix}_obr", [1, D], F32, stack=sc)
        p.dma("sync", obr[:], W["o_b"], writes=[obr], tag=f"{prefix}_lob")
        ob_bc = ctx.broadcast_row(obr, D, f"{prefix}_obbc", [ps[3]],
                                  stack=sc, row_reads=[obr])
        mcol = ctx.sb(f"{prefix}_mc", [128, 1], F32, stack=sc)
        m2col = ctx.sb(f"{prefix}_mc2", [128, 1], F32, stack=sc)
        p.dma("sync", mcol[:], mcol_d, writes=[mcol], tag=f"{prefix}_lm")
        p.dma("sync", m2col[:], m2col_d, writes=[m2col], tag=f"{prefix}_lm2")
        cand = ctx.sb(f"{prefix}_cand", [128, 4 * 512], BF16, stack=sc)
        oat_t = ctx.sb(f"{prefix}_oat", [128, D], BF16, stack=sc)
        oat_fm = ctx.sb(f"{prefix}_ofm", [128, DC * T], BF16, stack=sc)
        c2r = co2.ap().rearrange("(n p) d -> n p d", p=128)
        for i in range(NT):
            for slot, tidx in enumerate([i, 16 + i, 32 + i, 48 + i]):
                p.emit("sync", lambda e, slot=slot, tidx=tidx: e.dma_start(
                    out=cand[:, slot * 512:(slot + 1) * 512], in_=c2r[tidx]),
                    reads=[co2], writes=[(cand, slot)], dma=True,
                    tag=f"{prefix}_lc{i}{slot}")
            for hhalf in range(2):
                a, b = 2 * hhalf, 2 * hhalf + 1
                p.emit("vector", lambda e, a=a, hhalf=hhalf: e.tensor_scalar(
                    out=oat_t[:, hhalf * 512:(hhalf + 1) * 512],
                    in0=cand[:, a * 512:(a + 1) * 512], scalar1=mcol[:],
                    scalar2=None, op0=ALU.mult),
                    reads=[(cand, a), mcol], writes=[(oat_t, hhalf)],
                    tag=f"{prefix}_sa{i}{hhalf}")
                p.emit("vector", lambda e, b=b, hhalf=hhalf:
                       e.scalar_tensor_tensor(
                           out=oat_t[:, hhalf * 512:(hhalf + 1) * 512],
                           in0=cand[:, b * 512:(b + 1) * 512],
                           scalar=m2col[:],
                           in1=oat_t[:, hhalf * 512:(hhalf + 1) * 512],
                           op0=ALU.mult, op1=ALU.add),
                       reads=[(cand, b), m2col, (oat_t, hhalf)],
                       writes=[(oat_t, hhalf)], tag=f"{prefix}_sb{i}{hhalf}")
            for c in range(DC):
                slot = c % 4
                bank = trp[(c // 4) % 2]
                p.emit("tensor", lambda e, c=c, bank=bank, slot=slot:
                       e.transpose(bank[:, slot * 128:slot * 128 + 128],
                                   oat_t[:, c * 128:(c + 1) * 128],
                                   ctx.ident[:]),
                       reads=[(oat_t, 0), (oat_t, 1), ctx.ident],
                       writes=[(bank, "s")], tag=f"{prefix}_t{i}{c}")
                if slot == 3:
                    p.emit("vector", lambda e, c=c, bank=bank, i=i:
                           e.tensor_copy(
                               oat_fm.rearrange("p (c t) -> p c t", t=T)
                               [:, c - 3:c + 1, i * 128:(i + 1) * 128],
                               bank.rearrange("p (c t) -> p c t", t=128)
                               [:, 0:4, :]),
                           reads=[(bank, "s")],
                           writes=[(oat_fm, f"{c // 4}_{i}")],
                           tag=f"{prefix}_te{i}{c}")
        z_sb = ctx.sb(f"{prefix}_z", [128, 4 * D], F32, stack=sc)
        for i in range(NT):
            j = i % 4
            for half in range(2):
                bank = ps[(2 * i + half) % 4]
                pairs = [(oat_fm[:, cc * T + i * 128:cc * T + (i + 1) * 128],
                          ow[:, cc * D + half * 512:cc * D + half * 512 + 512],
                          [(oat_fm, f"{cc // 4}_{i}"), (ow, cc)])
                         for cc in range(DC)]
                mm_accum(p, bank, "s", pairs, 0, 512,
                         tag=f"{prefix}_mz{i}h{half}")
                p.emit("vector", lambda e, bank=bank, j=j, half=half:
                       e.tensor_add(
                           z_sb[:, j * D + half * 512:j * D + half * 512 + 512],
                           bank[:, 0:512],
                           ob_bc[:, half * 512:half * 512 + 512]),
                       reads=[(bank, "s"), (ob_bc, f"c{half * 512}")],
                       writes=[(z_sb, f"t{j}")], tag=f"{prefix}_ez{i}{half}")
            if j == 3:
                with ExitStack() as esc:
                    sandwich_epilogue(ctx, z_sb, [0, 1, 2, 3], x_buf,
                                      [i - 3, i - 2, i - 1, i],
                                      lambda t: f"t{t}",
                                      f"{prefix}_e{i // 4}", esc)


def emit_l3_attn_proj(ctx, W, x_tm, oat_dram, prefix="a3b"):
    nc, p = ctx.nc, ctx.p
    with ExitStack() as sc:
        ps = [ctx.ps_f32(f"{prefix}_ps{i}", stack=sc) for i in range(4)]
        trp = [ctx.ps_bf16(f"{prefix}_trp{i}", stack=sc) for i in range(2)]
        ow = ctx.sb(f"{prefix}_ow", [128, DC * D], BF16, stack=sc)
        owr = W["o_w"].rearrange("(c p) o -> c p o", p=128)
        for c in range(DC):
            p.dma("gpsimd", ow[:, c * D:(c + 1) * D], owr[c],
                  writes=[(ow, c)], tag=f"{prefix}_low{c}")
        obr = ctx.sb(f"{prefix}_obr", [1, D], F32, stack=sc)
        p.dma("sync", obr[:], W["o_b"], writes=[obr], tag=f"{prefix}_lob")
        ob_bc = ctx.broadcast_row(obr, D, f"{prefix}_obbc", [ps[3]],
                                  stack=sc, row_reads=[obr])
        oat_tm = ctx.sb(f"{prefix}_oat", [128, NT * D], BF16, stack=sc)
        oar = oat_dram.rearrange("(n p) d -> n p d", p=128)
        for i in range(NT):
            p.dma("sync", oat_tm[:, i * D:(i + 1) * D], oar[i],
                  writes=[(oat_tm, f"t{i}")], tag=f"{prefix}_lo{i}")
        oat_fm = ctx.sb(f"{prefix}_ofm", [128, DC * T], BF16, stack=sc)
        transpose_tm_to_fm(ctx, oat_tm, list(range(NT)), oat_fm, 0, T,
                           f"{prefix}_t", trp)
        z_sb = ctx.sb(f"{prefix}_z", [128, 4 * D], F32, stack=sc)
        for i in range(NT):
            j = i % 4
            for half in range(2):
                bank = ps[half]
                pairs = [(oat_fm[:, cc * T + i * 128:cc * T + (i + 1) * 128],
                          ow[:, cc * D + half * 512:cc * D + half * 512 + 512],
                          [(oat_fm, f"c{cc}g{(i // 4) * 512}"), (ow, cc)])
                         for cc in range(DC)]
                mm_accum(p, bank, "s", pairs, 0, 512,
                         tag=f"{prefix}_mz{i}h{half}")
                p.emit("vector", lambda e, bank=bank, j=j, half=half:
                       e.tensor_add(
                           z_sb[:, j * D + half * 512:j * D + half * 512 + 512],
                           bank[:, 0:512],
                           ob_bc[:, half * 512:half * 512 + 512]),
                       reads=[(bank, "s"), (ob_bc, f"c{half * 512}")],
                       writes=[(z_sb, f"t{j}")], tag=f"{prefix}_ez{i}{half}")
            if j == 3:
                with ExitStack() as esc:
                    sandwich_epilogue(ctx, z_sb, [0, 1, 2, 3], x_tm,
                                      [i - 3, i - 2, i - 1, i],
                                      lambda t: f"t{t}",
                                      f"{prefix}_e{i // 4}", esc)


_PROG_CACHE = {}


def build_tokenlocal_prog(weight_specs, emit_fn):
    nc = bass.Bass("TRN2", target_bir_lowering=False)
    x = nc.dram_tensor("x", [T, D], F32, kind="ExternalInput").ap()
    W = {n: nc.dram_tensor(n, shp, dt, kind="ExternalInput").ap()
         for n, shp, dt in weight_specs}
    xout = nc.dram_tensor("xout", [T, D], F32, kind="ExternalOutput").ap()
    stack = ExitStack()
    with stack:
        p = Prog(nc)
        ctx = Ctx(nc, p, stack)
        yb_fm = ctx.sb("yb_fm", [128, DC * T], BF16)
        x_tm = ctx.sb("x_tm", [128, NT * D], F32)
        xr = x.rearrange("(n p) d -> n p d", p=128)
        xor_ = xout.rearrange("(n p) d -> n p d", p=128)
        for i in range(NT):
            p.dma("sync", x_tm[:, i * D:(i + 1) * D], xr[i],
                  writes=[(x_tm, f"t{i}")], tag=f"ldx{i}")
        with ExitStack() as sc0:
            trps = [ctx.ps_bf16(f"trps{i}", stack=sc0) for i in range(2)]
            ln_prologue(ctx, x_tm, list(range(NT)), lambda i: f"t{i}",
                        yb_fm, 0, T, "p", trps, sc0)
        emit_fn(ctx, W, yb_fm, x_tm, list(range(NT)), lambda i: f"t{i}")
        for i in range(NT):
            p.dma("sync", xor_[i], x_tm[:, i * D:(i + 1) * D],
                  reads=[(x_tm, f"t{i}")], tag=f"stx{i}")
        p.finalize(stack)
    return nc


def _run_sharded(progkey, build_fn, x_flat, wmap):
    if progkey not in _PROG_CACHE:
        _PROG_CACHE[progkey] = build_fn()
    nc = _PROG_CACHE[progkey]
    in_maps = [dict(wmap, x=s) for s in _shard_tokens(x_flat)]
    res = run_bass_kernel_spmd(nc, in_maps, core_ids=list(range(N_CORES)))
    return np.concatenate([r["xout"] for r in res.results], axis=0)


def run_l0_ffn(x_flat, inputs):
    wmap = {
        "base_wT": np.ascontiguousarray(
            np.asarray(inputs["L0_ffn_base_w"], np.float32).T),
        "base_b": np.asarray(
            inputs["L0_ffn_base_b"], np.float32).reshape(1, D),
        "gate_wT": np.ascontiguousarray(
            np.asarray(inputs["L0_ffn_gate_w"], np.float32).T),
        "A_stackT": np.ascontiguousarray(
            np.asarray(inputs["L0_ffn_lora_A"], np.float32)
            .transpose(2, 0, 1).reshape(D, 128)),
        "B_stackT": np.ascontiguousarray(
            np.asarray(inputs["L0_ffn_lora_B"], np.float32)
            .transpose(0, 2, 1).reshape(128, D)),
    }
    specs = [("base_wT", [D, D], F32), ("base_b", [1, D], F32),
             ("gate_wT", [D, 10], F32), ("A_stackT", [D, 128], F32),
             ("B_stackT", [128, D], F32)]
    return _run_sharded(
        "l0_ffn",
        lambda: build_tokenlocal_prog(specs, lambda ctx, W, yb, xb, tl, kf:
                                      emit_l0_ffn(ctx, W, yb, xb, tl, kf)),
        x_flat, wmap)


def build_l0_attn_prog():
    nc = bass.Bass("TRN2", target_bir_lowering=False)
    x = nc.dram_tensor("x", [T, D], F32, kind="ExternalInput").ap()
    xn = nc.dram_tensor("xn", [T, D], F32, kind="ExternalInput").ap()
    specs = [("w_qkv", [D, 1536], F32), ("b_qkv", [1, 1536], F32),
             ("w_fc", [512, D], F32), ("b_fc", [1, D], F32),
             ("gamma", [1, D], F32), ("beta", [1, D], F32)]
    W = {n: nc.dram_tensor(n, shp, dt, kind="ExternalInput").ap()
         for n, shp, dt in specs}
    xout = nc.dram_tensor("xout", [T, D], F32, kind="ExternalOutput").ap()
    stack = ExitStack()
    with stack:
        p = Prog(nc)
        ctx = Ctx(nc, p, stack)
        x_tm = ctx.sb("x_tm", [128, NT * D], F32)
        xr = x.rearrange("(n p) d -> n p d", p=128)
        xor_ = xout.rearrange("(n p) d -> n p d", p=128)
        for i in range(NT):
            p.dma("sync", x_tm[:, i * D:(i + 1) * D], xr[i],
                  writes=[(x_tm, f"t{i}")], tag=f"ldx{i}")
        emit_l0_attn(ctx, W, x_tm, xn)
        for i in range(NT):
            p.dma("sync", xor_[i], x_tm[:, i * D:(i + 1) * D],
                  reads=[(x_tm, f"t{i}")], tag=f"stx{i}")
        p.finalize(stack)
    return nc


def run_l0_attn(x_flat, inputs):
    f32 = lambda k: np.asarray(inputs[k], np.float32)
    wmap = {
        "w_qkv": np.ascontiguousarray(f32("L0_attn_w_qkv")),
        "b_qkv": f32("L0_attn_b_qkv").reshape(1, 1536),
        "w_fc": np.ascontiguousarray(f32("L0_attn_w_fc")),
        "b_fc": f32("L0_attn_b_fc").reshape(1, D),
        "gamma": f32("L0_attn_gamma").reshape(1, D),
        "beta": f32("L0_attn_beta").reshape(1, D),
    }
    if "l0_attn" not in _PROG_CACHE:
        _PROG_CACHE["l0_attn"] = build_l0_attn_prog()
    nc = _PROG_CACHE["l0_attn"]
    shards = _shard_tokens(x_flat)
    in_maps = [dict(wmap, x=shards[c], xn=shards[c ^ 1])
               for c in range(N_CORES)]
    res = run_bass_kernel_spmd(nc, in_maps, core_ids=list(range(N_CORES)))
    return np.concatenate([r["xout"] for r in res.results], axis=0)


def build_l1_attn_prog():
    nc = bass.Bass("TRN2", target_bir_lowering=False)
    x = nc.dram_tensor("x", [T, D], F32, kind="ExternalInput").ap()
    xn = nc.dram_tensor("xn", [T, D], F32, kind="ExternalInput").ap()
    specs = [("cl", [D, 256], F32), ("cr", [D, 256], F32),
             ("thr", [1, 1], F32), ("gb", [1, 1], F32)]
    W = {n: nc.dram_tensor(n, shp, dt, kind="ExternalInput").ap()
         for n, shp, dt in specs}
    xout = nc.dram_tensor("xout", [T, D], F32, kind="ExternalOutput").ap()
    stack = ExitStack()
    with stack:
        p = Prog(nc)
        ctx = Ctx(nc, p, stack)
        x_tm = ctx.sb("x_tm", [128, NT * D], F32)
        xr = x.rearrange("(n p) d -> n p d", p=128)
        xor_ = xout.rearrange("(n p) d -> n p d", p=128)
        for i in range(NT):
            p.dma("sync", x_tm[:, i * D:(i + 1) * D], xr[i],
                  writes=[(x_tm, f"t{i}")], tag=f"ldx{i}")
        emit_l1_attn(ctx, W, x_tm, xn)
        for i in range(NT):
            p.dma("sync", xor_[i], x_tm[:, i * D:(i + 1) * D],
                  reads=[(x_tm, f"t{i}")], tag=f"stx{i}")
        p.finalize(stack)
    return nc


def run_l1_attn(x_flat, inputs):
    f32 = lambda k: np.asarray(inputs[k], np.float32)
    wmap = {
        "cl": np.ascontiguousarray(f32("L1_attn_compare_left")),
        "cr": np.ascontiguousarray(f32("L1_attn_compare_right")),
        "thr": f32("L1_attn_threshold").reshape(1, 1),
        "gb": f32("L1_attn_gate_bias").reshape(1, 1),
    }
    if "l1_attn" not in _PROG_CACHE:
        _PROG_CACHE["l1_attn"] = build_l1_attn_prog()
    nc = _PROG_CACHE["l1_attn"]
    shards = _shard_tokens(x_flat)
    in_maps = [dict(wmap, x=shards[c], xn=shards[c ^ 1])
               for c in range(N_CORES)]
    res = run_bass_kernel_spmd(nc, in_maps, core_ids=list(range(N_CORES)))
    return np.concatenate([r["xout"] for r in res.results], axis=0)


def build_l2_attn_prog():
    nc = bass.Bass("TRN2", target_bir_lowering=False)
    x = nc.dram_tensor("x", [T, D], F32, kind="ExternalInput").ap()
    specs = [("caT", [3072, D], F32), ("qw", [D, D], F32),
             ("kw", [D, D], F32), ("vw", [D, D], F32),
             ("cab", [1, 3072], F32), ("qb", [1, D], F32),
             ("kb", [1, D], F32), ("vb", [1, D], F32),
             ("o_w", [D, D], F32), ("o_b", [1, D], F32)]
    W = {n: nc.dram_tensor(n, shp, dt, kind="ExternalInput").ap()
         for n, shp, dt in specs}
    xout = nc.dram_tensor("xout", [T, D], F32, kind="ExternalOutput").ap()
    stack = ExitStack()
    with stack:
        p = Prog(nc)
        ctx = Ctx(nc, p, stack)
        yb_fm = ctx.sb("yb_fm", [128, DC * T], BF16)
        out_fm = ctx.sb("out_fm", [128, DC * T], BF16)
        xr = x.rearrange("(n p) d -> n p d", p=128)
        xor_ = xout.rearrange("(n p) d -> n p d", p=128)
        with ExitStack() as sc0:
            x_tm = ctx.sb("x_tm", [128, NT * D], F32, stack=sc0)
            for i in range(NT):
                p.dma("sync", x_tm[:, i * D:(i + 1) * D], xr[i],
                      writes=[(x_tm, f"t{i}")], tag=f"ldx{i}")
            trps = [ctx.ps_bf16(f"trps{i}", stack=sc0) for i in range(2)]
            ln_prologue(ctx, x_tm, list(range(NT)), lambda i: f"t{i}",
                        yb_fm, 0, T, "p", trps, sc0)
        with ExitStack() as sqa:
            emit_l2_attn(ctx, W, yb_fm, out_fm, sqa)
        with ExitStack() as sc:
            ps = [ctx.ps_f32(f"op{i}", stack=sc) for i in range(4)]
            ow = ctx.sb("ow", [128, DC * D], BF16, stack=sc)
            owr = W["o_w"].rearrange("(c p) o -> c p o", p=128)
            for c in range(DC):
                p.dma("gpsimd", ow[:, c * D:(c + 1) * D], owr[c],
                      writes=[(ow, c)], tag=f"low{c}")
            obr = ctx.sb("obr", [1, D], F32, stack=sc)
            p.dma("sync", obr[:], W["o_b"], writes=[obr], tag="lob")
            ob_bc = ctx.broadcast_row(obr, D, "obbc", [ps[3]],
                                      stack=sc, row_reads=[obr])
            z_sb = ctx.sb("z_sb", [128, 4 * D], F32, stack=sc)
            xg = ctx.sb("xg", [128, 4 * D], F32, stack=sc)
            for i in range(NT):
                j = i % 4
                for half in range(2):
                    bank = ps[half]
                    pairs = [(out_fm[:, cc * T + i * 128:
                                     cc * T + (i + 1) * 128],
                              ow[:, cc * D + half * 512:
                                 cc * D + half * 512 + 512],
                              [(out_fm, f"{cc}_{(i * 128) // 512}"),
                               (ow, cc)]) for cc in range(DC)]
                    mm_accum(p, bank, "s", pairs, 0, 512,
                             tag=f"mz{i}h{half}")
                    p.emit("vector", lambda e, bank=bank, j=j, half=half:
                           e.tensor_add(
                               z_sb[:, j * D + half * 512:
                                    j * D + half * 512 + 512],
                               bank[:, 0:512],
                               ob_bc[:, half * 512:half * 512 + 512]),
                           reads=[(bank, "s"), (ob_bc, f"c{half * 512}")],
                           writes=[(z_sb, f"t{j}")], tag=f"ez{i}{half}")
                if j == 3:
                    g = i // 4
                    with ExitStack() as esc:
                        for jj in range(4):
                            p.dma("sync", xg[:, jj * D:(jj + 1) * D],
                                  xr[g * 4 + jj],
                                  writes=[(xg, f"t{jj}")], tag=f"eldx{i}{jj}")
                        sandwich_epilogue(ctx, z_sb, [0, 1, 2, 3], xg,
                                          [0, 1, 2, 3], lambda t: f"t{t}",
                                          f"ep{g}", esc)
                        for jj in range(4):
                            p.dma("sync", xor_[g * 4 + jj],
                                  xg[:, jj * D:(jj + 1) * D],
                                  reads=[(xg, f"t{jj}")], tag=f"estx{i}{jj}")
        p.finalize(stack)
    return nc


def run_l2_attn(x_flat, inputs):
    f32 = lambda k: np.asarray(inputs[k], np.float32)
    wmap = {
        "caT": np.ascontiguousarray(f32("L2_attn_c_attn_w").T),
        "qw": np.ascontiguousarray(f32("L2_attn_q_w")),
        "kw": np.ascontiguousarray(f32("L2_attn_k_w")),
        "vw": np.ascontiguousarray(f32("L2_attn_v_w")),
        "cab": f32("L2_attn_c_attn_b").reshape(1, 3072),
        "qb": f32("L2_attn_q_b").reshape(1, D),
        "kb": f32("L2_attn_k_b").reshape(1, D),
        "vb": f32("L2_attn_v_b").reshape(1, D),
        "o_w": np.ascontiguousarray(f32("L2_attn_o_w")),
        "o_b": f32("L2_attn_o_b").reshape(1, D),
    }
    return _run_sharded("l2_attn", build_l2_attn_prog, x_flat, wmap)


def build_l3_attn_a_prog():
    nc = bass.Bass("TRN2", target_bir_lowering=False)
    x1 = nc.dram_tensor("x1", [T, D], F32, kind="ExternalInput").ap()
    x2 = nc.dram_tensor("x2", [T, D], F32, kind="ExternalInput").ap()
    specs = [("wq", [D, 512], F32), ("wk", [D, 512], F32),
             ("wv", [D, 512], F32), ("bq", [1, 512], F32),
             ("bk", [1, 512], F32), ("bv", [1, 512], F32)]
    W = {n: nc.dram_tensor(n, shp, dt, kind="ExternalInput").ap()
         for n, shp, dt in specs}
    oat = nc.dram_tensor("oat", [2 * T, 512], BF16,
                         kind="ExternalOutput").ap()
    stack = ExitStack()
    with stack:
        p = Prog(nc)
        ctx = Ctx(nc, p, stack)
        emit_l3_attn_core(ctx, W, x1, x2, oat)
        p.finalize(stack)
    return nc


def build_l3_attn_b_prog():
    nc = bass.Bass("TRN2", target_bir_lowering=False)
    x = nc.dram_tensor("x", [T, D], F32, kind="ExternalInput").ap()
    oat = nc.dram_tensor("oat", [T, D], BF16, kind="ExternalInput").ap()
    W = {"o_w": nc.dram_tensor("o_w", [D, D], F32, kind="ExternalInput").ap(),
         "o_b": nc.dram_tensor("o_b", [1, D], F32, kind="ExternalInput").ap()}
    xout = nc.dram_tensor("xout", [T, D], F32, kind="ExternalOutput").ap()
    stack = ExitStack()
    with stack:
        p = Prog(nc)
        ctx = Ctx(nc, p, stack)
        x_tm = ctx.sb("x_tm", [128, NT * D], F32)
        xr = x.rearrange("(n p) d -> n p d", p=128)
        xor_ = xout.rearrange("(n p) d -> n p d", p=128)
        for i in range(NT):
            p.dma("sync", x_tm[:, i * D:(i + 1) * D], xr[i],
                  writes=[(x_tm, f"t{i}")], tag=f"ldx{i}")
        emit_l3_attn_proj(ctx, W, x_tm, oat)
        for i in range(NT):
            p.dma("sync", xor_[i], x_tm[:, i * D:(i + 1) * D],
                  reads=[(x_tm, f"t{i}")], tag=f"stx{i}")
        p.finalize(stack)
    return nc


def run_l3_attn(x_flat, inputs):
    f32 = lambda k: np.asarray(inputs[k], np.float32)
    aw = f32("L3_attn_attn_W")
    ab = f32("L3_attn_attn_b")
    if "l3_attn_a" not in _PROG_CACHE:
        _PROG_CACHE["l3_attn_a"] = build_l3_attn_a_prog()
        _PROG_CACHE["l3_attn_b"] = build_l3_attn_b_prog()
    in_maps = []
    for c in range(N_CORES):
        heads = range(0, 8) if c % 2 == 0 else range(8, 16)
        qc = [h * 192 + s for h in heads for s in range(64)]
        kc = [h * 192 + 64 + s for h in heads for s in range(64)]
        vc = [h * 192 + 128 + s for h in heads for s in range(64)]
        b = c // 2
        in_maps.append({
            "x1": np.ascontiguousarray(x_flat[b * 4096:b * 4096 + T]),
            "x2": np.ascontiguousarray(x_flat[b * 4096 + T:(b + 1) * 4096]),
            "wq": np.ascontiguousarray(aw[:, qc]),
            "wk": np.ascontiguousarray(aw[:, kc]),
            "wv": np.ascontiguousarray(aw[:, vc]),
            "bq": np.ascontiguousarray(ab[qc]).reshape(1, 512),
            "bk": np.ascontiguousarray(ab[kc]).reshape(1, 512),
            "bv": np.ascontiguousarray(ab[vc]).reshape(1, 512),
        })
    res = run_bass_kernel_spmd(_PROG_CACHE["l3_attn_a"], in_maps,
                               core_ids=list(range(N_CORES)))
    oats = [r["oat"] for r in res.results]
    wmap = {"o_w": np.ascontiguousarray(f32("L3_attn_out_W")),
            "o_b": f32("L3_attn_out_b").reshape(1, D)}
    in_maps2 = []
    for c in range(N_CORES):
        b = c // 2
        r0 = 0 if c % 2 == 0 else T
        oat_own = np.concatenate(
            [np.asarray(oats[2 * b])[r0:r0 + T],
             np.asarray(oats[2 * b + 1])[r0:r0 + T]], axis=1)
        in_maps2.append(dict(
            wmap, x=np.ascontiguousarray(x_flat[c * T:(c + 1) * T]),
            oat=np.ascontiguousarray(oat_own)))
    res2 = run_bass_kernel_spmd(_PROG_CACHE["l3_attn_b"], in_maps2,
                                core_ids=list(range(N_CORES)))
    return np.concatenate([r["xout"] for r in res2.results], axis=0)


def build_l3_ffn_prog():
    nc = bass.Bass("TRN2", target_bir_lowering=False)
    x = nc.dram_tensor("x", [T, D], F32, kind="ExternalInput").ap()
    specs = [("gate_wT", [D, 16], F32), ("wgT", [16, D, D], F32),
             ("w1T", [16, D, D], F32), ("w2T", [16, D, D], F32),
             ("sgwT", [D, 2048], F32), ("s1wT", [D, 2048], F32),
             ("s2wT", [2048, D], F32), ("sggT", [D, 1], F32)]
    W = {n: nc.dram_tensor(n, shp, dt, kind="ExternalInput").ap()
         for n, shp, dt in specs}
    xout = nc.dram_tensor("xout", [T, D], F32, kind="ExternalOutput").ap()
    stack = ExitStack()
    with stack:
        p = Prog(nc)
        ctx = Ctx(nc, p, stack)
        yb_fm = ctx.sb("yb_fm", [128, DC * T], BF16)
        acc = ctx.sb("acc", [128, NT * D], F32)
        xr = x.rearrange("(n p) d -> n p d", p=128)
        xor_ = xout.rearrange("(n p) d -> n p d", p=128)
        with ExitStack() as sc0:
            x_tm = ctx.sb("x_tm", [128, NT * D], F32, stack=sc0)
            for i in range(NT):
                p.dma("sync", x_tm[:, i * D:(i + 1) * D], xr[i],
                      writes=[(x_tm, f"t{i}")], tag=f"ldx{i}")
            trps = [ctx.ps_bf16(f"trps{i}", stack=sc0) for i in range(2)]
            ln_prologue(ctx, x_tm, list(range(NT)), lambda i: f"t{i}",
                        yb_fm, 0, T, "p", trps, sc0)
        emit_l3_ffn(ctx, W, yb_fm, acc)
        for g in range(4):
            with ExitStack() as esc:
                xg = ctx.sb(f"xg{g}", [128, 4 * D], F32, stack=esc)
                for j in range(4):
                    i = g * 4 + j
                    p.dma("sync", xg[:, j * D:(j + 1) * D], xr[i],
                          writes=[(xg, f"t{j}")], tag=f"eldx{i}")
                sandwich_epilogue(ctx, acc, [g * 4 + j for j in range(4)],
                                  xg, [0, 1, 2, 3], lambda i: f"t{i}",
                                  f"ep{g}", esc)
                for j in range(4):
                    i = g * 4 + j
                    p.dma("sync", xor_[i], xg[:, j * D:(j + 1) * D],
                          reads=[(xg, f"t{j}")], tag=f"estx{i}")
        p.finalize(stack)
    return nc


def _shard_tokens(a):
    return [np.ascontiguousarray(a[i * T:(i + 1) * T]) for i in range(N_CORES)]


def run_l1_ffn(x_flat, inputs):
    wmap = {
        "wgT": np.ascontiguousarray(np.asarray(inputs["L1_ffn_w_gate"]).T),
        "wuT": np.ascontiguousarray(np.asarray(inputs["L1_ffn_w_up"]).T),
        "wdT": np.ascontiguousarray(np.asarray(inputs["L1_ffn_w_down"]).T),
    }
    specs = [("wgT", [D, 3072], I32), ("wuT", [D, 3072], I32),
             ("wdT", [3072, D], I32)]
    return _run_sharded(
        "l1_ffn",
        lambda: build_tokenlocal_prog(specs, lambda ctx, W, yb, xb, tl, kf:
                                      emit_l1_ffn(ctx, W, yb, xb, tl, kf)),
        x_flat, wmap)


def run_l2_ffn(x_flat, inputs):
    f32 = lambda k: np.asarray(inputs[k], np.float32)
    wmap = {
        "base_WT": np.ascontiguousarray(f32("L2_ffn_base_W").T),
        "base_b": f32("L2_ffn_base_b").reshape(1, D),
        "sA_T": np.ascontiguousarray(f32("L2_ffn_shared_A").T),
        "sB_T": np.ascontiguousarray(f32("L2_ffn_shared_B").T),
        "A_stackT": np.ascontiguousarray(
            f32("L2_ffn_routed_A").transpose(2, 0, 1).reshape(D, 96)),
        "B_stackT": np.ascontiguousarray(
            f32("L2_ffn_routed_B").transpose(0, 2, 1).reshape(96, D)),
        "rw_T": np.ascontiguousarray(f32("L2_ffn_router_W").T),
        "router_b": f32("L2_ffn_router_b").reshape(1, 6),
    }
    specs = [("base_WT", [D, D], F32), ("base_b", [1, D], F32),
             ("sA_T", [D, 16], F32), ("sB_T", [16, D], F32),
             ("A_stackT", [D, 96], F32), ("B_stackT", [96, D], F32),
             ("rw_T", [D, 6], F32), ("router_b", [1, 6], F32)]
    return _run_sharded(
        "l2_ffn",
        lambda: build_tokenlocal_prog(specs, lambda ctx, W, yb, xb, tl, kf:
                                      emit_l2_ffn(ctx, W, yb, xb, tl, kf)),
        x_flat, wmap)


def run_l3_ffn(x_flat, inputs):
    f32 = lambda k: np.asarray(inputs[k], np.float32)
    wmap = {
        "gate_wT": np.ascontiguousarray(f32("L3_ffn_gate_w").T),
        "wgT": np.ascontiguousarray(f32("L3_ffn_w_gate").transpose(0, 2, 1)),
        "w1T": np.ascontiguousarray(f32("L3_ffn_w1").transpose(0, 2, 1)),
        "w2T": np.ascontiguousarray(f32("L3_ffn_w2").transpose(0, 2, 1)),
        "sgwT": np.ascontiguousarray(f32("L3_ffn_shared_w_gate").T),
        "s1wT": np.ascontiguousarray(f32("L3_ffn_shared_w1").T),
        "s2wT": np.ascontiguousarray(f32("L3_ffn_shared_w2").T),
        "sggT": np.ascontiguousarray(f32("L3_ffn_shared_gate_w").T),
    }
    return _run_sharded("l3_ffn", build_l3_ffn_prog, x_flat, wmap)



FUSED_NBLOCKS = 8


def _fused_weight_specs():
    s = []
    s += [("a0_" + n, shp, F32) for n, shp in
          [("w_qkv", [D, 1536]), ("b_qkv", [1, 1536]), ("w_fc", [512, D]),
           ("b_fc", [1, D]), ("gamma", [1, D]), ("beta", [1, D])]]
    s += [("f0_" + n, shp, F32) for n, shp in
          [("base_wT", [D, D]), ("base_b", [1, D]), ("gate_wT", [D, 10]),
           ("A_stackT", [D, 128]), ("B_stackT", [128, D])]]
    s += [("a1_" + n, shp, F32) for n, shp in
          [("cl", [D, 256]), ("cr", [D, 256]), ("thr", [1, 1]),
           ("gb", [1, 1])]]
    s += [("f1_" + n, shp, I32) for n, shp in
          [("wgT", [D, 3072]), ("wuT", [D, 3072]), ("wdT", [3072, D])]]
    s += [("a2_" + n, shp, F32) for n, shp in
          [("caT", [3072, D]), ("qw", [D, D]), ("kw", [D, D]),
           ("vw", [D, D]), ("cab", [1, 3072]), ("qb", [1, D]),
           ("kb", [1, D]), ("vb", [1, D]), ("o_w", [D, D]), ("o_b", [1, D])]]
    s += [("f2_" + n, shp, F32) for n, shp in
          [("base_WT", [D, D]), ("base_b", [1, D]), ("sA_T", [D, 16]),
           ("sB_T", [16, D]), ("A_stackT", [D, 96]), ("B_stackT", [96, D]),
           ("rw_T", [D, 6]), ("router_b", [1, 6])]]
    s += [("a3_" + n, shp, F32) for n, shp in
          [("wq", [D, 512]), ("wk", [D, 512]), ("wv", [D, 512]),
           ("bq", [1, 512]), ("bk", [1, 512]), ("bv", [1, 512]),
           ("o_w", [D, D]), ("o_b", [1, D]), ("m", [128, 1]),
           ("m2", [128, 1])]]
    s += [("f3_" + n, shp, F32) for n, shp in
          [("gate_wT", [D, 16]), ("wgT", [16, D, D]), ("w1T", [16, D, D]),
           ("w2T", [16, D, D]), ("sgwT", [D, 2048]), ("s1wT", [D, 2048]),
           ("s2wT", [2048, D]), ("sggT", [D, 1])]]
    return s


def build_fused_prog(nblocks=8):
    nc = bass.Bass("TRN2", target_bir_lowering=False)
    x = nc.dram_tensor("x", [T, D], F32, kind="ExternalInput").ap()
    W = {}
    for n, shp, dt in _fused_weight_specs():
        W[n] = nc.dram_tensor(n, shp, dt, kind="ExternalInput").ap()
    xout = nc.dram_tensor("xout", [T, D], F32, kind="ExternalOutput").ap()
    Wp = lambda pre: {k[len(pre):]: v for k, v in W.items()
                      if k.startswith(pre)}
    ex0i = nc.dram_tensor("ex0i", [T, D], BF16, kind="Internal")
    ex0o = nc.dram_tensor("ex0o", [2 * T, D], BF16, kind="Internal")
    ex1i = nc.dram_tensor("ex1i", [T, D], BF16, kind="Internal")
    ex1o = nc.dram_tensor("ex1o", [2 * T, D], BF16, kind="Internal")
    ex2i = nc.dram_tensor("ex2i", [T, D], BF16, kind="Internal")
    ex2o = nc.dram_tensor("ex2o", [2 * T, D], BF16, kind="Internal")
    ex3i = nc.dram_tensor("ex3i", [2 * T, 512], BF16, kind="Internal")
    ex3o = nc.dram_tensor("ex3o", [4 * T, 512], BF16, kind="Internal")
    xsp = nc.dram_tensor("xsp", [T, D], F32, kind="Internal")
    xspr = xsp.ap().rearrange("(n p) d -> n p d", p=128)
    stack = ExitStack()
    with stack:
        p = Prog(nc)
        ctx = Ctx(nc, p, stack)
        slab = ctx.sb("slab", [128, NT * D], F32)
        qk = slab.bitcast(BF16)
        keyf = lambda i: f"t{i}"
        xr = x.rearrange("(n p) d -> n p d", p=128)
        xor_ = xout.rearrange("(n p) d -> n p d", p=128)
        for i in range(NT):
            p.dma("sync", slab[:, i * D:(i + 1) * D], xr[i],
                  writes=[(slab, f"t{i}")], tag=f"ldx{i}")

        def spill():
            for i in range(NT):
                p.dma("sync", xspr[i], slab[:, i * D:(i + 1) * D],
                      reads=[(slab, f"t{i}")], writes=[(xsp, i)],
                      tag=f"sp{i}")

        def reload():
            for i in range(NT):
                p.dma("sync", slab[:, i * D:(i + 1) * D], xspr[i],
                      reads=[(xsp, i)], writes=[(slab, f"t{i}")],
                      tag=f"rl{i}")

        def store_out():
            for i in range(NT):
                p.dma("sync", xor_[i], slab[:, i * D:(i + 1) * D],
                      reads=[(slab, f"t{i}")], tag=f"stx{i}")

        nb = 0
        with ExitStack() as b:
            emit_l0_attn(ctx, Wp("a0_"), slab, None, prefix="a0", scope=b,
                         co=ex0o, exch=(ex0i, ex0o))
        nb += 1
        if nblocks <= nb:
            store_out()
            p.finalize(stack)
            return nc
        with ExitStack() as b:
            yb = ctx.sb("f0_yb", [128, DC * T], BF16, stack=b)
            with ExitStack() as sc0:
                trps = [ctx.ps_bf16(f"f0_tr{i}", stack=sc0)
                        for i in range(2)]
                ln_prologue(ctx, slab, list(range(NT)), keyf, yb, 0, T,
                            "f0p", trps, sc0)
            emit_l0_ffn(ctx, Wp("f0_"), yb, slab, list(range(NT)), keyf)
        nb += 1
        if nblocks <= nb:
            store_out()
            p.finalize(stack)
            return nc
        with ExitStack() as b:
            emit_l1_attn(ctx, Wp("a1_"), slab, None, prefix="a1", scope=b,
                         co=ex1o, exch=(ex1i, ex1o))
        nb += 1
        if nblocks <= nb:
            store_out()
            p.finalize(stack)
            return nc
        with ExitStack() as b:
            yb = ctx.sb("f1_yb", [128, DC * T], BF16, stack=b)
            with ExitStack() as sc0:
                trps = [ctx.ps_bf16(f"f1_tr{i}", stack=sc0)
                        for i in range(2)]
                ln_prologue(ctx, slab, list(range(NT)), keyf, yb, 0, T,
                            "f1p", trps, sc0)
            emit_l1_ffn(ctx, Wp("f1_"), yb, slab, list(range(NT)), keyf)
        nb += 1
        if nblocks <= nb:
            store_out()
            p.finalize(stack)
            return nc
        with ExitStack() as b:
            out2 = ctx.sb("a2_out", [128, DC * T], BF16, stack=b)
            with ExitStack() as by:
                yb = ctx.sb("a2_yb", [128, DC * T], BF16, stack=by)
                with ExitStack() as sc0:
                    trps = [ctx.ps_bf16(f"a2_tr{i}", stack=sc0)
                            for i in range(2)]
                    ln_prologue(ctx, slab, list(range(NT)), keyf, yb, 0, T,
                                "a2p", trps, sc0)
                spill()
                q_ap = qk[:, 0:DC * T]
                k_ap = qk[:, DC * T:2 * DC * T]
                p.transition([q_ap, k_ap], [slab])
                emit_l2_attn(ctx, Wp("a2_"), yb, out2, by, prefix="a2",
                             q_fm=q_ap, k_fm=k_ap)
            p.transition([slab], [q_ap, k_ap])
            reload()
            ps_ = [ctx.ps_f32(f"a2o_ps{i}", stack=b) for i in range(4)]
            ow = ctx.sb("a2o_ow", [128, DC * D], BF16, stack=b)
            owr = W["a2_o_w"].rearrange("(c p) o -> c p o", p=128)
            for c in range(DC):
                p.dma("gpsimd", ow[:, c * D:(c + 1) * D], owr[c],
                      writes=[(ow, c)], tag=f"a2o_low{c}")
            obr = ctx.sb("a2o_obr", [1, D], F32, stack=b)
            p.dma("sync", obr[:], W["a2_o_b"], writes=[obr], tag="a2o_lob")
            ob_bc = ctx.broadcast_row(obr, D, "a2o_obbc", [ps_[3]],
                                      stack=b, row_reads=[obr])
            z_sb = ctx.sb("a2o_z", [128, 4 * D], F32, stack=b)
            for i in range(NT):
                j = i % 4
                for half in range(2):
                    bank = ps_[(2 * i + half) % 4]
                    pairs = [(out2[:, cc * T + i * 128:cc * T + (i + 1) * 128],
                              ow[:, cc * D + half * 512:
                                 cc * D + half * 512 + 512],
                              [(out2, f"{cc}_{(i * 128) // 512}"), (ow, cc)])
                             for cc in range(DC)]
                    mm_accum(p, bank, "s", pairs, 0, 512,
                             tag=f"a2o_mz{i}h{half}")
                    p.emit("vector", lambda e, bank=bank, j=j, half=half:
                           e.tensor_add(
                               z_sb[:, j * D + half * 512:
                                    j * D + half * 512 + 512],
                               bank[:, 0:512],
                               ob_bc[:, half * 512:half * 512 + 512]),
                           reads=[(bank, "s"), (ob_bc, f"c{half * 512}")],
                           writes=[(z_sb, f"t{j}")], tag=f"a2o_ez{i}{half}")
                if j == 3:
                    with ExitStack() as esc:
                        sandwich_epilogue(ctx, z_sb, [0, 1, 2, 3], slab,
                                          [i - 3, i - 2, i - 1, i], keyf,
                                          f"a2o_e{i // 4}", esc)
        nb += 1
        if nblocks <= nb:
            store_out()
            p.finalize(stack)
            return nc
        with ExitStack() as b:
            yb = ctx.sb("f2_yb", [128, DC * T], BF16, stack=b)
            with ExitStack() as sc0:
                trps = [ctx.ps_bf16(f"f2_tr{i}", stack=sc0)
                        for i in range(2)]
                ln_prologue(ctx, slab, list(range(NT)), keyf, yb, 0, T,
                            "f2p", trps, sc0)
            emit_l2_ffn(ctx, Wp("f2_"), yb, slab, list(range(NT)), keyf)
        nb += 1
        if nblocks <= nb:
            store_out()
            p.finalize(stack)
            return nc
        with ExitStack() as b:
            with ExitStack() as sc0:
                ybo = ctx.sb("a3_ybo", [128, NT * D], BF16, stack=sc0)
                st3 = ctx.sb("a3_st", [128, 2 * NT], F32, stack=sc0)
                ln_stats(ctx, slab, list(range(NT)), keyf, st3, 1e-5,
                         "a3_s0", sc0)
                ln_apply_bf16(ctx, slab, list(range(NT)), keyf, st3, ybo,
                              list(range(NT)), "a3p")
                emit_exchange_ag(ctx, ybo, NT, ex2i, ex2o, "a3_ex")
            spill()
            p.transition([qk], [slab])
            with ExitStack() as bcore:
                emit_l3_attn_core_fused(ctx, Wp("a3_"), ex2o, ex3i, ex3o, qk,
                                        scope=bcore)
            p.transition([slab], [qk])
            reload()
            emit_l3_attn_proj_fused(ctx, Wp("a3_"), slab, ex3o,
                                    W["a3_m"], W["a3_m2"], b)
        nb += 1
        if nblocks <= nb:
            store_out()
            p.finalize(stack)
            return nc
        with ExitStack() as b:
            yb = ctx.sb("f3_yb", [128, DC * T], BF16, stack=b)
            with ExitStack() as sc0:
                trps = [ctx.ps_bf16(f"f3_tr{i}", stack=sc0) for i in range(2)]
                ln_prologue(ctx, slab, list(range(NT)), keyf, yb, 0, T,
                            "f3p", trps, sc0)
            spill()
            p.transition([slab], [])
            emit_l3_ffn(ctx, Wp("f3_"), yb, slab)
            for g in range(4):
                with ExitStack() as esc:
                    xg = ctx.sb(f"f3_xg{g}", [128, 4 * D], F32, stack=esc)
                    for j in range(4):
                        i = g * 4 + j
                        p.dma("sync", xg[:, j * D:(j + 1) * D], xspr[i],
                              reads=[(xsp, i)], writes=[(xg, f"t{j}")],
                              tag=f"f3_eldx{i}")
                    sandwich_epilogue(ctx, slab, [g * 4 + j for j in range(4)],
                                      xg, [0, 1, 2, 3], lambda t: f"t{t}",
                                      f"f3_ep{g}", esc)
                    for j in range(4):
                        i = g * 4 + j
                        p.dma("sync", xor_[i], xg[:, j * D:(j + 1) * D],
                              reads=[(xg, f"t{j}")], tag=f"f3_estx{i}")
        p.finalize(stack)
    return nc


def _prep_fused_inmaps(inputs):
    f32 = lambda k: np.asarray(inputs[k], np.float32)
    com = {}
    com.update({"a0_" + k: v for k, v in {
        "w_qkv": np.ascontiguousarray(f32("L0_attn_w_qkv")),
        "b_qkv": f32("L0_attn_b_qkv").reshape(1, 1536),
        "w_fc": np.ascontiguousarray(f32("L0_attn_w_fc")),
        "b_fc": f32("L0_attn_b_fc").reshape(1, D),
        "gamma": f32("L0_attn_gamma").reshape(1, D),
        "beta": f32("L0_attn_beta").reshape(1, D)}.items()})
    com.update({"f0_" + k: v for k, v in {
        "base_wT": np.ascontiguousarray(f32("L0_ffn_base_w").T),
        "base_b": f32("L0_ffn_base_b").reshape(1, D),
        "gate_wT": np.ascontiguousarray(f32("L0_ffn_gate_w").T),
        "A_stackT": np.ascontiguousarray(
            f32("L0_ffn_lora_A").transpose(2, 0, 1).reshape(D, 128)),
        "B_stackT": np.ascontiguousarray(
            f32("L0_ffn_lora_B").transpose(0, 2, 1).reshape(128, D))}.items()})
    com.update({"a1_" + k: v for k, v in {
        "cl": np.ascontiguousarray(f32("L1_attn_compare_left")),
        "cr": np.ascontiguousarray(f32("L1_attn_compare_right")),
        "thr": f32("L1_attn_threshold").reshape(1, 1),
        "gb": f32("L1_attn_gate_bias").reshape(1, 1)}.items()})
    com.update({"f1_" + k: v for k, v in {
        "wgT": np.ascontiguousarray(np.asarray(inputs["L1_ffn_w_gate"]).T),
        "wuT": np.ascontiguousarray(np.asarray(inputs["L1_ffn_w_up"]).T),
        "wdT": np.ascontiguousarray(
            np.asarray(inputs["L1_ffn_w_down"]).T)}.items()})
    com.update({"a2_" + k: v for k, v in {
        "caT": np.ascontiguousarray(f32("L2_attn_c_attn_w").T),
        "qw": np.ascontiguousarray(f32("L2_attn_q_w")),
        "kw": np.ascontiguousarray(f32("L2_attn_k_w")),
        "vw": np.ascontiguousarray(f32("L2_attn_v_w")),
        "cab": f32("L2_attn_c_attn_b").reshape(1, 3072),
        "qb": f32("L2_attn_q_b").reshape(1, D),
        "kb": f32("L2_attn_k_b").reshape(1, D),
        "vb": f32("L2_attn_v_b").reshape(1, D),
        "o_w": np.ascontiguousarray(f32("L2_attn_o_w")),
        "o_b": f32("L2_attn_o_b").reshape(1, D)}.items()})
    com.update({"f2_" + k: v for k, v in {
        "base_WT": np.ascontiguousarray(f32("L2_ffn_base_W").T),
        "base_b": f32("L2_ffn_base_b").reshape(1, D),
        "sA_T": np.ascontiguousarray(f32("L2_ffn_shared_A").T),
        "sB_T": np.ascontiguousarray(f32("L2_ffn_shared_B").T),
        "A_stackT": np.ascontiguousarray(
            f32("L2_ffn_routed_A").transpose(2, 0, 1).reshape(D, 96)),
        "B_stackT": np.ascontiguousarray(
            f32("L2_ffn_routed_B").transpose(0, 2, 1).reshape(96, D)),
        "rw_T": np.ascontiguousarray(f32("L2_ffn_router_W").T),
        "router_b": f32("L2_ffn_router_b").reshape(1, 6)}.items()})
    com.update({"f3_" + k: v for k, v in {
        "gate_wT": np.ascontiguousarray(f32("L3_ffn_gate_w").T),
        "wgT": np.ascontiguousarray(f32("L3_ffn_w_gate").transpose(0, 2, 1)),
        "w1T": np.ascontiguousarray(f32("L3_ffn_w1").transpose(0, 2, 1)),
        "w2T": np.ascontiguousarray(f32("L3_ffn_w2").transpose(0, 2, 1)),
        "sgwT": np.ascontiguousarray(f32("L3_ffn_shared_w_gate").T),
        "s1wT": np.ascontiguousarray(f32("L3_ffn_shared_w1").T),
        "s2wT": np.ascontiguousarray(f32("L3_ffn_shared_w2").T),
        "sggT": np.ascontiguousarray(
            f32("L3_ffn_shared_gate_w").T)}.items()})
    aw = f32("L3_attn_attn_W")
    ab = f32("L3_attn_attn_b")
    x_flat = np.ascontiguousarray(
        np.asarray(inputs["L0_attn_x"], np.float32).reshape(-1, D))
    in_maps = []
    for c in range(N_CORES):
        heads = range(0, 8) if c % 2 == 0 else range(8, 16)
        qc = [h * 192 + s for h in heads for s in range(64)]
        kc = [h * 192 + 64 + s for h in heads for s in range(64)]
        vc = [h * 192 + 128 + s for h in heads for s in range(64)]
        m = 1.0 if c % 2 == 0 else 0.0
        im = dict(com)
        im.update({
            "x": np.ascontiguousarray(x_flat[c * T:(c + 1) * T]),
            "a3_wq": np.ascontiguousarray(aw[:, qc]),
            "a3_wk": np.ascontiguousarray(aw[:, kc]),
            "a3_wv": np.ascontiguousarray(aw[:, vc]),
            "a3_bq": np.ascontiguousarray(ab[qc]).reshape(1, 512),
            "a3_bk": np.ascontiguousarray(ab[kc]).reshape(1, 512),
            "a3_bv": np.ascontiguousarray(ab[vc]).reshape(1, 512),
            "a3_o_w": np.ascontiguousarray(f32("L3_attn_out_W")),
            "a3_o_b": f32("L3_attn_out_b").reshape(1, D),
            "a3_m": np.full((128, 1), m, np.float32),
            "a3_m2": np.full((128, 1), 1.0 - m, np.float32),
        })
        in_maps.append(im)
    return in_maps


def run_fused(inputs, nblocks=8):
    key = f"fused{nblocks}"
    if key not in _PROG_CACHE:
        _PROG_CACHE[key] = build_fused_prog(nblocks)
    nc = _PROG_CACHE[key]
    in_maps = _prep_fused_inmaps(inputs)
    res = run_bass_kernel_spmd(nc, in_maps, core_ids=list(range(N_CORES)))
    return np.concatenate([r["xout"] for r in res.results], axis=0)


def kernel(**inputs):
    x = run_fused(inputs)
    return np.ascontiguousarray(
        np.asarray(x, np.float32).reshape(4, 4096, D))


# version 11 (final)
# speedup vs baseline: 1.0347x; 1.0347x over previous
import math
from contextlib import ExitStack
import numpy as np
import ml_dtypes
import concourse.bass as bass
import concourse.mybir as mybir
from concourse.bass_utils import run_bass_kernel_spmd

F32 = mybir.dt.float32
BF16 = mybir.dt.bfloat16
I32 = mybir.dt.int32
I8 = mybir.dt.int8
AF = mybir.ActivationFunctionType
ALU = mybir.AluOpType
AX = mybir.AxisListType
BF16NP = ml_dtypes.bfloat16

N_CORES = 8
D = 1024
T = 2048
NT = T // 128
DC = D // 128

WHOLE = "__whole__"
SERIALIZE_DMA = True
DMA_LANES = 8
L2_BISECT = ""


class _Instr:
    __slots__ = ("eng", "fn", "reads", "writes", "dma", "idx", "deps",
                 "milestone", "val", "tag")

    def __init__(self, eng, fn, reads, writes, dma, idx, tag):
        self.eng = eng
        self.fn = fn
        self.reads = reads
        self.writes = writes
        self.dma = dma
        self.idx = idx
        self.deps = []
        self.milestone = False
        self.val = None
        self.tag = tag


def _regions(lst):
    out = []
    for it in lst:
        if isinstance(it, tuple) and len(it) == 2 and not isinstance(it[0], str):
            out.append((id(it[0]), it[1]))
        else:
            out.append((id(it), WHOLE))
    return out


class Prog:

    ENGINES = ("tensor", "vector", "scalar", "gpsimd", "sync")

    def __init__(self, nc):
        self.nc = nc
        self.instrs = {e: [] for e in self.ENGINES}
        self.order = []
        self.mem = {}
        self.n = 0
        self.freed = {"sbuf": [], "psum": []}

    def _buf_state(self, bid):
        st = self.mem.get(bid)
        if st is None:
            st = {"whole_w": None, "w": {}, "whole_r": [], "r": {},
                  "phantom": []}
            self.mem[bid] = st
        return st

    def note_alloc(self, buf, space):
        st = self._buf_state(id(buf))
        st["phantom"] = list(self.freed[space])

    def note_free(self, bufs, space):
        acc = self.freed[space]
        for buf in bufs:
            st = self.mem.get(id(buf))
            if st is None:
                continue
            if st["whole_w"] is not None:
                acc.append(st["whole_w"])
            acc.extend(st["w"].values())
            acc.extend(st["whole_r"])
            for lst in st["r"].values():
                acc.extend(lst)

    def transition(self, dsts, srcs):
        acc = []
        for buf in list(srcs) + list(dsts):
            st = self.mem.get(id(buf))
            if st is None:
                continue
            if st["whole_w"] is not None:
                acc.append(st["whole_w"])
            acc.extend(st["w"].values())
            acc.extend(st["whole_r"])
            for lst in st["r"].values():
                acc.extend(lst)
            acc.extend(st["phantom"])
            self.mem[id(buf)] = {"whole_w": None, "w": {}, "whole_r": [],
                                 "r": {}, "phantom": []}
        for dst in dsts:
            self.mem[id(dst)] = {"whole_w": None, "w": {}, "whole_r": [],
                                 "r": {}, "phantom": list(acc)}

    def emit(self, eng, fn, reads=(), writes=(), dma=False, tag=""):
        ins = _Instr(eng, fn, _regions(reads), _regions(writes), dma,
                     self.n, tag)
        self.n += 1
        deps = []
        for bid, key in ins.reads:
            st = self._buf_state(bid)
            if st["whole_w"] is not None:
                deps.append(st["whole_w"])
            if key == WHOLE:
                deps.extend(st["w"].values())
            elif key in st["w"]:
                deps.append(st["w"][key])
        for bid, key in ins.writes:
            st = self._buf_state(bid)
            deps.extend(st["phantom"])
            if st["whole_w"] is not None:
                deps.append(st["whole_w"])
            if key == WHOLE:
                deps.extend(st["w"].values())
            elif key in st["w"]:
                deps.append(st["w"][key])
            deps.extend(st["whole_r"])
            if key == WHOLE:
                for lst in st["r"].values():
                    deps.extend(lst)
            elif key in st["r"]:
                deps.extend(st["r"][key])
        ins.deps = []
        seen = set()
        for d in deps:
            if d is ins or id(d) in seen:
                continue
            if d.eng == "tensor" and eng == "tensor" and not d.dma and not dma:
                continue
            seen.add(id(d))
            ins.deps.append(d)
        for bid, key in ins.reads:
            st = self._buf_state(bid)
            if key == WHOLE:
                st["whole_r"].append(ins)
            else:
                st["r"].setdefault(key, []).append(ins)
        for bid, key in ins.writes:
            st = self._buf_state(bid)
            if key == WHOLE:
                st["whole_w"] = ins
                st["w"].clear()
                st["whole_r"] = []
                st["r"].clear()
                st["phantom"] = []
            else:
                st["w"][key] = ins
                if key in st["r"]:
                    st["r"][key] = []
        self.instrs[eng].append(ins)
        self.order.append(ins)
        return ins

    def dma(self, eng, out, in_, reads=(), writes=(), tag="", **kw):
        return self.emit(
            eng, lambda e: e.dma_start(out=out, in_=in_, **kw),
            reads=reads, writes=writes, dma=True, tag=tag)

    def finalize(self, stack):
        nc = self.nc
        for ins in self.order:
            for d in ins.deps:
                d.milestone = True
        for ins in self.order:
            if ins.dma:
                ins.milestone = True
        R = DMA_LANES
        sems = {}
        for eng in self.ENGINES:
            if any((not i.dma) and i.milestone for i in self.instrs[eng]):
                sems[(eng, False)] = stack.enter_context(
                    nc.semaphore(f"sc_{eng}"))
            if any(i.dma for i in self.instrs[eng]):
                for r in range(R):
                    sems[(eng, True, r)] = stack.enter_context(
                        nc.semaphore(f"sd_{eng}{r}"))
        counts = {}
        dmacnt = {e: 0 for e in self.ENGINES}
        for ins in self.order:
            if ins.dma:
                i = dmacnt[ins.eng]
                dmacnt[ins.eng] += 1
                ins.val = ((ins.eng, True, i % R), 16 * (i // R + 1))
                counts[ins.val[0]] = ins.val[1]
            elif ins.milestone:
                k = (ins.eng, False)
                counts[k] = counts.get(k, 0) + 1
                ins.val = (k, counts[k])
        blk = stack.enter_context(nc.Block())
        for eng in self.ENGINES:
            ilist = self.instrs[eng]
            if not ilist:
                continue

            def body(engobj, ilist=ilist, eng=eng):
                waited = {}

                def wait(key, val):
                    if waited.get(key, 0) < val:
                        engobj.wait_ge(sems[key], val)
                        waited[key] = val

                ndma = 0
                nms = 0
                for ins in ilist:
                    for d in ins.deps:
                        wait(*d.val)
                    if ins.dma:
                        lane = ndma % R
                        prev = ndma // R
                        if prev > 0:
                            wait((eng, True, lane), 16 * prev)
                        ndma += 1
                    if (ins.milestone and not ins.dma and eng != "tensor"
                            and nms > 0):
                        wait((eng, False), nms)
                    if ins.milestone and not ins.dma:
                        nms += 1
                    inst = ins.fn(engobj)
                    if ins.milestone or ins.dma:
                        key = ins.val[0]
                        inst.then_inc(sems[key], 16 if ins.dma else 1)
                for r in range(R):
                    k = (eng, True, r)
                    if k in counts:
                        wait(k, counts[k])

            getattr(blk, eng)(body)
        return blk



class Ctx:
    def __init__(self, nc, p, stack):
        self.nc = nc
        self.p = p
        self.stack = stack
        self.ident = self.sb("c_ident", [128, 128], BF16)
        self.ones_row = self.sb("c_ones_row", [1, 128], F32)
        p.emit("gpsimd", lambda e: e.memset(self.ident[:], 0.0),
               writes=[self.ident])
        p.emit("gpsimd", lambda e: e.affine_select(
            out=self.ident[:], in_=self.ident[:],
            compare_op=ALU.not_equal, fill=1.0, base=0,
            pattern=[[-1, 128]], channel_multiplier=1),
            reads=[self.ident], writes=[self.ident])
        p.emit("gpsimd", lambda e: e.memset(self.ones_row[:], 1.0),
               writes=[self.ones_row])
        self._const_cols = {}

    def ident32(self):
        if getattr(self, "_id32", None) is None:
            t = self.sb("c_ident32", [128, 128], F32)
            self.p.emit("gpsimd", lambda e: e.memset(t[:], 0.0), writes=[t])
            self.p.emit("gpsimd", lambda e: e.affine_select(
                out=t[:], in_=t[:], compare_op=ALU.not_equal, fill=1.0,
                base=0, pattern=[[-1, 128]], channel_multiplier=1),
                reads=[t], writes=[t])
            self._id32 = t
        return self._id32

    def const_col(self, value):
        value = float(value)
        t = self._const_cols.get(value)
        if t is None:
            t = self.sb(f"c_col_{len(self._const_cols)}", [128, 1], F32)
            self.p.emit("gpsimd", lambda e: e.memset(t[:], value),
                        writes=[t])
            self._const_cols[value] = t
        return t

    def sb(self, name, shape, dt, stack=None):
        side = "left" if stack is None else "right"
        t = (stack or self.stack).enter_context(
            self.nc.sbuf_tensor(name, shape, dt, side=side))
        self.p.note_alloc(t, "sbuf")
        if stack is not None:
            stack.callback(lambda: self.p.note_free([t], "sbuf"))
        return t

    def _ps(self, name, shape, dt, stack):
        side = "left" if stack is None else "right"
        t = (stack or self.stack).enter_context(
            self.nc.psum_tensor(name, shape, dt, side=side))
        self.p.note_alloc(t, "psum")
        if stack is not None:
            stack.callback(lambda: self.p.note_free([t], "psum"))
        return t

    def ps_f32(self, name, stack=None):
        return self._ps(name, [128, 512], F32, stack)

    def ps_bf16(self, name, stack=None):
        return self._ps(name, [128, 1024], BF16, stack)

    def broadcast_row(self, row_sb, F, name, psum_pool, scale=1.0,
                      stack=None, row_reads=()):
        p = self.p
        bc = self.sb(name, [128, F], F32, stack=stack)
        for j0 in range(0, F, 512):
            w = min(512, F - j0)
            bank = psum_pool[(j0 // 512) % len(psum_pool)]
            p.emit("tensor", lambda e, j0=j0, w=w, bank=bank: e.matmul(
                bank[:, 0:w], self.ones_row[:], row_sb[0:1, j0:j0 + w],
                start=True, stop=True),
                reads=[self.ones_row] + list(row_reads), writes=[(bank, "s")],
                tag=f"{name}_mm")
            p.emit("vector", lambda e, j0=j0, w=w, bank=bank: e.tensor_copy(
                bc[:, j0:j0 + w], bank[:, 0:w]),
                reads=[(bank, "s")], writes=[(bc, f"c{j0}")],
                tag=f"{name}_cp")
        return bc



def ln_stats(ctx, src, src_tiles, src_keyf, stats, eps, prefix, scratch_stack):
    p = ctx.p
    n = len(src_tiles)
    bn = ctx.sb(f"{prefix}_bn", [128, 12], F32, stack=scratch_stack)
    mv = ctx.sb(f"{prefix}_mv", [128, 2 * n], F32, stack=scratch_stack)
    std = ctx.sb(f"{prefix}_std", [128, n], F32, stack=scratch_stack)
    for j, i in enumerate(src_tiles):
        sk = (src, src_keyf(i))
        p.emit("vector", lambda e, i=i: e.bn_stats(
            bn[:, 0:6], src[:, i * D:i * D + 512]),
            reads=[sk], writes=[(bn, 0)], tag=f"{prefix}_bns{i}a")
        p.emit("vector", lambda e, i=i: e.bn_stats(
            bn[:, 6:12], src[:, i * D + 512:(i + 1) * D]),
            reads=[sk], writes=[(bn, 1)], tag=f"{prefix}_bns{i}b")
        p.emit("vector", lambda e, j=j: e.bn_aggr(
            mv[:, 2 * j:2 * j + 2], bn[:, 0:12]),
            reads=[(bn, 0), (bn, 1)], writes=[(mv, j)],
            tag=f"{prefix}_bna{j}")
    mvreads = [(mv, j) for j in range(n)]
    epsc = ctx.const_col(eps)
    p.emit("scalar", lambda e: e.activation(
        out=std[:, 0:n], in_=mv[:, 1:2 * n:2], func=AF.Sqrt, bias=epsc[:]),
        reads=mvreads + [epsc], writes=[std], tag=f"{prefix}_sqrt")
    for j in range(n):
        p.emit("vector", lambda e, j=j: e.reciprocal(
            stats[:, 2 * j:2 * j + 1], std[:, j:j + 1]),
            reads=[std], writes=[(stats, 2 * j)], tag=f"{prefix}_rcp{j}")
        p.emit("vector", lambda e, j=j: e.scalar_tensor_tensor(
            out=stats[:, 2 * j + 1:2 * j + 2], in0=mv[:, 2 * j:2 * j + 1],
            scalar=-1.0, in1=stats[:, 2 * j:2 * j + 1],
            op0=ALU.mult, op1=ALU.mult),
            reads=[(mv, j), (stats, 2 * j)], writes=[(stats, 2 * j + 1)],
            tag=f"{prefix}_nmr{j}")


def ln_apply_bf16(ctx, src, src_tiles, src_keyf, stats, dst, dst_tiles,
                  prefix):
    p = ctx.p
    for j, (i, o) in enumerate(zip(src_tiles, dst_tiles)):
        p.emit("scalar", lambda e, i=i, o=o, j=j: e.activation(
            out=dst[:, o * D:(o + 1) * D], in_=src[:, i * D:(i + 1) * D],
            func=AF.Identity, scale=stats[:, 2 * j:2 * j + 1],
            bias=stats[:, 2 * j + 1:2 * j + 2]),
            reads=[(src, src_keyf(i)), (stats, 2 * j), (stats, 2 * j + 1)],
            writes=[(dst, f"t{o}")], tag=f"{prefix}_ln{o}")


def transpose_tm_to_fm(ctx, src_tm, src_tiles, dst_fm, dst_cols0, ntok_fm,
                       prefix, psum_pool, eng="vector"):
    p = ctx.p
    nb = len(psum_pool)
    bsel = 0
    n = len(src_tiles)
    for c in range(DC):
        for j0 in range(0, n, 4):
            grp = list(range(j0, min(j0 + 4, n)))
            bank = psum_pool[bsel % nb]
            bsel += 1
            for slot, j in enumerate(grp):
                i = src_tiles[j]
                p.emit("tensor", lambda e, i=i, c=c, bank=bank, slot=slot:
                       e.transpose(
                           bank[:, slot * 128:slot * 128 + 128],
                           src_tm[:, i * D + c * 128:i * D + (c + 1) * 128],
                           ctx.ident[:]),
                       reads=[(src_tm, f"t{i}"), ctx.ident],
                       writes=[(bank, "s")],
                       tag=f"{prefix}_tr{i}_{c}")
            w = len(grp) * 128
            p.emit(eng, lambda e, j0=j0, c=c, bank=bank, w=w:
                   e.tensor_copy(
                       dst_fm[:, c * ntok_fm + dst_cols0 + j0 * 128:
                              c * ntok_fm + dst_cols0 + j0 * 128 + w],
                       bank[:, 0:w]),
                   reads=[(bank, "s")],
                   writes=[(dst_fm, f"c{c}g{dst_cols0 + j0 * 128}")],
                   tag=f"{prefix}_tre{c}_{j0}")


def ln_prologue(ctx, x_src, x_tiles, x_keyf, yb_fm, fm_cols0, ntok_fm,
                prefix, psum_pool, scope, want_tm=False):
    n = len(x_tiles)
    stats = ctx.sb(f"{prefix}_st1", [128, 2 * n], F32, stack=scope)
    ln_stats(ctx, x_src, x_tiles, x_keyf, stats, 1e-5, f"{prefix}_s1", scope)
    yb_tm = ctx.sb(f"{prefix}_ybtm", [128, n * D], BF16, stack=scope)
    ln_apply_bf16(ctx, x_src, x_tiles, x_keyf, stats, yb_tm, list(range(n)),
                  prefix)
    transpose_tm_to_fm(ctx, yb_tm, list(range(n)), yb_fm, fm_cols0, ntok_fm,
                       prefix, psum_pool)
    return yb_tm if want_tm else None


def sandwich_epilogue(ctx, z_sb, z_tiles, x_buf, x_tiles, x_keyf, prefix,
                      scope):
    p = ctx.p
    n = len(z_tiles)
    stats = ctx.sb(f"{prefix}_st2", [128, 2 * n], F32, stack=scope)
    ln_stats(ctx, z_sb, z_tiles, lambda i: f"t{i}", stats, 1e-5,
             f"{prefix}_s2", scope)
    for j, (zi, xi) in enumerate(zip(z_tiles, x_tiles)):
        p.emit("vector", lambda e, xi=xi, j=j: e.tensor_scalar(
            out=x_buf[:, xi * D:(xi + 1) * D],
            in0=x_buf[:, xi * D:(xi + 1) * D],
            scalar1=stats[:, 2 * j + 1:2 * j + 2], scalar2=None, op0=ALU.add),
            reads=[(x_buf, x_keyf(xi)), (stats, 2 * j + 1)],
            writes=[(x_buf, x_keyf(xi))], tag=f"{prefix}_radd{xi}")
        p.emit("vector", lambda e, zi=zi, xi=xi, j=j: e.scalar_tensor_tensor(
            out=x_buf[:, xi * D:(xi + 1) * D], in0=z_sb[:, zi * D:(zi + 1) * D],
            scalar=stats[:, 2 * j:2 * j + 1],
            in1=x_buf[:, xi * D:(xi + 1) * D],
            op0=ALU.mult, op1=ALU.add),
            reads=[(z_sb, f"t{zi}"), (stats, 2 * j), (x_buf, x_keyf(xi))],
            writes=[(x_buf, x_keyf(xi))], tag=f"{prefix}_rfma{xi}")


RG_PAIRS = [[0, 1], [2, 3], [4, 5], [6, 7]]


def emit_exchange_ag(ctx, src_tm, ntiles, exin_t, exout_t, name):
    p = ctx.p
    er = exin_t.ap().rearrange("(n p) d -> n p d", p=128)
    for i in range(ntiles):
        p.dma("sync", er[i], src_tm[:, i * D:(i + 1) * D],
              reads=[(src_tm, f"t{i}")], writes=[(exin_t, i)],
              tag=f"{name}_o{i}")
    p.emit("gpsimd", lambda e: e.collective_compute(
        "AllGather", ALU.bypass, replica_groups=RG_PAIRS,
        ins=[exin_t.ap().opt()], outs=[exout_t.ap().opt()]),
        reads=[(exin_t, i) for i in range(ntiles)], writes=[exout_t],
        tag=f"{name}_cc")


def mm_accum(p, bank, bank_key, pairs, n0, n1, tag="", accum_more=False,
             m=128):
    last = len(pairs) - 1
    for k, (lt, rh, rr) in enumerate(pairs):
        st = (k == 0) and not accum_more
        p.emit("tensor", lambda e, lt=lt, rh=rh, k=k, st=st: e.matmul(
            bank[0:m, n0:n1], lt, rh, start=st, stop=(k == last)),
            reads=rr, writes=[(bank, bank_key)], tag=f"{tag}_k{k}")



class GateScratch:
    def __init__(self, ctx, E, prefix, scope):
        self.E = E
        self.nm = ctx.sb(f"{prefix}_gsnm", [128, 1], F32, stack=scope)
        self.e = ctx.sb(f"{prefix}_gse", [128, E], F32, stack=scope)
        self.m1 = ctx.sb(f"{prefix}_gsm1", [128, E], F32, stack=scope)
        self.e2 = ctx.sb(f"{prefix}_gse2", [128, E], F32, stack=scope)
        self.m2 = ctx.sb(f"{prefix}_gsm2", [128, 1], F32, stack=scope)
        self.w = ctx.sb(f"{prefix}_gsw", [128, E], F32, stack=scope)
        self.d = ctx.sb(f"{prefix}_gsd", [128, 1], F32, stack=scope)
        self.d2 = ctx.sb(f"{prefix}_gsd2", [128, 1], F32, stack=scope)
        self.rc = ctx.sb(f"{prefix}_gsrc", [128, 1], F32, stack=scope)


def emit_gating(ctx, sc, logits, lkey, lcol, ew, ewkey, ecol, variant,
                prefix, n_true=None, scale_out=1.0):
    p = ctx.p
    E = sc.E
    L = logits[:, lcol:lcol + E]
    lr = (logits, lkey)
    p.emit("vector", lambda e: e.reduce_max(sc.nm[:], L, axis=AX.X,
                                            negate=True),
           reads=[lr], writes=[sc.nm], tag=f"{prefix}_gmax")
    p.emit("scalar", lambda e: e.activation(out=sc.e[:], in_=L, func=AF.Exp,
                                            bias=sc.nm[:]),
           reads=[lr, sc.nm], writes=[sc.e], tag=f"{prefix}_gexp")
    if variant == "eps":
        p.emit("vector", lambda e: e.tensor_reduce(
            sc.d[:], sc.e[:], axis=AX.X, op=ALU.add),
            reads=[sc.e], writes=[sc.d], tag=f"{prefix}_gZ")
        p.emit("vector", lambda e: e.reciprocal(sc.d2[:], sc.d[:]),
               reads=[sc.d], writes=[sc.d2], tag=f"{prefix}_gZr")
        p.emit("vector", lambda e: e.tensor_scalar(
            out=sc.e[:], in0=sc.e[:], scalar1=sc.d2[:], scalar2=None,
            op0=ALU.mult),
            reads=[sc.e, sc.d2], writes=[sc.e], tag=f"{prefix}_gp")
        top1 = sc.d2
        p.emit("vector", lambda e: e.tensor_scalar(
            out=sc.m1[:], in0=sc.e[:], scalar1=top1[:], scalar2=None,
            op0=ALU.is_ge),
            reads=[sc.e, top1], writes=[sc.m1], tag=f"{prefix}_gm1")
    else:
        p.emit("vector", lambda e: e.tensor_single_scalar(
            out=sc.m1[:], in_=sc.e[:], scalar=1.0, op=ALU.is_ge),
            reads=[sc.e], writes=[sc.m1], tag=f"{prefix}_gm1")
    p.emit("vector", lambda e: e.tensor_mul(sc.e2[:], sc.e[:], sc.m1[:]),
           reads=[sc.e, sc.m1], writes=[sc.e2], tag=f"{prefix}_ge2a")
    p.emit("vector", lambda e: e.tensor_sub(sc.e2[:], sc.e[:], sc.e2[:]),
           reads=[sc.e, sc.e2], writes=[sc.e2], tag=f"{prefix}_ge2b")
    p.emit("vector", lambda e: e.reduce_max(sc.m2[:], sc.e2[:], axis=AX.X),
           reads=[sc.e2], writes=[sc.m2], tag=f"{prefix}_gm2")
    p.emit("vector", lambda e: e.tensor_scalar(
        out=sc.w[:], in0=sc.e[:], scalar1=sc.m2[:], scalar2=None,
        op0=ALU.is_ge),
        reads=[sc.e, sc.m2], writes=[sc.w], tag=f"{prefix}_gsel")
    p.emit("vector", lambda e: e.tensor_mul(sc.w[:], sc.w[:], sc.e[:]),
           reads=[sc.w, sc.e], writes=[sc.w], tag=f"{prefix}_gw")
    nt = n_true if n_true is not None else E
    p.emit("vector", lambda e: e.tensor_reduce(
        sc.d[:], sc.w[:, 0:nt], axis=AX.X, op=ALU.add),
        reads=[sc.w], writes=[sc.d], tag=f"{prefix}_gden")
    if variant == "null8":
        p.emit("vector", lambda e: e.tensor_single_scalar(
            out=sc.d2[:], in_=sc.d[:], scalar=0.0, op=ALU.is_equal),
            reads=[sc.d], writes=[sc.d2], tag=f"{prefix}_gz")
        p.emit("vector", lambda e: e.tensor_add(sc.d2[:], sc.d2[:], sc.d[:]),
               reads=[sc.d2, sc.d], writes=[sc.d2], tag=f"{prefix}_gz2")
    elif variant == "eps":
        p.emit("vector", lambda e: e.tensor_single_scalar(
            out=sc.d2[:], in_=sc.d[:], scalar=1e-6, op=ALU.add),
            reads=[sc.d], writes=[sc.d2], tag=f"{prefix}_geps")
    else:
        p.emit("vector", lambda e: e.tensor_copy(sc.d2[:], sc.d[:]),
               reads=[sc.d], writes=[sc.d2], tag=f"{prefix}_gcp")
    p.emit("vector", lambda e: e.reciprocal(sc.rc[:], sc.d2[:]),
           reads=[sc.d2], writes=[sc.rc], tag=f"{prefix}_grc")
    if scale_out != 1.0:
        p.emit("vector", lambda e: e.tensor_scalar(
            out=ew[:, ecol:ecol + nt], in0=sc.w[:, 0:nt], scalar1=sc.rc[:],
            scalar2=float(scale_out), op0=ALU.mult, op1=ALU.mult),
            reads=[sc.w, sc.rc], writes=[(ew, ewkey)], tag=f"{prefix}_gew")
    else:
        p.emit("vector", lambda e: e.tensor_scalar(
            out=ew[:, ecol:ecol + nt], in0=sc.w[:, 0:nt], scalar1=sc.rc[:],
            scalar2=None, op0=ALU.mult),
            reads=[sc.w, sc.rc], writes=[(ew, ewkey)], tag=f"{prefix}_gew")



def emit_l1_ffn(ctx, W, yb_fm, x_buf, x_tiles, x_keyf, prefix="f1"):
    nc, p = ctx.nc, ctx.p
    H = 3072
    HC = H // 128
    TG = 512
    SCALE = float(np.float32(1.0) / np.float32(127.0))
    wgr = W["wgT"].rearrange("(c p) h -> c p h", p=128)
    wur = W["wuT"].rearrange("(c p) h -> c p h", p=128)
    wdr = W["wdT"].rearrange("(c p) d -> c p d", p=128)

    with ExitStack() as sc:
        ps = [ctx.ps_f32(f"{prefix}_ps{i}", stack=sc) for i in range(6)]
        wd = ctx.sb(f"{prefix}_wd", [128, (HC // 2) * D], BF16, stack=sc)
        HH = H // 2
        wg = ctx.sb(f"{prefix}_wg", [128, DC * HH], BF16, stack=sc)
        wu = ctx.sb(f"{prefix}_wu", [128, DC * HH], BF16, stack=sc)
        hbuf = ctx.sb(f"{prefix}_h", [128, (HC // 2) * TG], BF16, stack=sc)
        sg = ctx.sb(f"{prefix}_sg", [128, TG], BF16, stack=sc)
        z_sb = ctx.sb(f"{prefix}_z", [128, 4 * D], F32, stack=sc)
        def load_cvt(dst, dkey, src_ap, width, tag):
            p.dma("gpsimd", dst, src_ap, writes=[dkey], tag=f"{tag}_l")
        for g in range(T // TG):
            t0 = g * TG
            for hh in range(2):
                h0 = hh * HH
                for c in range(DC):
                    load_cvt(wg[:, c * HH:(c + 1) * HH], (wg, c),
                             wgr[c][:, h0:h0 + HH], HH,
                             f"{prefix}_lwg{g}{hh}{c}")
                for c in range(DC):
                    load_cvt(wu[:, c * HH:(c + 1) * HH], (wu, c),
                             wur[c][:, h0:h0 + HH], HH,
                             f"{prefix}_lwu{g}{hh}{c}")
                for cl in range(HC // 2):
                    load_cvt(wd[:, cl * D:(cl + 1) * D], (wd, cl),
                             wdr[hh * (HC // 2) + cl], D,
                             f"{prefix}_lwd{g}{hh}{cl}")
                for cl in range(HC // 2):
                    c = hh * (HC // 2) + cl
                    bgk = ps[(2 * cl) % 4]
                    buk = ps[(2 * cl + 1) % 4]
                    pairs_g = [
                        (wg[:, dc * HH + cl * 128:dc * HH + cl * 128 + 128],
                         yb_fm[:, dc * T + t0:dc * T + t0 + TG],
                         [(wg, dc), (yb_fm, WHOLE)])
                        for dc in range(DC)]
                    pairs_u = [
                        (wu[:, dc * HH + cl * 128:dc * HH + cl * 128 + 128],
                         yb_fm[:, dc * T + t0:dc * T + t0 + TG],
                         [(wu, dc), (yb_fm, WHOLE)])
                        for dc in range(DC)]
                    mm_accum(p, bgk, "s", pairs_g, 0, TG,
                             tag=f"{prefix}_mg{g}{c}")
                    mm_accum(p, buk, "s", pairs_u, 0, TG,
                             tag=f"{prefix}_mu{g}{c}")
                    p.emit("scalar", lambda e, bgk=bgk: e.activation(
                        out=sg[:, 0:TG], in_=bgk[:, 0:TG], func=AF.Silu,
                        scale=SCALE),
                        reads=[(bgk, "s")], writes=[sg],
                        tag=f"{prefix}_silu{g}{c}")
                    p.emit("vector", lambda e, buk=buk, cl=cl:
                           e.scalar_tensor_tensor(
                               out=hbuf[:, cl * TG:(cl + 1) * TG],
                               in0=buk[:, 0:TG], scalar=SCALE,
                               in1=sg[:, 0:TG], op0=ALU.mult, op1=ALU.mult),
                           reads=[(buk, "s"), sg], writes=[(hbuf, cl)],
                           tag=f"{prefix}_hm{g}{c}")
                for s in range(TG // 128):
                    zt = s
                    for half in range(2):
                        bank = ps[4 + half]
                        pairs = [
                            (hbuf[:, cl * TG + s * 128:cl * TG + (s + 1) * 128],
                             wd[:, cl * D + half * 512:
                                cl * D + half * 512 + 512],
                             [(hbuf, cl), (wd, cl)])
                            for cl in range(HC // 2)]
                        mm_accum(p, bank, "s", pairs, 0, 512,
                                 tag=f"{prefix}_md{g}{hh}{s}{half}")
                        if hh == 0:
                            p.emit("scalar",
                                   lambda e, bank=bank, zt=zt, half=half:
                                   e.activation(
                                       out=z_sb[:, zt * D + half * 512:
                                                zt * D + half * 512 + 512],
                                       in_=bank[:, 0:512], func=AF.Copy,
                                       scale=SCALE),
                                   reads=[(bank, "s")],
                                   writes=[(z_sb, f"t{zt}")],
                                   tag=f"{prefix}_ev{g}{hh}{s}{half}")
                        else:
                            p.emit("vector",
                                   lambda e, bank=bank, zt=zt, half=half:
                                   e.scalar_tensor_tensor(
                                       out=z_sb[:, zt * D + half * 512:
                                                zt * D + half * 512 + 512],
                                       in0=bank[:, 0:512], scalar=SCALE,
                                       in1=z_sb[:, zt * D + half * 512:
                                                zt * D + half * 512 + 512],
                                       op0=ALU.mult, op1=ALU.add),
                                   reads=[(bank, "s"), (z_sb, f"t{zt}")],
                                   writes=[(z_sb, f"t{zt}")],
                                   tag=f"{prefix}_ev{g}{hh}{s}{half}")
            with ExitStack() as esc:
                sandwich_epilogue(ctx, z_sb, [0, 1, 2, 3], x_buf,
                                  x_tiles[g * 4:(g + 1) * 4], x_keyf,
                                  f"{prefix}_e{g}", esc)



def emit_l0_ffn(ctx, W, yb_fm, x_buf, x_tiles, x_keyf, prefix="f0"):
    nc, p = ctx.nc, ctx.p
    with ExitStack() as sc:
        ps = [ctx.ps_f32(f"{prefix}_ps{i}", stack=sc) for i in range(6)]
        bw = ctx.sb(f"{prefix}_bw", [128, DC * D], BF16, stack=sc)
        gw = ctx.sb(f"{prefix}_gw", [128, DC * 10], BF16, stack=sc)
        As = ctx.sb(f"{prefix}_As", [128, DC * 128], BF16, stack=sc)
        Bs = ctx.sb(f"{prefix}_Bs", [128, D], BF16, stack=sc)
        bwr = W["base_wT"].rearrange("(c p) o -> c p o", p=128)
        gwr = W["gate_wT"].rearrange("(c p) o -> c p o", p=128)
        asr = W["A_stackT"].rearrange("(c p) o -> c p o", p=128)
        for c in range(DC):
            p.dma("gpsimd", bw[:, c * D:(c + 1) * D], bwr[c],
                  writes=[(bw, c)], tag=f"{prefix}_lbw{c}")
            p.dma("gpsimd", gw[:, c * 10:(c + 1) * 10], gwr[c],
                  writes=[(gw, c)], tag=f"{prefix}_lgw{c}")
            p.dma("gpsimd", As[:, c * 128:(c + 1) * 128], asr[c],
                  writes=[(As, c)], tag=f"{prefix}_lAs{c}")
        p.dma("gpsimd", Bs[:], W["B_stackT"], writes=[Bs],
              tag=f"{prefix}_lBs")
        brow = ctx.sb(f"{prefix}_brow", [1, D], F32, stack=sc)
        p.dma("sync", brow[:], W["base_b"], writes=[brow],
              tag=f"{prefix}_lbb")
        b_bc = ctx.broadcast_row(brow, D, f"{prefix}_bbc", [ps[3]],
                                 stack=sc, row_reads=[brow])
        IND = ctx.sb(f"{prefix}_ind", [8, 128], F32, stack=sc)
        p.emit("gpsimd", lambda e: e.memset(IND[:], 1.0), writes=[IND])
        p.emit("gpsimd", lambda e: e.affine_select(
            out=IND[:], in_=IND[:], compare_op=ALU.is_ge, fill=0.0,
            base=0, pattern=[[1, 128]], channel_multiplier=-16),
            reads=[IND], writes=[IND])
        p.emit("gpsimd", lambda e: e.affine_select(
            out=IND[:], in_=IND[:], compare_op=ALU.is_ge, fill=0.0,
            base=15, pattern=[[-1, 128]], channel_multiplier=16),
            reads=[IND], writes=[IND])

        logit = ctx.sb(f"{prefix}_lg", [128, NT * 10], F32, stack=sc)
        for i in range(NT):
            pairs = [(yb_fm[:, dc * T + i * 128:dc * T + (i + 1) * 128],
                      gw[:, dc * 10:(dc + 1) * 10],
                      [(yb_fm, WHOLE), (gw, dc)]) for dc in range(DC)]
            mm_accum(p, ps[0], "s", pairs, i * 10, (i + 1) * 10,
                     tag=f"{prefix}_mlg{i}")
        p.emit("vector", lambda e: e.tensor_copy(
            logit[:], ps[0][:, 0:NT * 10]),
            reads=[(ps[0], "s")], writes=[logit], tag=f"{prefix}_elg")
        ew_all = ctx.sb(f"{prefix}_ew", [128, NT * 8], F32, stack=sc)
        gs = GateScratch(ctx, 10, prefix, sc)
        for i in range(NT):
            emit_gating(ctx, gs, logit, WHOLE, i * 10, ew_all, f"t{i}", i * 8,
                        "null8", f"{prefix}_g{i}", n_true=8, scale_out=2.0)
        ew_em = ctx.sb(f"{prefix}_ewem", [8, T], F32, stack=sc)
        id32 = ctx.ident32()
        for i in range(NT):
            bank = ps[1 + (i // 4) % 2]
            slot = i % 4
            p.emit("tensor", lambda e, i=i, bank=bank, slot=slot: e.transpose(
                bank[0:8, slot * 128:slot * 128 + 128],
                ew_all[:, i * 8:(i + 1) * 8], id32[:]),
                reads=[(ew_all, f"t{i}"), id32], writes=[(bank, "s")],
                tag=f"{prefix}_trew{i}")
            if slot == 3:
                p.emit("vector", lambda e, i=i, bank=bank: e.tensor_copy(
                    ew_em[0:8, (i - 3) * 128:(i + 1) * 128],
                    bank[0:8, 0:512]),
                    reads=[(bank, "s")], writes=[(ew_em, i // 4)],
                    tag=f"{prefix}_eew{i}")
        Zb = ctx.sb(f"{prefix}_Zb", [128, T], BF16, stack=sc)
        EWb = ctx.sb(f"{prefix}_EWb", [128, T], BF16, stack=sc)
        for g in range(T // 512):
            t0 = g * 512
            p.emit("tensor", lambda e, t0=t0: e.matmul(
                ps[3][:, 0:512], IND[:], ew_em[0:8, t0:t0 + 512],
                start=True, stop=True),
                reads=[IND, (ew_em, g)], writes=[(ps[3], "s")],
                tag=f"{prefix}_mbc{g}")
            p.emit("vector", lambda e, t0=t0: e.tensor_copy(
                EWb[:, t0:t0 + 512], ps[3][:, 0:512]),
                reads=[(ps[3], "s")], writes=[(EWb, g)],
                tag=f"{prefix}_ebc{g}")
            pairs = [(As[:, dc * 128:(dc + 1) * 128],
                      yb_fm[:, dc * T + t0:dc * T + t0 + 512],
                      [(As, dc), (yb_fm, WHOLE)]) for dc in range(DC)]
            mm_accum(p, ps[4 + g % 2], "s", pairs, 0, 512,
                     tag=f"{prefix}_mZ{g}")
            p.emit("vector", lambda e, g=g, t0=t0: e.tensor_copy(
                Zb[:, t0:t0 + 512], ps[4 + g % 2][:, 0:512]),
                reads=[(ps[4 + g % 2], "s")], writes=[(Zb, g)],
                tag=f"{prefix}_eZ{g}")
            p.emit("vector", lambda e, g=g, t0=t0: e.tensor_mul(
                Zb[:, t0:t0 + 512], Zb[:, t0:t0 + 512], EWb[:, t0:t0 + 512]),
                reads=[(Zb, g), (EWb, g)], writes=[(Zb, g)],
                tag=f"{prefix}_mZp{g}")
        z_sb = ctx.sb(f"{prefix}_z", [128, 4 * D], F32, stack=sc)
        for i in range(NT):
            g = (i * 128) // 512
            j = i % 4
            for half in range(2):
                bank = ps[2 + (2 * i + half) % 4]
                pairs = [(yb_fm[:, dc * T + i * 128:dc * T + (i + 1) * 128],
                          bw[:, dc * D + half * 512:dc * D + half * 512 + 512],
                          [(yb_fm, WHOLE), (bw, dc)]) for dc in range(DC)]
                pairs.append((Zb[:, i * 128:(i + 1) * 128],
                              Bs[:, half * 512:half * 512 + 512],
                              [(Zb, g), (Bs, WHOLE)]))
                mm_accum(p, bank, "s", pairs, 0, 512,
                         tag=f"{prefix}_mz{i}h{half}")
                p.emit("vector", lambda e, bank=bank, j=j, half=half:
                       e.tensor_add(
                           z_sb[:, j * D + half * 512:j * D + half * 512 + 512],
                           bank[:, 0:512], b_bc[:, half * 512:half * 512 + 512]),
                       reads=[(bank, "s"), (b_bc, f"c{half * 512}")],
                       writes=[(z_sb, f"t{j}")], tag=f"{prefix}_ez{i}h{half}")
            if j == 3:
                with ExitStack() as esc:
                    sandwich_epilogue(ctx, z_sb, [0, 1, 2, 3], x_buf,
                                      x_tiles[i - 3:i + 1], x_keyf,
                                      f"{prefix}_e{i // 4}", esc)



def emit_l2_ffn(ctx, W, yb_fm, x_buf, x_tiles, x_keyf, prefix="f2"):
    nc, p = ctx.nc, ctx.p
    E = 6
    with ExitStack() as sc:
        ps = [ctx.ps_f32(f"{prefix}_ps{i}", stack=sc) for i in range(6)]
        bw = ctx.sb(f"{prefix}_bw", [128, DC * D], BF16, stack=sc)
        sA = ctx.sb(f"{prefix}_sA", [128, DC * 16], BF16, stack=sc)
        sB = ctx.sb(f"{prefix}_sB", [16, D], BF16, stack=sc)
        As = ctx.sb(f"{prefix}_As", [128, DC * 96], BF16, stack=sc)
        Bs = ctx.sb(f"{prefix}_Bs", [96, D], BF16, stack=sc)
        rw = ctx.sb(f"{prefix}_rw", [128, DC * E], BF16, stack=sc)
        bwr = W["base_WT"].rearrange("(c p) o -> c p o", p=128)
        sar = W["sA_T"].rearrange("(c p) o -> c p o", p=128)
        asr = W["A_stackT"].rearrange("(c p) o -> c p o", p=128)
        rwr = W["rw_T"].rearrange("(c p) o -> c p o", p=128)
        for c in range(DC):
            p.dma("gpsimd", bw[:, c * D:(c + 1) * D], bwr[c],
                  writes=[(bw, c)], tag=f"{prefix}_lbw{c}")
            p.dma("gpsimd", sA[:, c * 16:(c + 1) * 16], sar[c],
                  writes=[(sA, c)], tag=f"{prefix}_lsa{c}")
            p.dma("gpsimd", As[:, c * 96:(c + 1) * 96], asr[c],
                  writes=[(As, c)], tag=f"{prefix}_las{c}")
            p.dma("gpsimd", rw[:, c * E:(c + 1) * E], rwr[c],
                  writes=[(rw, c)], tag=f"{prefix}_lrw{c}")
        p.dma("gpsimd", sB[:], W["sB_T"], writes=[sB], tag=f"{prefix}_lsb")
        p.dma("gpsimd", Bs[:], W["B_stackT"], writes=[Bs], tag=f"{prefix}_lbs")
        brow = ctx.sb(f"{prefix}_brow", [1, D], F32, stack=sc)
        p.dma("sync", brow[:], W["base_b"], writes=[brow], tag=f"{prefix}_lbb")
        b_bc = ctx.broadcast_row(brow, D, f"{prefix}_bbc", [ps[3]],
                                 stack=sc, row_reads=[brow])
        rbrow = ctx.sb(f"{prefix}_rbrow", [1, E], F32, stack=sc)
        p.dma("sync", rbrow[:], W["router_b"], writes=[rbrow],
              tag=f"{prefix}_lrb")
        rb_bc = ctx.broadcast_row(rbrow, E, f"{prefix}_rbbc", [ps[3]],
                                  stack=sc, row_reads=[rbrow])
        IND = ctx.sb(f"{prefix}_ind", [6, 96], F32, stack=sc)
        p.emit("gpsimd", lambda e: e.memset(IND[:], 1.0), writes=[IND])
        p.emit("gpsimd", lambda e: e.affine_select(
            out=IND[:], in_=IND[:], compare_op=ALU.is_ge, fill=0.0,
            base=0, pattern=[[1, 96]], channel_multiplier=-16),
            reads=[IND], writes=[IND])
        p.emit("gpsimd", lambda e: e.affine_select(
            out=IND[:], in_=IND[:], compare_op=ALU.is_ge, fill=0.0,
            base=15, pattern=[[-1, 96]], channel_multiplier=16),
            reads=[IND], writes=[IND])
        s1b = ctx.sb(f"{prefix}_s1", [16, T], BF16, stack=sc)
        for g in range(T // 512):
            t0 = g * 512
            pairs = [(sA[:, dc * 16:(dc + 1) * 16],
                      yb_fm[:, dc * T + t0:dc * T + t0 + 512],
                      [(sA, dc), (yb_fm, WHOLE)]) for dc in range(DC)]
            mm_accum(p, ps[0], "s", pairs, 0, 512, tag=f"{prefix}_ms1{g}", m=16)
            p.emit("vector", lambda e, t0=t0: e.tensor_copy(
                s1b[0:16, t0:t0 + 512], ps[0][0:16, 0:512]),
                reads=[(ps[0], "s")], writes=[(s1b, g)],
                tag=f"{prefix}_es1{g}")
        logit = ctx.sb(f"{prefix}_lg", [128, NT * E], F32, stack=sc)
        for i in range(NT):
            pairs = [(yb_fm[:, dc * T + i * 128:dc * T + (i + 1) * 128],
                      rw[:, dc * E:(dc + 1) * E],
                      [(yb_fm, WHOLE), (rw, dc)]) for dc in range(DC)]
            mm_accum(p, ps[0], "s", pairs, i * E, (i + 1) * E,
                     tag=f"{prefix}_mlg{i}")
        for i in range(NT):
            p.emit("vector", lambda e, i=i: e.tensor_add(
                logit[:, i * E:(i + 1) * E], ps[0][:, i * E:(i + 1) * E],
                rb_bc[:, 0:E]),
                reads=[(ps[0], "s"), (rb_bc, "c0")], writes=[(logit, i)],
                tag=f"{prefix}_elg{i}")
        sw_all = ctx.sb(f"{prefix}_sw", [128, NT * E], F32, stack=sc)
        gs = GateScratch(ctx, E, prefix, sc)
        for i in range(NT):
            emit_gating(ctx, gs, logit, i, i * E, sw_all, f"t{i}", i * E,
                        "eps", f"{prefix}_g{i}")
        ew_em = ctx.sb(f"{prefix}_ewem", [6, T], F32, stack=sc)
        id32 = ctx.ident32()
        for i in range(NT):
            bank = ps[1 + (i // 4) % 2]
            slot = i % 4
            p.emit("tensor", lambda e, i=i, bank=bank, slot=slot: e.transpose(
                bank[0:E, slot * 128:slot * 128 + 128],
                sw_all[:, i * E:(i + 1) * E], id32[:]),
                reads=[(sw_all, f"t{i}"), id32], writes=[(bank, "s")],
                tag=f"{prefix}_trew{i}")
            if slot == 3:
                p.emit("vector", lambda e, i=i, bank=bank: e.tensor_copy(
                    ew_em[0:E, (i - 3) * 128:(i + 1) * 128],
                    bank[0:E, 0:512]),
                    reads=[(bank, "s")], writes=[(ew_em, i // 4)],
                    tag=f"{prefix}_eew{i}")
        Zb = ctx.sb(f"{prefix}_Zb", [96, T], BF16, stack=sc)
        EWb = ctx.sb(f"{prefix}_EWb", [96, T], BF16, stack=sc)
        for g in range(T // 512):
            t0 = g * 512
            p.emit("tensor", lambda e, t0=t0, g=g: e.matmul(
                ps[3][0:96, 0:512], IND[:], ew_em[0:E, t0:t0 + 512],
                start=True, stop=True),
                reads=[IND, (ew_em, g)], writes=[(ps[3], "s")],
                tag=f"{prefix}_mbc{g}")
            p.emit("vector", lambda e, t0=t0, g=g: e.tensor_copy(
                EWb[0:96, t0:t0 + 512], ps[3][0:96, 0:512]),
                reads=[(ps[3], "s")], writes=[(EWb, g)],
                tag=f"{prefix}_ebc{g}")
            pairs = [(As[:, dc * 96:(dc + 1) * 96],
                      yb_fm[:, dc * T + t0:dc * T + t0 + 512],
                      [(As, dc), (yb_fm, WHOLE)]) for dc in range(DC)]
            mm_accum(p, ps[4 + g % 2], "s", pairs, 0, 512,
                     tag=f"{prefix}_mZ{g}", m=96)
            p.emit("vector", lambda e, g=g, t0=t0: e.tensor_copy(
                Zb[0:96, t0:t0 + 512], ps[4 + g % 2][0:96, 0:512]),
                reads=[(ps[4 + g % 2], "s")], writes=[(Zb, g)],
                tag=f"{prefix}_eZ{g}")
            p.emit("vector", lambda e, g=g, t0=t0: e.tensor_mul(
                Zb[0:96, t0:t0 + 512], Zb[0:96, t0:t0 + 512],
                EWb[0:96, t0:t0 + 512]),
                reads=[(Zb, g), (EWb, g)], writes=[(Zb, g)],
                tag=f"{prefix}_mZp{g}")
        z_sb = ctx.sb(f"{prefix}_z", [128, 4 * D], F32, stack=sc)
        for i in range(NT):
            g = (i * 128) // 512
            j = i % 4
            for half in range(2):
                bank = ps[2 + (2 * i + half) % 4]
                pairs = [(yb_fm[:, dc * T + i * 128:dc * T + (i + 1) * 128],
                          bw[:, dc * D + half * 512:dc * D + half * 512 + 512],
                          [(yb_fm, WHOLE), (bw, dc)]) for dc in range(DC)]
                pairs.append((s1b[0:16, i * 128:(i + 1) * 128],
                              sB[0:16, half * 512:half * 512 + 512],
                              [(s1b, g), (sB, WHOLE)]))
                pairs.append((Zb[0:96, i * 128:(i + 1) * 128],
                              Bs[0:96, half * 512:half * 512 + 512],
                              [(Zb, g), (Bs, WHOLE)]))
                mm_accum(p, bank, "s", pairs, 0, 512,
                         tag=f"{prefix}_mz{i}h{half}")
                p.emit("vector", lambda e, bank=bank, j=j, half=half:
                       e.tensor_add(
                           z_sb[:, j * D + half * 512:j * D + half * 512 + 512],
                           bank[:, 0:512],
                           b_bc[:, half * 512:half * 512 + 512]),
                       reads=[(bank, "s"), (b_bc, f"c{half * 512}")],
                       writes=[(z_sb, f"t{j}")], tag=f"{prefix}_ez{i}h{half}")
            if j == 3:
                with ExitStack() as esc:
                    sandwich_epilogue(ctx, z_sb, [0, 1, 2, 3], x_buf,
                                      x_tiles[i - 3:i + 1], x_keyf,
                                      f"{prefix}_e{i // 4}", esc)



def emit_l3_ffn(ctx, W, yb_fm, acc, prefix="f3"):
    nc, p = ctx.nc, ctx.p
    E = 16
    with ExitStack() as sc:
        ps = [ctx.ps_f32(f"{prefix}_ps{i}", stack=sc) for i in range(8)]
        gwt = ctx.sb(f"{prefix}_gwt", [128, DC * E], BF16, stack=sc)
        gwr = W["gate_wT"].rearrange("(c p) o -> c p o", p=128)
        for c in range(DC):
            p.dma("gpsimd", gwt[:, c * E:(c + 1) * E], gwr[c],
                  writes=[(gwt, c)], tag=f"{prefix}_lgw{c}")
        logit = ctx.sb(f"{prefix}_lg", [128, NT * E], F32, stack=sc)
        for i in range(NT):
            pairs = [(yb_fm[:, dc * T + i * 128:dc * T + (i + 1) * 128],
                      gwt[:, dc * E:(dc + 1) * E],
                      [(yb_fm, WHOLE), (gwt, dc)]) for dc in range(DC)]
            mm_accum(p, ps[0], "s", pairs, i * E, (i + 1) * E,
                     tag=f"{prefix}_mlg{i}")
        p.emit("vector", lambda e: e.tensor_copy(
            logit[:], ps[0][:, 0:NT * E]),
            reads=[(ps[0], "s")], writes=[logit], tag=f"{prefix}_elg")
        comb = ctx.sb(f"{prefix}_comb", [128, NT * E], F32, stack=sc)
        gs = GateScratch(ctx, E, prefix, sc)
        for i in range(NT):
            emit_gating(ctx, gs, logit, WHOLE, i * E, comb, f"t{i}", i * E,
                        "plain", f"{prefix}_g{i}")
        sggt = ctx.sb(f"{prefix}_sggt", [128, DC], BF16, stack=sc)
        sggr = W["sggT"].rearrange("(c p) o -> c p o", p=128)
        for c in range(DC):
            p.dma("gpsimd", sggt[:, c:c + 1], sggr[c],
                  writes=[(sggt, c)], tag=f"{prefix}_lsgg{c}")
        sgc = ctx.sb(f"{prefix}_sgc", [128, NT], F32, stack=sc)
        for i in range(NT):
            pairs = [(yb_fm[:, dc * T + i * 128:dc * T + (i + 1) * 128],
                      sggt[:, dc:dc + 1],
                      [(yb_fm, WHOLE), (sggt, dc)]) for dc in range(DC)]
            mm_accum(p, ps[0], "s", pairs, i, i + 1, tag=f"{prefix}_msg{i}")
        p.emit("scalar", lambda e: e.activation(
            out=sgc[:], in_=ps[0][:, 0:NT], func=AF.Sigmoid),
            reads=[(ps[0], "s")], writes=[sgc], tag=f"{prefix}_sig")

        wgb2 = [ctx.sb(f"{prefix}_wgb{v}", [128, DC * D], BF16, stack=sc)
                for v in range(2)]
        w1b2 = [ctx.sb(f"{prefix}_w1b{v}", [128, DC * D], BF16, stack=sc)
                for v in range(2)]
        w2b = ctx.sb(f"{prefix}_w2b", [128, DC * D], BF16, stack=sc)
        silu_sb = ctx.sb(f"{prefix}_silu", [128, 1024], BF16, stack=sc)
        ub2 = [ctx.sb(f"{prefix}_ub{v}", [128, 512], BF16, stack=sc)
               for v in range(2)]
        hbuf = ctx.sb(f"{prefix}_h", [128, DC * (T // 2)], BF16, stack=sc)

        def swiglu_pass(widx, wg_r, w1_r, w2_r, scale_keyf, first, tagp):
            wgb = wgb2[widx % 2]
            w1b = w1b2[widx % 2]
            for c in range(DC):
                p.dma("gpsimd", wgb[:, c * D:(c + 1) * D], wg_r(c),
                      writes=[(wgb, c)], tag=f"{tagp}_lwg{c}")
            for c in range(DC):
                p.dma("gpsimd", w1b[:, c * D:(c + 1) * D], w1_r(c),
                      writes=[(w1b, c)], tag=f"{tagp}_lw1{c}")
            for c in range(DC):
                p.dma("gpsimd", w2b[:, c * D:(c + 1) * D], w2_r(c),
                      writes=[(w2b, c)], tag=f"{tagp}_lw2{c}")
            for th in range(2):
                t00 = th * (T // 2)
                for hc in range(DC):
                    for gg in range(2):
                        t0 = t00 + gg * 512
                        g = (t0 // 512)
                        bg = ps[(2 * gg) % 4]
                        bu = ps[(2 * gg + 1) % 4]
                        pg = [(wgb[:, dc * D + hc * 128:
                                   dc * D + hc * 128 + 128],
                               yb_fm[:, dc * T + t0:dc * T + t0 + 512],
                               [(wgb, dc), (yb_fm, WHOLE)])
                              for dc in range(DC)]
                        pu = [(w1b[:, dc * D + hc * 128:
                                   dc * D + hc * 128 + 128],
                               yb_fm[:, dc * T + t0:dc * T + t0 + 512],
                               [(w1b, dc), (yb_fm, WHOLE)])
                              for dc in range(DC)]
                        mm_accum(p, bg, "s", pg, 0, 512,
                                 tag=f"{tagp}_mg{hc}{g}")
                        mm_accum(p, bu, "s", pu, 0, 512,
                                 tag=f"{tagp}_mu{hc}{g}")
                        p.emit("scalar", lambda e, bg=bg, t0=t0, t00=t00:
                               e.activation(
                                   out=silu_sb[:, t0 - t00:t0 - t00 + 512],
                                   in_=bg[:, 0:512], func=AF.Silu),
                               reads=[(bg, "s")], writes=[(silu_sb, gg)],
                               tag=f"{tagp}_si{hc}{g}")
                        ub = ub2[gg % 2]
                        p.emit("scalar", lambda e, bu=bu, ub=ub: e.activation(
                            out=ub[:, 0:512], in_=bu[:, 0:512],
                            func=AF.Identity),
                            reads=[(bu, "s")], writes=[ub],
                            tag=f"{tagp}_ue{hc}{g}")
                        p.emit("vector", lambda e, hc=hc, t0=t0, t00=t00,
                               gg=gg, ub=ub: e.tensor_mul(
                                   hbuf[:, hc * (T // 2) + t0 - t00:
                                        hc * (T // 2) + t0 - t00 + 512],
                                   silu_sb[:, t0 - t00:t0 - t00 + 512],
                                   ub[:, 0:512]),
                               reads=[(silu_sb, gg), ub],
                               writes=[(hbuf, f"{hc}_{gg}")],
                               tag=f"{tagp}_hm{hc}{g}")
                for i in range(th * (NT // 2), (th + 1) * (NT // 2)):
                    il = i - th * (NT // 2)
                    for half in range(2):
                        bank = ps[4 + (2 * i + half) % 4]
                        pairs = [
                            (hbuf[:, hc * (T // 2) + il * 128:
                                  hc * (T // 2) + (il + 1) * 128],
                             w2b[:, hc * D + half * 512:
                                 hc * D + half * 512 + 512],
                             [(hbuf, f"{hc}_{(il * 128) // 512}"), (w2b, hc)])
                            for hc in range(DC)]
                        mm_accum(p, bank, "s", pairs, 0, 512,
                                 tag=f"{tagp}_mv{i}h{half}")
                        scol = scale_keyf(i)
                        if first:
                            p.emit("vector",
                                   lambda e, bank=bank, i=i, half=half,
                                   scol=scol: e.tensor_scalar(
                                       out=acc[:, i * D + half * 512:
                                               i * D + half * 512 + 512],
                                       in0=bank[:, 0:512], scalar1=scol[0],
                                       scalar2=None, op0=ALU.mult),
                                   reads=[(bank, "s"), scol[1]],
                                   writes=[(acc, f"t{i}")],
                                   tag=f"{tagp}_va{i}h{half}")
                        else:
                            p.emit("vector",
                                   lambda e, bank=bank, i=i, half=half,
                                   scol=scol: e.scalar_tensor_tensor(
                                       out=acc[:, i * D + half * 512:
                                               i * D + half * 512 + 512],
                                       in0=bank[:, 0:512], scalar=scol[0],
                                       in1=acc[:, i * D + half * 512:
                                               i * D + half * 512 + 512],
                                       op0=ALU.mult, op1=ALU.add),
                                   reads=[(bank, "s"), scol[1],
                                          (acc, f"t{i}")],
                                   writes=[(acc, f"t{i}")],
                                   tag=f"{tagp}_va{i}h{half}")

        wgr3 = W["wgT"].rearrange("e (c p) h -> e c p h", p=128)
        w1r3 = W["w1T"].rearrange("e (c p) h -> e c p h", p=128)
        w2r3 = W["w2T"].rearrange("e (c p) h -> e c p h", p=128)
        for ex in range(E):
            swiglu_pass(
                ex,
                lambda c, ex=ex: wgr3[ex, c],
                lambda c, ex=ex: w1r3[ex, c],
                lambda c, ex=ex: w2r3[ex, c],
                lambda i, ex=ex: (comb[:, i * E + ex:i * E + ex + 1],
                                  (comb, f"t{i}")),
                first=(ex == 0), tagp=f"{prefix}_x{ex}")
        sgwr = W["sgwT"].rearrange("(c p) h -> c p h", p=128)
        s1wr = W["s1wT"].rearrange("(c p) h -> c p h", p=128)
        s2wr = W["s2wT"].rearrange("(hh c p) o -> hh c p o", p=128, c=DC)
        for hh in range(2):
            swiglu_pass(
                16 + hh,
                lambda c, hh=hh: sgwr[c][:, hh * D:(hh + 1) * D],
                lambda c, hh=hh: s1wr[c][:, hh * D:(hh + 1) * D],
                lambda c, hh=hh: s2wr[hh, c],
                lambda i: (sgc[:, i:i + 1], sgc),
                first=False, tagp=f"{prefix}_s{hh}")


def emit_l3_ffn_block(ctx, W, yb_fm, x_buf, x_tiles, x_keyf, prefix="f3"):
    acc = ctx.sb(f"{prefix}_acc", [128, NT * D], F32)
    emit_l3_ffn(ctx, W, yb_fm, acc, prefix)
    with ExitStack() as esc:
        sandwich_epilogue(ctx, acc, list(range(NT)), x_buf, x_tiles,
                          x_keyf, f"{prefix}_e", esc)



def row_to_cols(ctx, row_sb, n, dst, prefix, bank, row_reads=(), scale=None):
    p = ctx.p
    id32 = ctx.ident32()
    for j in range(n):
        p.emit("tensor", lambda e, j=j: e.transpose(
            bank[0:128, j:j + 1], row_sb[0:1, j * 128:(j + 1) * 128],
            id32[0:1, 0:1]),
            reads=list(row_reads) + [id32], writes=[(bank, "s")],
            tag=f"{prefix}_r2c{j}")
    if scale is None:
        p.emit("vector", lambda e: e.tensor_copy(dst[:, 0:n], bank[:, 0:n]),
               reads=[(bank, "s")], writes=[dst], tag=f"{prefix}_r2ce")
    else:
        p.emit("vector", lambda e: e.tensor_scalar_mul(
            dst[:, 0:n], bank[:, 0:n], float(scale)),
            reads=[(bank, "s")], writes=[dst], tag=f"{prefix}_r2ce")


def emit_l0_attn(ctx, W, x_tm, x_nbr_dram, prefix="a0", scope=None,
                 co=None, exch=None):
    nc, p = ctx.nc, ctx.p
    H, DX = 8, 64
    QK = 512

    stats1 = ctx.sb(f"{prefix}_st1", [128, 2 * NT], F32, stack=scope)
    with ExitStack() as s0:
        ln_stats(ctx, x_tm, list(range(NT)), lambda i: f"t{i}", stats1,
                 1e-5, f"{prefix}_s1", s0)
    out_fm = ctx.sb(f"{prefix}_ofm", [128, 4 * T], BF16, stack=scope)
    sbig = ExitStack()
    with sbig:
        yb_fm = ctx.sb(f"{prefix}_ybfm", [128, DC * T], BF16, stack=sbig)
        with ExitStack() as s1:
            yb_tm = ctx.sb(f"{prefix}_ybtm", [128, NT * D], BF16, stack=s1)
            ln_apply_bf16(ctx, x_tm, list(range(NT)), lambda i: f"t{i}",
                          stats1, yb_tm, list(range(NT)), prefix)
            if exch is not None:
                emit_exchange_ag(ctx, yb_tm, NT, exch[0], exch[1],
                                 f"{prefix}_ex")
            trps = [ctx.ps_bf16(f"{prefix}_trp{i}", stack=s1)
                    for i in range(2)]
            transpose_tm_to_fm(ctx, yb_tm, list(range(NT)), yb_fm, 0, T,
                               prefix, trps)

        wq = ctx.sb(f"{prefix}_wq", [128, DC * 1536], BF16, stack=sbig)
        wqr = W["w_qkv"].rearrange("(c p) o -> c p o", p=128)
        for c in range(DC):
            p.dma("gpsimd", wq[:, c * 1536:(c + 1) * 1536], wqr[c],
                  writes=[(wq, c)], tag=f"{prefix}_lwq{c}")
        v_fm = ctx.sb(f"{prefix}_vfm", [128, 4 * T], BF16, stack=sbig)
        S_acc = ctx.sb(f"{prefix}_Sacc", [64, H * 64], F32, stack=sbig)

        s2 = sbig.enter_context(ExitStack())
        ps = [ctx.ps_f32(f"{prefix}_ps{i}", stack=s2) for i in range(6)]
        brow = ctx.sb(f"{prefix}_bqkv", [1, 1536], F32, stack=s2)
        p.dma("sync", brow[:], W["b_qkv"], writes=[brow],
              tag=f"{prefix}_lbq")
        bqk_bc = ctx.broadcast_row(brow, 1024, f"{prefix}_bqkbc", [ps[5]],
                                  stack=s2, row_reads=[brow])
        bv_col = ctx.sb(f"{prefix}_bvc", [128, 4], F32, stack=s2)
        row_to_cols(ctx, brow[0:1, 1024:1536], 4,
                    bv_col, f"{prefix}_bv", ps[5], row_reads=[brow])
        for c in range(4):
            for g in range(4):
                t0 = g * 512
                bank = ps[g % 2]
                pairs = [(wq[:, dc * 1536 + 1024 + c * 128:
                             dc * 1536 + 1024 + (c + 1) * 128],
                          yb_fm[:, dc * T + t0:dc * T + t0 + 512],
                          [(wq, dc), (yb_fm, WHOLE)]) for dc in range(DC)]
                mm_accum(p, bank, "s", pairs, 0, 512,
                         tag=f"{prefix}_mv{c}{g}")
                p.emit("scalar", lambda e, bank=bank, c=c, t0=t0:
                       e.activation(
                           out=v_fm[:, c * T + t0:c * T + t0 + 512],
                           in_=bank[:, 0:512], func=AF.Identity,
                           bias=bv_col[:, c:c + 1]),
                       reads=[(bank, "s"), bv_col],
                       writes=[(v_fm, f"{c}_{g}")], tag=f"{prefix}_ev{c}{g}")
        qg2 = [ctx.sb(f"{prefix}_qg{v}", [128, 2 * QK], BF16, stack=s2)
               for v in range(2)]
        kg2 = [ctx.sb(f"{prefix}_kg{v}", [128, 2 * QK], BF16, stack=s2)
               for v in range(2)]
        xng = ctx.sb(f"{prefix}_xng", [128, 2 * D], F32, stack=s2)
        ybg_tm = ctx.sb(f"{prefix}_ybgtm", [128, 2 * D], BF16, stack=s2)
        ybg_fm = ctx.sb(f"{prefix}_ybgfm", [128, DC * 256], BF16, stack=s2)
        stg = ctx.sb(f"{prefix}_stg", [128, 4], F32, stack=s2)
        xnr = (x_nbr_dram.rearrange("(n p) d -> n p d", p=128)
               if x_nbr_dram is not None else None)
        cor = (co.ap().rearrange("(n p) d -> n p d", p=128)
               if co is not None else None)
        for tg in range(NT):
            qg = qg2[tg % 2]
            kg = kg2[tg % 2]
            own = tg < 8
            if co is not None:
                with ExitStack() as sg_:
                    for j in range(2):
                        p.dma("sync", ybg_tm[:, j * D:(j + 1) * D],
                              cor[2 * tg + j], reads=[co],
                              writes=[(ybg_tm, f"t{j}")],
                              tag=f"{prefix}_lco{tg}{j}")
                    trps2 = [ctx.ps_bf16(f"{prefix}_t2{tg}_{v}", stack=sg_)
                             for v in range(2)]
                    transpose_tm_to_fm(ctx, ybg_tm, [0, 1], ybg_fm, 0, 256,
                                       f"{prefix}_tn{tg}", trps2)
                src_fm, fmoff, ntokf = ybg_fm, 0, 256
                fmreads = [(ybg_fm, WHOLE)]
            elif own:
                src_fm, fmoff, ntokf = yb_fm, tg * 256, T
                fmreads = [(yb_fm, WHOLE)]
            else:
                j0 = (tg - 8) * 2
                with ExitStack() as sg_:
                    for j in range(2):
                        p.dma("sync", xng[:, j * D:(j + 1) * D], xnr[j0 + j],
                              writes=[(xng, f"t{j}")], tag=f"{prefix}_lxn")
                    ln_stats(ctx, xng, [0, 1], lambda i: f"t{i}", stg,
                             1e-5, f"{prefix}_sn{tg}", sg_)
                    ln_apply_bf16(ctx, xng, [0, 1], lambda i: f"t{i}", stg,
                                  ybg_tm, [0, 1], f"{prefix}_an{tg}")
                    trps2 = [ctx.ps_bf16(f"{prefix}_t2{tg}_{v}", stack=sg_)
                             for v in range(2)]
                    transpose_tm_to_fm(ctx, ybg_tm, [0, 1], ybg_fm, 0, 256,
                                       f"{prefix}_tn{tg}", trps2)
                src_fm, fmoff, ntokf = ybg_fm, 0, 256
                fmreads = [(ybg_fm, WHOLE)]
            for j in range(2):
                bq = ps[2 * ((2 * tg + j) % 2)]
                bk = ps[2 * ((2 * tg + j) % 2) + 1]
                pq = [(src_fm[:, dc * ntokf + fmoff + j * 128:
                              dc * ntokf + fmoff + (j + 1) * 128],
                       wq[:, dc * 1536:dc * 1536 + 512],
                       fmreads + [(wq, dc)]) for dc in range(DC)]
                pk = [(src_fm[:, dc * ntokf + fmoff + j * 128:
                              dc * ntokf + fmoff + (j + 1) * 128],
                       wq[:, dc * 1536 + 512:dc * 1536 + 1024],
                       fmreads + [(wq, dc)]) for dc in range(DC)]
                mm_accum(p, bq, "s", pq, 0, 512, tag=f"{prefix}_mq{tg}{j}")
                mm_accum(p, bk, "s", pk, 0, 512, tag=f"{prefix}_mk{tg}{j}")
                p.emit("vector", lambda e, bq=bq, j=j: e.tensor_add(
                    qg[:, j * QK:(j + 1) * QK], bq[:, 0:512],
                    bqk_bc[:, 0:512]),
                    reads=[(bq, "s"), (bqk_bc, "c0")], writes=[(qg, j)],
                    tag=f"{prefix}_eq{tg}{j}")
                p.emit("vector", lambda e, bk=bk, j=j: e.tensor_add(
                    kg[:, j * QK:(j + 1) * QK], bk[:, 0:512],
                    bqk_bc[:, 512:1024]),
                    reads=[(bk, "s"), (bqk_bc, "c512")], writes=[(kg, j)],
                    tag=f"{prefix}_ek{tg}{j}")
            gb_ = ps[4 + tg % 2]
            for h in range(H):
                for j in range(2):
                    p.emit("tensor", lambda e, h=h, j=j, gb_=gb_: e.matmul(
                        gb_[0:64, h * 64:(h + 1) * 64],
                        qg[:, j * QK + h * 64:j * QK + (h + 1) * 64],
                        kg[:, j * QK + h * 64:j * QK + (h + 1) * 64],
                        start=(j == 0), stop=(j == 1)),
                        reads=[(qg, j), (kg, j)], writes=[(gb_, "s")],
                        tag=f"{prefix}_mS{tg}h{h}")
            if tg == 0:
                p.emit("vector", lambda e, gb_=gb_: e.tensor_copy(
                    S_acc[0:64, :], gb_[0:64, 0:H * 64]),
                    reads=[(gb_, "s")], writes=[S_acc],
                    tag=f"{prefix}_eS{tg}")
            else:
                p.emit("vector", lambda e, gb_=gb_: e.tensor_add(
                    S_acc[0:64, :], S_acc[0:64, :], gb_[0:64, 0:H * 64]),
                    reads=[S_acc, (gb_, "s")], writes=[S_acc],
                    tag=f"{prefix}_eS{tg}")
        attn_sb = ctx.sb(f"{prefix}_attn", [128, 4 * 64], BF16, stack=s2)
        nmx = ctx.sb(f"{prefix}_nmx", [64, 2], F32, stack=s2)
        esc_ = ctx.sb(f"{prefix}_esc", [64, 64], F32, stack=s2)
        for h in range(H):
            Sh = S_acc[0:64, h * 64:(h + 1) * 64]
            p.emit("vector", lambda e, Sh=Sh: e.reduce_max(
                nmx[0:64, 0:1], Sh, axis=AX.X, negate=True),
                reads=[S_acc], writes=[(nmx, 0)], tag=f"{prefix}_smx{h}")
            p.emit("vector", lambda e: e.tensor_scalar_mul(
                nmx[0:64, 1:2], nmx[0:64, 0:1], 0.125),
                reads=[(nmx, 0)], writes=[(nmx, 1)], tag=f"{prefix}_sms{h}")
            p.emit("scalar", lambda e, Sh=Sh: e.activation(
                out=esc_[0:64, :], in_=Sh, func=AF.Exp,
                bias=nmx[0:64, 1:2], scale=0.125),
                reads=[S_acc, (nmx, 1)], writes=[esc_],
                tag=f"{prefix}_sex{h}")
            p.emit("vector", lambda e: e.tensor_reduce(
                nmx[0:64, 0:1], esc_[0:64, :], axis=AX.X, op=ALU.add),
                reads=[esc_], writes=[(nmx, 0)], tag=f"{prefix}_ssum{h}")
            p.emit("vector", lambda e: e.reciprocal(
                nmx[0:64, 1:2], nmx[0:64, 0:1]),
                reads=[(nmx, 0)], writes=[(nmx, 1)], tag=f"{prefix}_srcp{h}")
            r0 = 64 * (h % 2)
            p.emit("vector", lambda e, h=h, r0=r0: e.tensor_scalar(
                out=attn_sb[r0:r0 + 64, (h // 2) * 64:(h // 2 + 1) * 64],
                in0=esc_[0:64, :], scalar1=nmx[0:64, 1:2], scalar2=None,
                op0=ALU.mult),
                reads=[esc_, (nmx, 1)], writes=[(attn_sb, h)],
                tag=f"{prefix}_snrm{h}")
        for pp in range(4):
            for g in range(4):
                t0 = g * 512
                ba, bb = ps[0], ps[1]
                p.emit("tensor", lambda e, pp=pp, t0=t0, ba=ba: e.matmul(
                    ba[0:64, 0:512],
                    attn_sb[0:64, pp * 64:(pp + 1) * 64],
                    v_fm[0:64, pp * T + t0:pp * T + t0 + 512],
                    start=True, stop=True, tile_position=(0, 0)),
                    reads=[(attn_sb, 2 * pp), (v_fm, f"{pp}_{g}")],
                    writes=[(ba, "s")], tag=f"{prefix}_mpa{pp}{g}")
                p.emit("tensor", lambda e, pp=pp, t0=t0, bb=bb: e.matmul(
                    bb[0:64, 0:512],
                    attn_sb[64:128, pp * 64:(pp + 1) * 64],
                    v_fm[64:128, pp * T + t0:pp * T + t0 + 512],
                    start=True, stop=True, tile_position=(64, 0)),
                    reads=[(attn_sb, 2 * pp + 1), (v_fm, f"{pp}_{g}")],
                    writes=[(bb, "s")], tag=f"{prefix}_mpb{pp}{g}")
                p.emit("vector", lambda e, pp=pp, t0=t0, ba=ba: e.tensor_copy(
                    out_fm[0:64, pp * T + t0:pp * T + t0 + 512],
                    ba[0:64, 0:512]),
                    reads=[(ba, "s")], writes=[(out_fm, f"{pp}_{g}")],
                    tag=f"{prefix}_epa{pp}{g}")
                p.emit("vector", lambda e, pp=pp, t0=t0, bb=bb: e.tensor_copy(
                    out_fm[64:128, pp * T + t0:pp * T + t0 + 512],
                    bb[0:64, 0:512]),
                    reads=[(bb, "s")], writes=[(out_fm, f"{pp}_{g}")],
                    tag=f"{prefix}_epb{pp}{g}")

    with ExitStack() as s3:
        qs = [ctx.ps_f32(f"{prefix}_qs{i}", stack=s3) for i in range(4)]
        wfc = ctx.sb(f"{prefix}_wfc", [128, 4 * D], BF16, stack=s3)
        wfr = W["w_fc"].rearrange("(c p) o -> c p o", p=128)
        for c in range(4):
            p.dma("gpsimd", wfc[:, c * D:(c + 1) * D], wfr[c],
                  writes=[(wfc, c)], tag=f"{prefix}_lwf{c}")
        rows = {}
        for nm in ["b_fc", "gamma", "beta"]:
            r = ctx.sb(f"{prefix}_{nm}r", [1, D], F32, stack=s3)
            p.dma("sync", r[:], W[nm], writes=[r], tag=f"{prefix}_l{nm}")
            rows[nm] = ctx.broadcast_row(r, D, f"{prefix}_{nm}bc", [qs[3]],
                                         stack=s3, row_reads=[r])
        z_sb = ctx.sb(f"{prefix}_z", [128, 4 * D], F32, stack=s3)
        ybuf = ctx.sb(f"{prefix}_yb2", [128, D], F32, stack=s3)
        ubuf = ctx.sb(f"{prefix}_ub2", [128, D], F32, stack=s3)
        sti = ctx.sb(f"{prefix}_sti", [128, 2 * 4], F32, stack=s3)
        for i in range(NT):
            j = i % 4
            p.emit("scalar", lambda e, i=i: e.activation(
                out=ybuf[:], in_=x_tm[:, i * D:(i + 1) * D],
                func=AF.Identity, scale=stats1[:, 2 * i:2 * i + 1],
                bias=stats1[:, 2 * i + 1:2 * i + 2]),
                reads=[(x_tm, f"t{i}"), (stats1, 2 * i), (stats1, 2 * i + 1)],
                writes=[ybuf], tag=f"{prefix}_y{i}")
            for half in range(2):
                bank = qs[half]
                pairs = [(out_fm[:, cc * T + i * 128:cc * T + (i + 1) * 128],
                          wfc[:, cc * D + half * 512:cc * D + half * 512 + 512],
                          [(out_fm, f"{cc}_{(i * 128) // 512}"), (wfc, cc)])
                         for cc in range(4)]
                mm_accum(p, bank, "s", pairs, 0, 512,
                         tag=f"{prefix}_mz{i}h{half}")
                sl = slice(j * D + half * 512, j * D + half * 512 + 512)
                p.emit("vector", lambda e, bank=bank, sl=sl, half=half:
                       e.tensor_add(z_sb[:, sl], bank[:, 0:512],
                                    rows["b_fc"][:, half * 512:
                                                 half * 512 + 512]),
                       reads=[(bank, "s"), (rows["b_fc"], f"c{half * 512}")],
                       writes=[(z_sb, f"t{j}")], tag=f"{prefix}_ez{i}{half}")
            p.emit("vector", lambda e, j=j: e.tensor_add(
                z_sb[:, j * D:(j + 1) * D], z_sb[:, j * D:(j + 1) * D],
                ybuf[:]),
                reads=[(z_sb, f"t{j}"), ybuf], writes=[(z_sb, f"t{j}")],
                tag=f"{prefix}_ezy{i}")
            with ExitStack() as s4:
                ln_stats(ctx, z_sb, [j], lambda t: f"t{t}", sti, 1e-6,
                         f"{prefix}_si{i}", s4)
            p.emit("scalar", lambda e, j=j: e.activation(
                out=ubuf[:], in_=z_sb[:, j * D:(j + 1) * D],
                func=AF.Identity, scale=sti[:, 0:1], bias=sti[:, 1:2]),
                reads=[(z_sb, f"t{j}"), (sti, 0), (sti, 1)],
                writes=[ubuf], tag=f"{prefix}_u{i}")
            p.emit("vector", lambda e: e.tensor_mul(
                ubuf[:], ubuf[:], rows["gamma"][:, 0:D]),
                reads=[ubuf, (rows["gamma"], "c0"), (rows["gamma"], "c512")],
                writes=[ubuf], tag=f"{prefix}_ug{i}")
            p.emit("vector", lambda e, j=j: e.tensor_add(
                z_sb[:, j * D:(j + 1) * D], ubuf[:], rows["beta"][:, 0:D]),
                reads=[ubuf, (rows["beta"], "c0"), (rows["beta"], "c512")],
                writes=[(z_sb, f"t{j}")], tag=f"{prefix}_ub{i}")
            if j == 3:
                with ExitStack() as esc:
                    sandwich_epilogue(ctx, z_sb, [0, 1, 2, 3], x_tm,
                                      [i - 3, i - 2, i - 1, i],
                                      lambda t: f"t{t}",
                                      f"{prefix}_e{i // 4}", esc)



def emit_l1_attn(ctx, W, x_tm, x_nbr_dram, prefix="a1", scope=None,
                 co=None, exch=None):
    nc, p = ctx.nc, ctx.p
    NK = 2 * T
    NKT = NK // 128
    y_all = ctx.sb(f"{prefix}_yall", [128, NKT * D], BF16, stack=scope)
    right = ctx.sb(f"{prefix}_right", [128, 2 * NK], BF16, stack=scope)
    left = ctx.sb(f"{prefix}_left", [128, 2 * T], BF16, stack=scope)
    if co is not None:
        with ExitStack() as s0:
            ybo = ctx.sb(f"{prefix}_ybo", [128, NT * D], BF16, stack=s0)
            st = ctx.sb(f"{prefix}_st", [128, 2 * NT], F32, stack=s0)
            ln_stats(ctx, x_tm, list(range(NT)), lambda i: f"t{i}", st, 1e-5,
                     f"{prefix}_s0", s0)
            ln_apply_bf16(ctx, x_tm, list(range(NT)), lambda i: f"t{i}", st,
                          ybo, list(range(NT)), prefix)
            emit_exchange_ag(ctx, ybo, NT, exch[0], exch[1], f"{prefix}_ex")
            p.dma("sync", y_all.rearrange("p (n d) -> p n d", d=D),
                  co.ap().rearrange("(n p) d -> p n d", p=128),
                  reads=[co], writes=[(y_all, f"t{t}") for t in range(NKT)],
                  tag=f"{prefix}_lyall")
            ps0 = [ctx.ps_f32(f"{prefix}_lp{i}", stack=s0) for i in range(2)]
            trp0 = [ctx.ps_bf16(f"{prefix}_ltp", stack=s0)]
            cl = ctx.sb(f"{prefix}_cl0", [128, DC * 256], BF16, stack=s0)
            clr = W["cl"].rearrange("(c p) o -> c p o", p=128)
            for c in range(DC):
                p.dma("gpsimd", cl[:, c * 256:(c + 1) * 256], clr[c],
                      writes=[(cl, c)], tag=f"{prefix}_lcl{c}")
            ybg0 = ctx.sb(f"{prefix}_ybg0", [128, DC * 128], BF16, stack=s0)
            for t in range(NT):
                for c in range(DC):
                    slot = c % 4
                    p.emit("tensor", lambda e, t=t, c=c, slot=slot:
                           e.transpose(
                               trp0[0][:, slot * 128:slot * 128 + 128],
                               ybo[:, t * D + c * 128:t * D + (c + 1) * 128],
                               ctx.ident[:]),
                           reads=[(ybo, f"t{t}"), ctx.ident],
                           writes=[(trp0[0], "s")],
                           tag=f"{prefix}_lty{t}{c}")
                    if slot == 3:
                        p.emit("vector", lambda e, c=c: e.tensor_copy(
                            ybg0[:, (c - 3) * 128:(c + 1) * 128],
                            trp0[0][:, 0:512]),
                            reads=[(trp0[0], "s")], writes=[(ybg0, c // 4)],
                            tag=f"{prefix}_ltye{t}{c}")
                for kc in range(2):
                    pairs = [(cl[:, dc * 256 + kc * 128:
                                 dc * 256 + kc * 128 + 128],
                              ybg0[:, dc * 128:(dc + 1) * 128],
                              [(cl, dc), (ybg0, dc // 4)])
                             for dc in range(DC)]
                    mm_accum(p, ps0[kc], "s", pairs, 0, 128,
                             tag=f"{prefix}_ml{t}{kc}")
                    p.emit("vector", lambda e, t=t, kc=kc: e.tensor_copy(
                        left[:, kc * T + t * 128:kc * T + (t + 1) * 128],
                        ps0[kc][:, 0:128]),
                        reads=[(ps0[kc], "s")],
                        writes=[(left, f"{kc}_{t}")],
                        tag=f"{prefix}_el{t}{kc}")
    else:
        with ExitStack() as s0:
            st = ctx.sb(f"{prefix}_st", [128, 2 * NT], F32, stack=s0)
            ln_stats(ctx, x_tm, list(range(NT)), lambda i: f"t{i}", st, 1e-5,
                     f"{prefix}_s0", s0)
            ln_apply_bf16(ctx, x_tm, list(range(NT)), lambda i: f"t{i}", st,
                          y_all, list(range(NT)), prefix)
            xng = ctx.sb(f"{prefix}_xng", [128, 4 * D], F32, stack=s0)
            stg = ctx.sb(f"{prefix}_stg", [128, 8], F32, stack=s0)
            xnr = x_nbr_dram.rearrange("(n p) d -> n p d", p=128)
            for g in range(4):
                with ExitStack() as sg_:
                    for j in range(4):
                        p.dma("sync", xng[:, j * D:(j + 1) * D],
                              xnr[g * 4 + j],
                              writes=[(xng, f"t{j}")],
                              tag=f"{prefix}_lxn{g}{j}")
                    ln_stats(ctx, xng, [0, 1, 2, 3], lambda i: f"t{i}", stg,
                             1e-5, f"{prefix}_sn{g}", sg_)
                    ln_apply_bf16(ctx, xng, [0, 1, 2, 3], lambda i: f"t{i}",
                                  stg, y_all,
                                  [16 + g * 4 + j for j in range(4)],
                                  f"{prefix}_an{g}")
    with ExitStack() as s1:
        ps = [ctx.ps_f32(f"{prefix}_ps{i}", stack=s1) for i in range(4)]
        trp = [ctx.ps_bf16(f"{prefix}_trp", stack=s1)]
        cl = ctx.sb(f"{prefix}_cls", [128, DC * 256], BF16, stack=s1)
        cr = ctx.sb(f"{prefix}_crs", [128, DC * 256], BF16, stack=s1)
        clr = W["cl"].rearrange("(c p) o -> c p o", p=128)
        crr = W["cr"].rearrange("(c p) o -> c p o", p=128)
        for c in range(DC):
            p.dma("gpsimd", cl[:, c * 256:(c + 1) * 256], clr[c],
                  writes=[(cl, c)], tag=f"{prefix}_lcl{c}")
            p.dma("gpsimd", cr[:, c * 256:(c + 1) * 256], crr[c],
                  writes=[(cr, c)], tag=f"{prefix}_lcr{c}")
        ybg = ctx.sb(f"{prefix}_ybg", [128, DC * 128], BF16, stack=s1)
        for t in range(NKT):
            for c in range(DC):
                slot = c % 4
                bank = trp[0]
                p.emit("tensor", lambda e, t=t, c=c, bank=bank, slot=slot:
                       e.transpose(bank[:, slot * 128:slot * 128 + 128],
                                   y_all[:, t * D + c * 128:
                                         t * D + (c + 1) * 128],
                                   ctx.ident[:]),
                       reads=[(y_all, f"t{t}"), ctx.ident],
                       writes=[(bank, "s")], tag=f"{prefix}_ty{t}{c}")
                if slot == 3:
                    p.emit("vector", lambda e, c=c, bank=bank: e.tensor_copy(
                        ybg[:, (c - 3) * 128:(c + 1) * 128], bank[:, 0:512]),
                        reads=[(bank, "s")], writes=[(ybg, c // 4)],
                        tag=f"{prefix}_tye{t}{c}")
            for kc in range(2):
                pairs = [(cr[:, dc * 256 + kc * 128:dc * 256 + kc * 128 + 128],
                          ybg[:, dc * 128:(dc + 1) * 128],
                          [(cr, dc), (ybg, dc // 4)]) for dc in range(DC)]
                mm_accum(p, ps[kc], "s", pairs, 0, 128,
                         tag=f"{prefix}_mr{t}{kc}")
                p.emit("vector", lambda e, t=t, kc=kc: e.tensor_copy(
                    right[:, kc * NK + t * 128:kc * NK + (t + 1) * 128],
                    ps[kc][:, 0:128]),
                    reads=[(ps[kc], "s")], writes=[(right, f"{kc}_{t}")],
                    tag=f"{prefix}_er{t}{kc}")
                if t < NT and co is None:
                    pairs = [(cl[:, dc * 256 + kc * 128:
                                 dc * 256 + kc * 128 + 128],
                              ybg[:, dc * 128:(dc + 1) * 128],
                              [(cl, dc), (ybg, dc // 4)]) for dc in range(DC)]
                    mm_accum(p, ps[2 + kc], "s", pairs, 0, 128,
                             tag=f"{prefix}_ml{t}{kc}")
                    p.emit("vector", lambda e, t=t, kc=kc: e.tensor_copy(
                        left[:, kc * T + t * 128:kc * T + (t + 1) * 128],
                        ps[2 + kc][:, 0:128]),
                        reads=[(ps[2 + kc], "s")],
                        writes=[(left, f"{kc}_{t}")],
                        tag=f"{prefix}_el{t}{kc}")
    with ExitStack() as s2:
        qs2 = [ctx.ps_f32(f"{prefix}_qs{i}", stack=s2) for i in range(4)]
        trp = [ctx.ps_bf16(f"{prefix}_qtp{i}", stack=s2) for i in range(2)]
        thr = ctx.sb(f"{prefix}_thrs", [1, 1], F32, stack=s2)
        gbr = ctx.sb(f"{prefix}_gbrs", [1, 1], F32, stack=s2)
        p.dma("sync", thr[:], W["thr"], writes=[thr], tag=f"{prefix}_lth")
        p.dma("sync", gbr[:], W["gb"], writes=[gbr], tag=f"{prefix}_lgb")
        cols = ctx.sb(f"{prefix}_cols", [128, 2], F32, stack=s2)
        p.emit("tensor", lambda e: e.matmul(
            qs2[0][:, 0:1], ctx.ones_row[:], thr[:], start=True, stop=True),
            reads=[ctx.ones_row, thr], writes=[(qs2[0], "s")],
            tag=f"{prefix}_bth")
        p.emit("vector", lambda e: e.tensor_scalar_mul(
            cols[:, 0:1], qs2[0][:, 0:1], 32.0),
            reads=[(qs2[0], "s")], writes=[(cols, 0)], tag=f"{prefix}_eth")
        p.emit("tensor", lambda e: e.matmul(
            qs2[0][:, 0:1], ctx.ones_row[:], gbr[:], start=True, stop=True),
            reads=[ctx.ones_row, gbr], writes=[(qs2[0], "s")],
            tag=f"{prefix}_bgb")
        p.emit("vector", lambda e: e.tensor_copy(cols[:, 1:2], qs2[0][:, 0:1]),
               reads=[(qs2[0], "s")], writes=[(cols, 1)], tag=f"{prefix}_egb")
        Ssb2 = [ctx.sb(f"{prefix}_S{v}", [128, NK], F32, stack=s2)
                for v in range(2)]
        e_sb = ctx.sb(f"{prefix}_eb", [128, NK], BF16, stack=s2)
        Pt = ctx.sb(f"{prefix}_Pt", [128, NK], BF16, stack=s2)
        mscr = ctx.sb(f"{prefix}_msc", [128, 4], F32, stack=s2)
        z_sb = ctx.sb(f"{prefix}_zz", [128, D], F32, stack=s2)

        def emit_S(i):
            S_sb = Ssb2[i % 2]
            for jg in range(NK // 512):
                bank = qs2[jg % 2]
                pairs = [(left[:, kc * T + i * 128:kc * T + (i + 1) * 128],
                          right[:, kc * NK + jg * 512:kc * NK + jg * 512 + 512],
                          [(left, f"{kc}_{i}")] +
                          [(right, f"{kc}_{tt}") for tt in
                           range(jg * 4, jg * 4 + 4)])
                         for kc in range(2)]
                mm_accum(p, bank, "s", pairs, 0, 512,
                         tag=f"{prefix}_mS{i}_{jg}")
                p.emit("vector", lambda e, bank=bank, jg=jg, S_sb=S_sb:
                       e.tensor_copy(
                    S_sb[:, jg * 512:(jg + 1) * 512], bank[:, 0:512]),
                    reads=[(bank, "s")], writes=[(S_sb, jg)],
                    tag=f"{prefix}_eS{i}_{jg}")

        emit_S(0)
        for i in range(NT):
            S_sb = Ssb2[i % 2]
            sreads = [(S_sb, jg) for jg in range(8)]
            p.emit("vector", lambda e, S_sb=S_sb: e.reduce_max(
                mscr[:, 0:1], S_sb[:], axis=AX.X, negate=False),
                reads=sreads, writes=[(mscr, 0)], tag=f"{prefix}_mx{i}")
            if i + 1 < NT:
                emit_S(i + 1)
            p.emit("vector", lambda e: e.scalar_tensor_tensor(
                out=mscr[:, 1:2], in0=mscr[:, 0:1], scalar=1.0 / 32.0,
                in1=cols[:, 1:2], op0=ALU.mult, op1=ALU.add),
                reads=[(mscr, 0), (cols, 1)], writes=[(mscr, 1)],
                tag=f"{prefix}_b1{i}")
            p.emit("vector", lambda e: e.tensor_sub(
                mscr[:, 2:3], cols[:, 1:2], mscr[:, 1:2]),
                reads=[(cols, 1), (mscr, 1)], writes=[(mscr, 2)],
                tag=f"{prefix}_b2{i}")
            p.emit("scalar", lambda e, S_sb=S_sb: e.activation(
                out=e_sb[:], in_=S_sb[:], func=AF.Exp, scale=1.0 / 32.0,
                bias=mscr[:, 2:3], accum_out=mscr[:, 3:4]),
                reads=sreads + [(mscr, 2)], writes=[e_sb, (mscr, 3)],
                tag=f"{prefix}_ex{i}")
            p.emit("vector", lambda e: e.reciprocal(
                mscr[:, 1:2], mscr[:, 3:4]),
                reads=[(mscr, 3)], writes=[(mscr, 1)], tag=f"{prefix}_rc{i}")
            p.emit("vector", lambda e, S_sb=S_sb: e.scalar_tensor_tensor(
                out=e_sb[:], in0=S_sb[:], scalar=cols[:, 0:1], in1=e_sb[:],
                op0=ALU.is_gt, op1=ALU.mult),
                reads=[e_sb, (cols, 0)] + sreads, writes=[e_sb],
                tag=f"{prefix}_P{i}")
            for jc in range(NKT):
                bank = trp[(jc // 4) % 2]
                slot = jc % 4
                p.emit("tensor", lambda e, jc=jc, bank=bank, slot=slot:
                       e.transpose(bank[:, slot * 128:slot * 128 + 128],
                                   e_sb[:, jc * 128:(jc + 1) * 128],
                                   ctx.ident[:]),
                       reads=[e_sb, ctx.ident], writes=[(bank, "s")],
                       tag=f"{prefix}_tP{i}_{jc}")
                if slot == 3:
                    p.emit("scalar", lambda e, jc=jc, bank=bank:
                           e.activation(
                               out=Pt[:, (jc - 3) * 128:(jc + 1) * 128],
                               in_=bank[:, 0:512], func=AF.Copy, scale=1.0),
                           reads=[(bank, "s")], writes=[(Pt, jc // 4)],
                           tag=f"{prefix}_tPe{i}_{jc}")
            for half in range(2):
                bank = qs2[2 + half]
                pairs = [(Pt[:, jc * 128:(jc + 1) * 128],
                          y_all[:, jc * D + half * 512:
                                jc * D + half * 512 + 512],
                          [(Pt, jc // 4), (y_all, f"t{jc}")])
                         for jc in range(NKT)]
                mm_accum(p, bank, "s", pairs, 0, 512,
                         tag=f"{prefix}_mpv{i}h{half}")
                p.emit("scalar", lambda e, bank=bank, half=half:
                       e.activation(
                           out=z_sb[:, half * 512:half * 512 + 512],
                           in_=bank[:, 0:512], func=AF.Identity,
                           scale=mscr[:, 1:2]),
                       reads=[(bank, "s"), (mscr, 1)],
                       writes=[(z_sb, f"t0")],
                       tag=f"{prefix}_epv{i}h{half}")
            with ExitStack() as esc:
                sandwich_epilogue(ctx, z_sb, [0], x_tm, [i],
                                  lambda t: f"t{t}", f"{prefix}_e{i}", esc)



def emit_l3_attn_core(ctx, W, x1_dram, x2_dram, oat_dram, prefix="a3"):
    nc, p = ctx.nc, ctx.p
    NB = 2 * T
    NBT = NB // 128
    HPC = 8
    AUG = 65
    q_fm = ctx.sb(f"{prefix}_qfm", [128, 4 * NB], BF16)
    k_fm = ctx.sb(f"{prefix}_kfm", [128, 4 * NB], BF16)
    v_tm = ctx.sb(f"{prefix}_vtm", [128, NBT * HPC * AUG], BF16)
    out_tm = ctx.sb(f"{prefix}_otm", [128, NBT * 512], BF16)
    p.emit("gpsimd", lambda e: e.memset(v_tm[:, 64::AUG], 1.0),
           writes=[(v_tm, "aug")], tag=f"{prefix}_aug")

    with ExitStack() as s1:
        ps = [ctx.ps_f32(f"{prefix}_ps{i}", stack=s1) for i in range(4)]
        trp = [ctx.ps_bf16(f"{prefix}_trp", stack=s1)]
        wq = ctx.sb(f"{prefix}_wq", [128, DC * 512], BF16, stack=s1)
        wk = ctx.sb(f"{prefix}_wk", [128, DC * 512], BF16, stack=s1)
        wv = ctx.sb(f"{prefix}_wv", [128, DC * 512], BF16, stack=s1)
        for nm, buf in [("wq", wq), ("wk", wk), ("wv", wv)]:
            r = W[nm].rearrange("(c p) o -> c p o", p=128)
            for c in range(DC):
                p.dma("gpsimd", buf[:, c * 512:(c + 1) * 512], r[c],
                      writes=[(buf, c)], tag=f"{prefix}_l{nm}{c}")
        bqr = ctx.sb(f"{prefix}_bqr", [1, 512], F32, stack=s1)
        bkr = ctx.sb(f"{prefix}_bkr", [1, 512], F32, stack=s1)
        bvr = ctx.sb(f"{prefix}_bvr", [1, 512], F32, stack=s1)
        for nm, buf in [("bq", bqr), ("bk", bkr), ("bv", bvr)]:
            p.dma("sync", buf[:], W[nm], writes=[buf], tag=f"{prefix}_l{nm}")
        bq_col = ctx.sb(f"{prefix}_bqc", [128, 4], F32, stack=s1)
        bk_col = ctx.sb(f"{prefix}_bkc", [128, 4], F32, stack=s1)
        row_to_cols(ctx, bqr[0:1, :], 4, bq_col, f"{prefix}_bqc", ps[3],
                    row_reads=[bqr], scale=0.125)
        row_to_cols(ctx, bkr[0:1, :], 4, bk_col, f"{prefix}_bkc", ps[3],
                    row_reads=[bkr])
        bv_bc = ctx.broadcast_row(bvr, 512, f"{prefix}_bvbc", [ps[3]],
                                  stack=s1, row_reads=[bvr])
        xg = ctx.sb(f"{prefix}_xg", [128, 4 * D], F32, stack=s1)
        stg = ctx.sb(f"{prefix}_stg", [128, 8], F32, stack=s1)
        ybg_tm = ctx.sb(f"{prefix}_ybgt", [128, 4 * D], BF16, stack=s1)
        ybg = ctx.sb(f"{prefix}_ybg", [128, DC * 512], BF16, stack=s1)
        x1r = x1_dram.rearrange("(n p) d -> n p d", p=128)
        x2r = x2_dram.rearrange("(n p) d -> n p d", p=128)
        for g in range(8):
            xsrc = x1r if g < 4 else x2r
            with ExitStack() as sg_:
                for j in range(4):
                    p.dma("sync", xg[:, j * D:(j + 1) * D],
                          xsrc[(g % 4) * 4 + j],
                          writes=[(xg, f"t{j}")], tag=f"{prefix}_lx{g}{j}")
                ln_stats(ctx, xg, [0, 1, 2, 3], lambda i: f"t{i}", stg, 1e-5,
                         f"{prefix}_sg{g}", sg_)
                ln_apply_bf16(ctx, xg, [0, 1, 2, 3], lambda i: f"t{i}", stg,
                              ybg_tm, [0, 1, 2, 3], f"{prefix}_ag{g}")
                transpose_tm_to_fm(ctx, ybg_tm, [0, 1, 2, 3], ybg, 0, 512,
                                   f"{prefix}_tg{g}", trp)
            t0 = g * 512
            for c in range(4):
                for nmb, wb, bc, dst in [("q", wq, bq_col, q_fm),
                                         ("k", wk, bk_col, k_fm)]:
                    bank = ps[0 if nmb == "q" else 1]
                    pairs = [(wb[:, dc * 512 + c * 128:dc * 512 + c * 128 + 128],
                              ybg[:, dc * 512:(dc + 1) * 512],
                              [(wb, dc), (ybg, WHOLE)]) for dc in range(DC)]
                    mm_accum(p, bank, "s", pairs, 0, 512,
                             tag=f"{prefix}_m{nmb}{g}{c}")
                    p.emit("scalar", lambda e, bank=bank, c=c, t0=t0, dst=dst,
                           bc=bc: e.activation(
                               out=dst[:, c * NB + t0:c * NB + t0 + 512],
                               in_=bank[:, 0:512], func=AF.Identity,
                               bias=bc[:, c:c + 1]),
                           reads=[(bank, "s"), bc],
                           writes=[(dst, f"{c}_{g}")],
                           tag=f"{prefix}_e{nmb}{g}{c}")
            for j in range(4):
                ti = g * 4 + j
                bank = ps[2]
                pairs = [(ybg[:, dc * 512 + j * 128:dc * 512 + (j + 1) * 128],
                          wv[:, dc * 512:(dc + 1) * 512],
                          [(ybg, WHOLE), (wv, dc)]) for dc in range(DC)]
                mm_accum(p, bank, "s", pairs, 0, 512,
                         tag=f"{prefix}_mv{g}{j}")
                vview = v_tm[:, ti * HPC * AUG:(ti + 1) * HPC * AUG]
                vv3 = vview.rearrange("p (h ea) -> p h ea", ea=AUG)
                p.emit("vector", lambda e, bank=bank, vv3=vv3: e.tensor_add(
                    vv3[:, :, 0:64],
                    bank[:, 0:512].rearrange("p (h ea) -> p h ea", ea=64),
                    bv_bc[:, 0:512].rearrange("p (h ea) -> p h ea", ea=64)),
                    reads=[(bank, "s"), (bv_bc, "c0")],
                    writes=[(v_tm, f"t{ti}")], tag=f"{prefix}_ev{g}{j}")
    masks = ctx.sb(f"{prefix}_msk", [128, 4 * 512], BF16)
    for kk in range(4):
        sl = masks[:, kk * 512:(kk + 1) * 512]
        p.emit("gpsimd", lambda e, sl=sl: e.memset(sl, 1.0),
               writes=[(masks, kk)], tag=f"{prefix}_mm{kk}")
        p.emit("gpsimd", lambda e, sl=sl, kk=kk: e.affine_select(
            out=sl, in_=sl, compare_op=ALU.is_ge, fill=0.0,
            base=-128 * kk, pattern=[[1, 512]], channel_multiplier=-1),
            reads=[(masks, kk)], writes=[(masks, kk)], tag=f"{prefix}_ma{kk}")
    with ExitStack() as s2:
        ps = [ctx.ps_f32(f"{prefix}_as{i}", stack=s2) for i in range(8)]
        id32 = ctx.ident32()
        PAs = [ctx.sb(f"{prefix}_PA{v}", [128, 512], BF16, stack=s2)
               for v in range(2)]
        PBs = [ctx.sb(f"{prefix}_PB{v}", [128, 512], BF16, stack=s2)
               for v in range(2)]
        nsb = ctx.sb(f"{prefix}_nsb", [128, 1024], F32, stack=s2)
        rcol = ctx.sb(f"{prefix}_rcol", [128, 2], F32, stack=s2)
        for qt in range(8):
            for pc in range(4):
                nA, nB = ps[4 + 2 * (pc % 2)], ps[5 + 2 * (pc % 2)]
                kmax = 4 * qt + 4
                for kc in range(kmax):
                    sA, sB = ps[2 * (kc % 2)], ps[2 * (kc % 2) + 1]
                    PA, PB = PAs[kc % 2], PBs[kc % 2]
                    p.emit("tensor", lambda e, pc=pc, kc=kc, qt=qt, sA=sA:
                           e.matmul(
                               sA[:, 0:512],
                               k_fm[0:64, pc * NB + kc * 128:
                                    pc * NB + (kc + 1) * 128],
                               q_fm[0:64, pc * NB + qt * 512:
                                    pc * NB + qt * 512 + 512],
                               start=True, stop=True, tile_position=(0, 0)),
                           reads=[(k_fm, f"{pc}_{kc // 4}"),
                                  (q_fm, f"{pc}_{qt}")],
                           writes=[(sA, "s")], tag=f"{prefix}_sA{qt}{pc}{kc}")
                    p.emit("tensor", lambda e, pc=pc, kc=kc, qt=qt, sB=sB:
                           e.matmul(
                               sB[:, 0:512],
                               k_fm[64:128, pc * NB + kc * 128:
                                    pc * NB + (kc + 1) * 128],
                               q_fm[64:128, pc * NB + qt * 512:
                                    pc * NB + qt * 512 + 512],
                               start=True, stop=True, tile_position=(64, 0)),
                           reads=[(k_fm, f"{pc}_{kc // 4}"),
                                  (q_fm, f"{pc}_{qt}")],
                           writes=[(sB, "s")], tag=f"{prefix}_sB{qt}{pc}{kc}")
                    for P, sbank in [(PA, sA), (PB, sB)]:
                        p.emit("scalar", lambda e, P=P, sbank=sbank:
                               e.activation(out=P[:], in_=sbank[:, 0:512],
                                            func=AF.Exp, scale=0.125),
                               reads=[(sbank, "s")], writes=[P],
                               tag=f"{prefix}_ex{qt}{pc}{kc}")
                    kk = kc - 4 * qt
                    if kk >= 0:
                        for P in (PA, PB):
                            p.emit("vector", lambda e, P=P, kk=kk:
                                   e.tensor_mul(
                                       P[:], P[:],
                                       masks[:, kk * 512:(kk + 1) * 512]),
                                   reads=[P, (masks, kk)], writes=[P],
                                   tag=f"{prefix}_mk{qt}{pc}{kc}")
                    for hh, (P, nb) in enumerate([(PA, nA), (PB, nB)]):
                        h = 2 * pc + hh
                        p.emit("tensor", lambda e, kc=kc, h=h, P=P, nb=nb,
                               st=(kc == 0), sp=(kc == kmax - 1): e.matmul(
                                   nb[0:AUG, 0:512],
                                   v_tm[:, kc * HPC * AUG + h * AUG:
                                        kc * HPC * AUG + (h + 1) * AUG],
                                   P[:], start=st, stop=sp),
                               reads=[(v_tm, f"t{kc}"), (v_tm, "aug"), P],
                               writes=[(nb, "s")],
                               tag=f"{prefix}_pv{qt}{pc}{kc}{hh}")
                for hh, nb in [(0, nA), (1, nB)]:
                    p.emit("scalar", lambda e, hh=hh, nb=nb: e.activation(
                        out=nsb[0:AUG, hh * 512:(hh + 1) * 512],
                        in_=nb[0:AUG, 0:512], func=AF.Copy, scale=1.0),
                        reads=[(nb, "s")], writes=[(nsb, hh)],
                        tag=f"{prefix}_ne{qt}{pc}{hh}")
                for hh in range(2):
                    h = 2 * pc + hh
                    for s in range(4):
                        bank = ps[2 + (s % 2)]
                        p.emit("tensor", lambda e, hh=hh, s=s, bank=bank:
                               e.transpose(
                                   bank[:, 0:AUG],
                                   nsb[0:AUG, hh * 512 + s * 128:
                                       hh * 512 + (s + 1) * 128],
                                   id32[0:AUG, 0:AUG]),
                               reads=[(nsb, hh), id32], writes=[(bank, "s")],
                               tag=f"{prefix}_nt{qt}{pc}{hh}{s}")
                        p.emit("vector", lambda e, bank=bank, hh=hh:
                               e.reciprocal(rcol[:, hh:hh + 1],
                                            bank[:, 64:65]),
                               reads=[(bank, "s")], writes=[(rcol, hh)],
                               tag=f"{prefix}_nr{qt}{pc}{hh}{s}")
                        ti = qt * 4 + s
                        p.emit("vector", lambda e, bank=bank, hh=hh, h=h,
                               ti=ti: e.tensor_scalar(
                                   out=out_tm[:, ti * 512 + h * 64:
                                              ti * 512 + (h + 1) * 64],
                                   in0=bank[:, 0:64],
                                   scalar1=rcol[:, hh:hh + 1], scalar2=None,
                                   op0=ALU.mult),
                               reads=[(bank, "s"), (rcol, hh)],
                               writes=[(out_tm, f"t{ti}h{h}")],
                               tag=f"{prefix}_no{qt}{pc}{hh}{s}")
    oar = oat_dram.rearrange("(n p) d -> n p d", p=128)
    for ti in range(NBT):
        p.dma("sync", oar[ti], out_tm[:, ti * 512:(ti + 1) * 512],
              reads=[(out_tm, f"t{ti}h{h}") for h in range(HPC)],
              tag=f"{prefix}_so{ti}")


def emit_l2_attn(ctx, W, yb_fm, out_fm, scope, prefix="a2", q_fm=None,
                 k_fm=None):
    nc, p = ctx.nc, ctx.p
    H, Dh, AUG = 16, 64, 65
    SEGS = ((512, 1), (1024, 2), (2048, 4))
    if L2_BISECT == "nodil":
        SEGS = ((512, 1),)
    if q_fm is None:
        q_fm = ctx.sb(f"{prefix}_qfm", [128, DC * T], BF16, stack=scope)
        k_fm = ctx.sb(f"{prefix}_kfm", [128, DC * T], BF16, stack=scope)
    v_fm = ctx.sb(f"{prefix}_vfm", [128, DC * T], BF16, stack=scope)
    for mi, (secname, dst, scale) in enumerate(
            [("q", q_fm, 0.125), ("k", k_fm, 1.0), ("v", v_fm, 1.0)]):
        with ExitStack() as sw:
            ps = [ctx.ps_f32(f"{prefix}_w{mi}p{i}", stack=sw)
                  for i in range(8)]
            weff = ctx.sb(f"{prefix}_weff{mi}", [128, DC * D], BF16, stack=sw)
            wsec = ctx.sb(f"{prefix}_wsec{mi}", [128, DC * D], BF16, stack=sw)
            wr = W[secname + "w"].rearrange("(c p) o -> c p o", p=128)
            for c in range(DC):
                p.dma("gpsimd", wsec[:, c * D:(c + 1) * D], wr[c],
                      writes=[(wsec, c)], tag=f"{prefix}_lw{mi}{c}")
            sw1 = sw.enter_context(ExitStack())
            cat2 = [ctx.sb(f"{prefix}_cat{mi}{v}", [128, D], BF16, stack=sw1)
                    for v in range(4)]
            catr = W["caT"].rearrange("(s c p) o -> s c p o", p=128, c=DC)
            for ph in range(2):
                combos = [(dc, half) for dc in range(DC) for half in range(2)
                          ][ph * 8:(ph + 1) * 8]
                for mc in range(DC):
                    cat = cat2[(ph * DC + mc) % 4]
                    p.dma("gpsimd", cat[:], catr[mi, mc],
                          writes=[cat], tag=f"{prefix}_lc{mi}{ph}{mc}")
                    for bi, (dc, half) in enumerate(combos):
                        p.emit("tensor", lambda e, bank=ps[bi], dc=dc,
                               half=half, mc=mc, cat=cat, wsec=wsec:
                               e.matmul(
                                   bank[:, 0:512],
                                   cat[:, dc * 128:(dc + 1) * 128],
                                   wsec[:, mc * D + half * 512:
                                        mc * D + half * 512 + 512],
                                   start=(mc == 0), stop=(mc == DC - 1)),
                               reads=[cat, (wsec, mc)],
                               writes=[(ps[bi], "s")],
                               tag=f"{prefix}_pm{mi}{ph}{mc}{bi}")
                for bi, (dc, half) in enumerate(combos):
                    p.emit("scalar", lambda e, bank=ps[bi], dc=dc,
                           half=half, weff=weff, scale=scale: e.activation(
                               out=weff[:, dc * D + half * 512:
                                        dc * D + half * 512 + 512],
                               in_=bank[:, 0:512], func=AF.Identity,
                               scale=float(scale)),
                           reads=[(ps[bi], "s")], writes=[(weff, dc)],
                           tag=f"{prefix}_pe{mi}{ph}{bi}")
            sw1.close()
            beffc = ctx.sb(f"{prefix}_beffc{mi}", [128, DC], F32, stack=sw)
            sw2 = sw.enter_context(ExitStack())
            cabr = ctx.sb(f"{prefix}_cabr{mi}", [1, D], F32, stack=sw2)
            p.dma("sync", cabr[:], W["cab"][0:1, mi * D:(mi + 1) * D],
                  writes=[cabr], tag=f"{prefix}_lcb{mi}")
            bsr = ctx.sb(f"{prefix}_bsr{mi}", [1, D], F32, stack=sw2)
            p.dma("sync", bsr[:], W[secname + "b"], writes=[bsr],
                  tag=f"{prefix}_lbs{mi}")
            bcol = ctx.sb(f"{prefix}_bcol{mi}", [128, DC], BF16, stack=sw2)
            row_to_cols(ctx, cabr[0:1, :], DC, bcol, f"{prefix}_bc{mi}",
                        ps[6], row_reads=[cabr])
            brow = ctx.sb(f"{prefix}_brow{mi}", [1, D], F32, stack=sw2)
            for half in range(2):
                bb = ps[6 + half]
                for mc in range(DC):
                    p.emit("tensor", lambda e, half=half, mc=mc,
                           bb=bb, bcol=bcol, wsec=wsec: e.matmul(
                        bb[0:1, 0:512],
                        bcol[:, mc:mc + 1],
                        wsec[:, mc * D + half * 512:mc * D + half * 512 + 512],
                        start=(mc == 0), stop=(mc == DC - 1)),
                        reads=[bcol, (wsec, mc)], writes=[(bb, "s")],
                        tag=f"{prefix}_pb{mi}{half}{mc}")
                p.emit("vector", lambda e, bb=bb, half=half, brow=brow,
                       bsr=bsr: e.tensor_add(
                    brow[0:1, half * 512:half * 512 + 512], bb[0:1, 0:512],
                    bsr[0:1, half * 512:half * 512 + 512]),
                    reads=[(bb, "s"), bsr], writes=[(brow, half)],
                    tag=f"{prefix}_pbe{mi}{half}")
            if scale != 1.0:
                p.emit("vector", lambda e, brow=brow, scale=scale:
                       e.tensor_scalar_mul(
                    brow[:], brow[:], float(scale)),
                    reads=[brow], writes=[brow], tag=f"{prefix}_pbs{mi}")
            row_to_cols(ctx, brow[0:1, :], DC, beffc, f"{prefix}_be{mi}",
                        ps[6], row_reads=[brow])
            sw2.close()
            for c in range(DC):
                for g in range(4):
                    bank = ps[g % 2]
                    pairs = [(weff[:, dc * D + c * 128:dc * D + c * 128 + 128],
                              yb_fm[:, dc * T + g * 512:dc * T + g * 512 + 512],
                              [(weff, dc), (yb_fm, WHOLE)])
                             for dc in range(DC)]
                    mm_accum(p, bank, "s", pairs, 0, 512,
                             tag=f"{prefix}_mp{mi}{c}{g}")
                    p.emit("scalar", lambda e, bank=bank, c=c, g=g, dst=dst,
                           beffc=beffc: e.activation(
                               out=dst[:, c * T + g * 512:c * T + g * 512 + 512],
                               in_=bank[:, 0:512], func=AF.Identity,
                               bias=beffc[:, c:c + 1]),
                           reads=[(bank, "s"), beffc],
                           writes=[(dst, f"{c}_{g}")],
                           tag=f"{prefix}_ep{mi}{c}{g}")
    masks = ctx.sb(f"{prefix}_msk", [128, 4 * 1024], BF16, stack=scope)
    for kk in range(4):
        for half in range(2):
            sl = masks[:, kk * 1024 + half * 512:kk * 1024 + half * 512 + 512]
            p.emit("gpsimd", lambda e, sl=sl: e.memset(sl, 1.0),
                   writes=[(masks, kk)], tag=f"{prefix}_mm{kk}{half}")
            p.emit("gpsimd", lambda e, sl=sl, kk=kk: e.affine_select(
                out=sl, in_=sl, compare_op=ALU.is_ge, fill=0.0,
                base=-128 * kk, pattern=[[1, 512]], channel_multiplier=-1),
                reads=[(masks, kk)], writes=[(masks, kk)],
                tag=f"{prefix}_ma{kk}{half}")
    lvl = 99
    if L2_BISECT.startswith("cut"):
        lvl = int(L2_BISECT[3])
    if "nodil" in L2_BISECT:
        SEGS = ((512, 1),)
    if L2_BISECT == "noattn":
        p.emit("gpsimd", lambda e: e.memset(out_fm[:], 0.125),
               writes=[(out_fm, f"{pc_}_{g_}") for pc_ in range(8)
                       for g_ in range(4)], tag=f"{prefix}_cut")
        return
    with ExitStack() as sa:
        ps = [ctx.ps_f32(f"{prefix}_as{i}", stack=sa) for i in range(6)]
        trb = ctx.ps_bf16(f"{prefix}_atr", stack=sa)
        id32 = ctx.ident32()
        qg = ctx.sb(f"{prefix}_qg", [128, 512], BF16, stack=sa)
        kg = ctx.sb(f"{prefix}_kg", [128, 512], BF16, stack=sa)
        vg = ctx.sb(f"{prefix}_vg", [128, 512], BF16, stack=sa)
        vgB = ctx.sb(f"{prefix}_vgB", [64, 512], BF16, stack=sa)
        vt = ctx.sb(f"{prefix}_vt", [128, 8 * AUG], BF16, stack=sa)
        p.emit("gpsimd", lambda e: e.memset(vt[:, 64::AUG], 1.0),
               writes=[(vt, "aug")], tag=f"{prefix}_va")
        PAs = [ctx.sb(f"{prefix}_PA{v}", [128, 512], BF16, stack=sa)
               for v in range(2)]
        PBs = [ctx.sb(f"{prefix}_PB{v}", [128, 512], BF16, stack=sa)
               for v in range(2)]
        nacc = ctx.sb(f"{prefix}_nacc", [128, 2 * T], F32, stack=sa)
        rcp = ctx.sb(f"{prefix}_rcp", [128, 16], F32, stack=sa)
        rrow = ctx.sb(f"{prefix}_rrow", [1, T], F32, stack=sa)
        for pc in range(8):
            for br, (w, r) in enumerate(SEGS):
                nseg = T // w
                for seg in range(nseg):
                    seg0 = seg * w
                    for gbuf, src in [(qg, q_fm), (kg, k_fm), (vg, v_fm)]:
                        if r == 1:
                            gb, gcol, grd = (src, pc * T + seg0,
                                             [(src, f"{pc}_{seg0 // 512}")])
                        else:
                            for hh in range(2):
                                off = (2 * pc + hh) % r
                                srd = [(src, f"{pc}_{t}") for t in range(4)]
                                p.emit("vector",
                                       lambda e, gbuf=gbuf, src=src, hh=hh,
                                       off=off, pc=pc, seg0=seg0, w=w, r=r:
                                       e.tensor_copy(
                                           gbuf[hh * 64:hh * 64 + 64, :],
                                           src[hh * 64:hh * 64 + 64,
                                               pc * T + seg0 + off:
                                               pc * T + seg0 + w:r]),
                                       reads=srd, writes=[(gbuf, hh)],
                                       tag=f"{prefix}_g{pc}{br}{seg}")
                            gb, gcol, grd = gbuf, 0, [(gbuf, 0), (gbuf, 1)]
                        if gbuf is qg:
                            qb_, qcol, qrd = gb, gcol, grd
                        elif gbuf is kg:
                            kb_, kcol, krd = gb, gcol, grd
                        else:
                            vb_, vcol, vrd = gb, gcol, grd
                    p.emit("vector", lambda e, vb_=vb_, vcol=vcol:
                           e.tensor_copy(vgB[0:64, :],
                                         vb_[64:128, vcol:vcol + 512]),
                           reads=vrd, writes=[vgB],
                           tag=f"{prefix}_vgb{pc}{br}{seg}")
                    for s in range(4):
                        for hh in range(2):
                            vsrc = (vb_[0:64, vcol + s * 128:
                                        vcol + (s + 1) * 128]
                                    if hh == 0 else
                                    vgB[0:64, s * 128:(s + 1) * 128])
                            vsrd = vrd if hh == 0 else [vgB]
                            p.emit("tensor", lambda e, vsrc=vsrc, s=s, hh=hh:
                                   e.transpose(
                                       trb[:, (2 * s + hh) * 128:
                                           (2 * s + hh) * 128 + 64],
                                       vsrc, ctx.ident[0:64, 0:64]),
                                   reads=vsrd + [ctx.ident],
                                   writes=[(trb, "s")],
                                   tag=f"{prefix}_vt{pc}{br}{seg}{s}{hh}")
                    p.emit("vector", lambda e: e.tensor_copy(
                        vt.rearrange("p (n ea) -> p n ea", ea=AUG)[:, :, 0:64],
                        trb.rearrange("p (n x) -> p n x", x=128)[:, 0:8, 0:64]),
                        reads=[(trb, "s")], writes=[(vt, "d")],
                        tag=f"{prefix}_vte{pc}{br}{seg}")
                    nA, nB = ps[4], ps[5]

                    def emit_qk2(s, kb_=kb_, kcol=kcol, qb_=qb_, qcol=qcol,
                                 pc=pc, br=br, seg=seg):
                        N = 512 - 128 * s
                        for hh, sbank in [(0, ps[2 * (s % 2)]),
                                          (1, ps[2 * (s % 2) + 1])]:
                            p.emit("tensor", lambda e, hh=hh, s=s, N=N,
                                   kb_=kb_, kcol=kcol, qb_=qb_, qcol=qcol,
                                   sbank=sbank: e.matmul(
                                       sbank[:, 0:N],
                                       kb_[hh * 64:hh * 64 + 64,
                                           kcol + s * 128:kcol + (s + 1) * 128],
                                       qb_[hh * 64:hh * 64 + 64,
                                           qcol + s * 128:qcol + 512],
                                       start=True, stop=True,
                                       tile_position=(64 * hh, 0)),
                                   reads=krd + qrd, writes=[(sbank, "s")],
                                   tag=f"{prefix}_qk{pc}{br}{seg}{s}{hh}")

                    if lvl >= 2:
                        emit_qk2(0)
                    for s in range(4):
                        N = 512 - 128 * s
                        if lvl < 2:
                            continue
                        if s + 1 < 4:
                            emit_qk2(s + 1)
                        for hh, (sbank, P, nb) in enumerate(
                                [(ps[2 * (s % 2)], PAs[s % 2], nA),
                                 (ps[2 * (s % 2) + 1], PBs[s % 2], nB)]):
                            p.emit("scalar", lambda e, P=P, sbank=sbank, N=N:
                                   e.activation(out=P[:, 0:N],
                                                in_=sbank[:, 0:N],
                                                func=AF.Exp),
                                   reads=[(sbank, "s")], writes=[P],
                                   tag=f"{prefix}_ex{pc}{br}{seg}{s}{hh}")
                            p.emit("vector", lambda e, P=P, s=s, N=N:
                                   e.tensor_mul(
                                       P[:, 0:N], P[:, 0:N],
                                       masks[:, s * 512 + 128 * s:
                                             s * 512 + 512]),
                                   reads=[P, (masks, s)], writes=[P],
                                   tag=f"{prefix}_msk{pc}{br}{seg}{s}{hh}")
                            if lvl < 3:
                                continue
                            p.emit("tensor", lambda e, P=P, nb=nb, s=s, N=N,
                                   hh=hh: e.matmul(
                                       nb[0:AUG, 128 * s:512],
                                       vt[:, (2 * s + hh) * AUG:
                                          (2 * s + hh + 1) * AUG],
                                       P[:, 0:N], start=(s == 0),
                                       stop=(s == 3)),
                                   reads=[(vt, "d"), (vt, "aug"), P],
                                   writes=[(nb, "s")],
                                   tag=f"{prefix}_pv{pc}{br}{seg}{s}{hh}")
                    for hh, nb in ([] if lvl < 3 else [(0, nA), (1, nB)]):
                        off = (2 * pc + hh) % r
                        if r == 1:
                            p.emit("vector", lambda e, hh=hh, nb=nb,
                                   seg0=seg0: e.tensor_copy(
                                       nacc[0:AUG, hh * T + seg0:
                                            hh * T + seg0 + 512],
                                       nb[0:AUG, 0:512]),
                                   reads=[(nb, "s")], writes=[(nacc, hh)],
                                   tag=f"{prefix}_na{pc}{br}{seg}{hh}")
                        else:
                            p.emit("vector", lambda e, hh=hh, nb=nb,
                                   seg0=seg0, off=off, w=w, r=r:
                                   e.tensor_add(
                                       nacc[0:AUG, hh * T + seg0 + off:
                                            hh * T + seg0 + w:r],
                                       nacc[0:AUG, hh * T + seg0 + off:
                                            hh * T + seg0 + w:r],
                                       nb[0:AUG, 0:512]),
                                   reads=[(nb, "s"), (nacc, hh)],
                                   writes=[(nacc, hh)],
                                   tag=f"{prefix}_na{pc}{br}{seg}{hh}")
            for hh in (range(2) if lvl >= 4 else []):
                for s in range(NT):
                    p.emit("tensor", lambda e, hh=hh, s=s: e.transpose(
                        ps[2][0:128, s:s + 1],
                        nacc[64:65, hh * T + s * 128:hh * T + (s + 1) * 128],
                        id32[64:65, 64:65]),
                        reads=[(nacc, hh), id32], writes=[(ps[2], "s")],
                        tag=f"{prefix}_dt{pc}{hh}{s}")
                p.emit("vector", lambda e, hh=hh: e.reciprocal(
                    rcp[:, 0:NT], ps[2][:, 0:NT]),
                    reads=[(ps[2], "s")], writes=[rcp],
                    tag=f"{prefix}_dr{pc}{hh}")
                for s in range(NT):
                    p.emit("tensor", lambda e, s=s: e.transpose(
                        ps[3][0:1, (s % 4) * 128:(s % 4) * 128 + 128],
                        rcp[:, s:s + 1], id32[:]),
                        reads=[rcp, id32], writes=[(ps[3], "s")],
                        tag=f"{prefix}_rt{pc}{hh}{s}")
                    if s % 4 == 3:
                        p.emit("vector", lambda e, s=s: e.tensor_copy(
                            rrow[0:1, (s - 3) * 128:(s + 1) * 128],
                            ps[3][0:1, 0:512]),
                            reads=[(ps[3], "s")], writes=[(rrow, s // 4)],
                            tag=f"{prefix}_rre{pc}{hh}{s}")
                for g in range(4):
                    p.emit("tensor", lambda e, g=g: e.matmul(
                        ps[2][:, 0:512], ctx.ones_row[:],
                        rrow[0:1, g * 512:(g + 1) * 512],
                        start=True, stop=True),
                        reads=[ctx.ones_row, (rrow, g)],
                        writes=[(ps[2], "s")], tag=f"{prefix}_rb{pc}{hh}{g}")
                    p.emit("vector", lambda e, hh=hh, g=g, pc=pc:
                           e.tensor_mul(
                               out_fm[hh * 64:hh * 64 + 64,
                                      pc * T + g * 512:pc * T + (g + 1) * 512],
                               nacc[0:64, hh * T + g * 512:
                                    hh * T + (g + 1) * 512],
                               ps[2][0:64, 0:512]),
                           reads=[(nacc, hh), (ps[2], "s")],
                           writes=[(out_fm, f"{pc}_{g}")],
                           tag=f"{prefix}_ou{pc}{hh}{g}")
        if lvl < 4:
            p.emit("gpsimd", lambda e: e.memset(out_fm[:], 0.125),
                   writes=[(out_fm, f"{pc_}_{g_}") for pc_ in range(8)
                           for g_ in range(4)], tag=f"{prefix}_cut2")


def emit_l3_attn_core_fused(ctx, W, co, exin, exout, qk, scope, prefix="a3"):
    nc, p = ctx.nc, ctx.p
    NB = 2 * T
    NBT = NB // 128
    HPC = 8
    AUG = 65
    QOFF, KOFF = 0, 4 * NB
    v_tm = ctx.sb(f"{prefix}_vtm", [128, NBT * HPC * AUG], BF16, stack=scope)
    out_tm = ctx.sb(f"{prefix}_otm", [128, NBT * 512], BF16, stack=scope)
    p.emit("gpsimd", lambda e: e.memset(v_tm[:, 64::AUG], 1.0),
           writes=[(v_tm, "aug")], tag=f"{prefix}_aug")

    with ExitStack() as s1:
        ps = [ctx.ps_f32(f"{prefix}_ps{i}", stack=s1) for i in range(4)]
        trp = [ctx.ps_bf16(f"{prefix}_trp", stack=s1)]
        wq = ctx.sb(f"{prefix}_wqs", [128, DC * 512], BF16, stack=s1)
        wk = ctx.sb(f"{prefix}_wks", [128, DC * 512], BF16, stack=s1)
        wv = ctx.sb(f"{prefix}_wvs", [128, DC * 512], BF16, stack=s1)
        for nm, buf in [("wq", wq), ("wk", wk), ("wv", wv)]:
            r = W[nm].rearrange("(c p) o -> c p o", p=128)
            for c in range(DC):
                p.dma("gpsimd", buf[:, c * 512:(c + 1) * 512], r[c],
                      writes=[(buf, c)], tag=f"{prefix}_l{nm}{c}")
        bqr = ctx.sb(f"{prefix}_bqrs", [1, 512], F32, stack=s1)
        bkr = ctx.sb(f"{prefix}_bkrs", [1, 512], F32, stack=s1)
        bvr = ctx.sb(f"{prefix}_bvrs", [1, 512], F32, stack=s1)
        for nm, buf in [("bq", bqr), ("bk", bkr), ("bv", bvr)]:
            p.dma("sync", buf[:], W[nm], writes=[buf], tag=f"{prefix}_l{nm}")
        bq_col = ctx.sb(f"{prefix}_bqcs", [128, 4], F32, stack=s1)
        bk_col = ctx.sb(f"{prefix}_bkcs", [128, 4], F32, stack=s1)
        row_to_cols(ctx, bqr[0:1, :], 4, bq_col, f"{prefix}_bqcs", ps[3],
                    row_reads=[bqr], scale=0.125)
        row_to_cols(ctx, bkr[0:1, :], 4, bk_col, f"{prefix}_bkcs", ps[3],
                    row_reads=[bkr])
        bv_bc = ctx.broadcast_row(bvr, 512, f"{prefix}_bvbcs", [ps[3]],
                                  stack=s1, row_reads=[bvr])
        ybg_tm = ctx.sb(f"{prefix}_ybgts", [128, 4 * D], BF16, stack=s1)
        ybg = ctx.sb(f"{prefix}_ybgs", [128, DC * 512], BF16, stack=s1)
        cor = co.ap().rearrange("(n p) d -> n p d", p=128)
        for g in range(8):
            with ExitStack() as sg_:
                for j in range(4):
                    p.dma("sync", ybg_tm[:, j * D:(j + 1) * D],
                          cor[g * 4 + j], reads=[co],
                          writes=[(ybg_tm, f"t{j}")],
                          tag=f"{prefix}_lx{g}{j}")
                transpose_tm_to_fm(ctx, ybg_tm, [0, 1, 2, 3], ybg, 0, 512,
                                   f"{prefix}_tg{g}", trp)
            t0 = g * 512
            for c in range(4):
                for nmb, wb, bc, dof in [("q", wq, bq_col, QOFF),
                                         ("k", wk, bk_col, KOFF)]:
                    bank = ps[0 if nmb == "q" else 1]
                    pairs = [(wb[:, dc * 512 + c * 128:dc * 512 + c * 128 + 128],
                              ybg[:, dc * 512:(dc + 1) * 512],
                              [(wb, dc), (ybg, WHOLE)]) for dc in range(DC)]
                    mm_accum(p, bank, "s", pairs, 0, 512,
                             tag=f"{prefix}_m{nmb}{g}{c}")
                    p.emit("scalar", lambda e, bank=bank, c=c, t0=t0, dof=dof,
                           bc=bc: e.activation(
                               out=qk[:, dof + c * NB + t0:
                                      dof + c * NB + t0 + 512],
                               in_=bank[:, 0:512], func=AF.Identity,
                               bias=bc[:, c:c + 1]),
                           reads=[(bank, "s"), bc],
                           writes=[(qk, f"{nmb}{c}_{g}")],
                           tag=f"{prefix}_e{nmb}{g}{c}")
            for j in range(4):
                ti = g * 4 + j
                bank = ps[2]
                pairs = [(ybg[:, dc * 512 + j * 128:dc * 512 + (j + 1) * 128],
                          wv[:, dc * 512:(dc + 1) * 512],
                          [(ybg, WHOLE), (wv, dc)]) for dc in range(DC)]
                mm_accum(p, bank, "s", pairs, 0, 512,
                         tag=f"{prefix}_mv{g}{j}")
                vview = v_tm[:, ti * HPC * AUG:(ti + 1) * HPC * AUG]
                vv3 = vview.rearrange("p (h ea) -> p h ea", ea=AUG)
                p.emit("vector", lambda e, bank=bank, vv3=vv3: e.tensor_add(
                    vv3[:, :, 0:64],
                    bank[:, 0:512].rearrange("p (h ea) -> p h ea", ea=64),
                    bv_bc[:, 0:512].rearrange("p (h ea) -> p h ea", ea=64)),
                    reads=[(bank, "s"), (bv_bc, "c0")],
                    writes=[(v_tm, f"t{ti}")], tag=f"{prefix}_ev{g}{j}")
    masks = ctx.sb(f"{prefix}_msk", [128, 4 * 1024], BF16, stack=scope)
    for kk in range(4):
        for half in range(2):
            sl = masks[:, kk * 1024 + half * 512:kk * 1024 + half * 512 + 512]
            p.emit("gpsimd", lambda e, sl=sl: e.memset(sl, 1.0),
                   writes=[(masks, kk)], tag=f"{prefix}_mm{kk}{half}")
            p.emit("gpsimd", lambda e, sl=sl, kk=kk: e.affine_select(
                out=sl, in_=sl, compare_op=ALU.is_ge, fill=0.0,
                base=-128 * kk, pattern=[[1, 512]], channel_multiplier=-1),
                reads=[(masks, kk)], writes=[(masks, kk)],
                tag=f"{prefix}_ma{kk}{half}")
    with ExitStack() as s2:
        qkp = [ctx._ps(f"{prefix}_qk{v}", [128, 1024], F32, s2)
               for v in range(2)]
        nab = [ctx._ps(f"{prefix}_nab{v}", [128, 1024], F32, s2)
               for v in range(2)]
        id32 = ctx.ident32()
        PABs = [ctx.sb(f"{prefix}_PAB{v}", [128, 1024], BF16, stack=s2)
                for v in range(2)]
        nsb = ctx.sb(f"{prefix}_nsb", [128, 1024], F32, stack=s2)
        rcol = ctx.sb(f"{prefix}_rcol", [128, 2], F32, stack=s2)
        for qt in range(8):
            for pc in range(4):
                nT = nab[pc % 2]
                kmax = 4 * qt + 4

                def emit_qk(kc, pc=pc, qt=qt):
                    sT = qkp[kc % 2]
                    for hh in range(2):
                        p.emit("tensor", lambda e, pc=pc, kc=kc, qt=qt,
                               sT=sT, hh=hh: e.matmul(
                                   sT[:, hh * 512:hh * 512 + 512],
                                   qk[hh * 64:hh * 64 + 64,
                                      KOFF + pc * NB + kc * 128:
                                      KOFF + pc * NB + (kc + 1) * 128],
                                   qk[hh * 64:hh * 64 + 64,
                                      QOFF + pc * NB + qt * 512:
                                      QOFF + pc * NB + qt * 512 + 512],
                                   start=True, stop=True,
                                   tile_position=(64 * hh, 0)),
                               reads=[(qk, f"k{pc}_{kc // 4}"),
                                      (qk, f"q{pc}_{qt}")],
                               writes=[(sT, "AB"[hh])],
                               tag=f"{prefix}_s{qt}{pc}{kc}{hh}")

                emit_qk(0)
                for kc in range(kmax):
                    sT = qkp[kc % 2]
                    PAB = PABs[kc % 2]
                    if kc + 1 < kmax:
                        emit_qk(kc + 1)
                    p.emit("scalar", lambda e, PAB=PAB, sT=sT:
                           e.activation(out=PAB[:], in_=sT[:, 0:1024],
                                        func=AF.Exp, scale=0.125),
                           reads=[(sT, "A"), (sT, "B")], writes=[PAB],
                           tag=f"{prefix}_ex{qt}{pc}{kc}")
                    kk = kc - 4 * qt
                    if kk >= 0:
                        p.emit("vector", lambda e, PAB=PAB, kk=kk:
                               e.tensor_mul(
                                   PAB[:], PAB[:],
                                   masks[:, kk * 1024:(kk + 1) * 1024]),
                               reads=[PAB, (masks, kk)], writes=[PAB],
                               tag=f"{prefix}_mk{qt}{pc}{kc}")
                    for hh in range(2):
                        h = 2 * pc + hh
                        p.emit("tensor", lambda e, kc=kc, h=h, PAB=PAB,
                               nT=nT, hh=hh,
                               st=(kc == 0), sp=(kc == kmax - 1): e.matmul(
                                   nT[0:AUG, hh * 512:hh * 512 + 512],
                                   v_tm[:, kc * HPC * AUG + h * AUG:
                                        kc * HPC * AUG + (h + 1) * AUG],
                                   PAB[:, hh * 512:hh * 512 + 512],
                                   start=st, stop=sp),
                               reads=[(v_tm, f"t{kc}"), (v_tm, "aug"), PAB],
                               writes=[(nT, "AB"[hh])],
                               tag=f"{prefix}_pv{qt}{pc}{kc}{hh}")
                p.emit("scalar", lambda e, nT=nT: e.activation(
                    out=nsb[0:AUG, 0:1024],
                    in_=nT[0:AUG, 0:1024], func=AF.Copy, scale=1.0),
                    reads=[(nT, "A"), (nT, "B")],
                    writes=[(nsb, 0), (nsb, 1)],
                    tag=f"{prefix}_ne{qt}{pc}")
                for hh in range(2):
                    h = 2 * pc + hh
                    for s in range(4):
                        b0 = (s % 2) * 512
                        bk = "AB"[s % 2]
                        bank = qkp[1]
                        p.emit("tensor", lambda e, hh=hh, s=s, bank=bank,
                               b0=b0: e.transpose(
                                   bank[:, b0:b0 + AUG],
                                   nsb[0:AUG, hh * 512 + s * 128:
                                       hh * 512 + (s + 1) * 128],
                                   id32[0:AUG, 0:AUG]),
                               reads=[(nsb, hh), id32],
                               writes=[(bank, bk)],
                               tag=f"{prefix}_nt{qt}{pc}{hh}{s}")
                        p.emit("vector", lambda e, bank=bank, hh=hh, b0=b0:
                               e.reciprocal(rcol[:, hh:hh + 1],
                                            bank[:, b0 + 64:b0 + 65]),
                               reads=[(bank, bk)], writes=[(rcol, hh)],
                               tag=f"{prefix}_nr{qt}{pc}{hh}{s}")
                        ti = qt * 4 + s
                        p.emit("vector", lambda e, bank=bank, hh=hh, h=h,
                               ti=ti, b0=b0: e.tensor_scalar(
                                   out=out_tm[:, ti * 512 + h * 64:
                                              ti * 512 + (h + 1) * 64],
                                   in0=bank[:, b0:b0 + 64],
                                   scalar1=rcol[:, hh:hh + 1], scalar2=None,
                                   op0=ALU.mult),
                               reads=[(bank, bk), (rcol, hh)],
                               writes=[(out_tm, f"t{ti}h{h}")],
                               tag=f"{prefix}_no{qt}{pc}{hh}{s}")
    er = exin.ap().rearrange("(n p) d -> n p d", p=128)
    for ti in range(NBT):
        p.dma("sync", er[ti], out_tm[:, ti * 512:(ti + 1) * 512],
              reads=[(out_tm, f"t{ti}h{h}") for h in range(2 * HPC)],
              writes=[(exin, ti)], tag=f"{prefix}_so{ti}")
    p.emit("gpsimd", lambda e: e.collective_compute(
        "AllGather", ALU.bypass, replica_groups=RG_PAIRS,
        ins=[exin.ap().opt()], outs=[exout.ap().opt()]),
        reads=[(exin, i) for i in range(NBT)], writes=[exout],
        tag=f"{prefix}_cc")


def emit_l3_attn_proj_fused(ctx, W, x_buf, co2, mcol_d, m2col_d, scope,
                            prefix="a3b", pre=None):
    nc, p = ctx.nc, ctx.p
    with ExitStack() as sc:
        ps = [ctx.ps_f32(f"{prefix}_ps{i}", stack=sc) for i in range(4)]
        trp = [ctx.ps_bf16(f"{prefix}_trp{i}", stack=sc) for i in range(2)]
        if pre is None:
            ow = ctx.sb(f"{prefix}_ow", [128, DC * D], BF16, stack=sc)
            owr = W["o_w"].rearrange("(c p) o -> c p o", p=128)
            for c in range(DC):
                p.dma("gpsimd", ow[:, c * D:(c + 1) * D], owr[c],
                      writes=[(ow, c)], tag=f"{prefix}_low{c}")
            obr = ctx.sb(f"{prefix}_obr", [1, D], F32, stack=sc)
            p.dma("sync", obr[:], W["o_b"], writes=[obr],
                  tag=f"{prefix}_lob")
            mcol = ctx.sb(f"{prefix}_mc", [128, 1], F32, stack=sc)
            m2col = ctx.sb(f"{prefix}_mc2", [128, 1], F32, stack=sc)
            p.dma("sync", mcol[:], mcol_d, writes=[mcol],
                  tag=f"{prefix}_lm")
            p.dma("sync", m2col[:], m2col_d, writes=[m2col],
                  tag=f"{prefix}_lm2")
        else:
            ow, obr, mcol, m2col = pre
        ob_bc = ctx.broadcast_row(obr, D, f"{prefix}_obbc", [ps[3]],
                                  stack=sc, row_reads=[obr])
        cand = ctx.sb(f"{prefix}_cand", [128, 4 * 512], BF16, stack=sc)
        oat_t = ctx.sb(f"{prefix}_oat", [128, D], BF16, stack=sc)
        oat_fm = ctx.sb(f"{prefix}_ofm", [128, DC * T], BF16, stack=sc)
        c2r = co2.ap().rearrange("(n p) d -> n p d", p=128)
        for i in range(NT):
            for slot, tidx in enumerate([i, 16 + i, 32 + i, 48 + i]):
                p.emit("sync", lambda e, slot=slot, tidx=tidx: e.dma_start(
                    out=cand[:, slot * 512:(slot + 1) * 512], in_=c2r[tidx]),
                    reads=[co2], writes=[(cand, slot)], dma=True,
                    tag=f"{prefix}_lc{i}{slot}")
            for hhalf in range(2):
                a, b = 2 * hhalf, 2 * hhalf + 1
                p.emit("vector", lambda e, a=a, hhalf=hhalf: e.tensor_scalar(
                    out=oat_t[:, hhalf * 512:(hhalf + 1) * 512],
                    in0=cand[:, a * 512:(a + 1) * 512], scalar1=mcol[:],
                    scalar2=None, op0=ALU.mult),
                    reads=[(cand, a), mcol], writes=[(oat_t, hhalf)],
                    tag=f"{prefix}_sa{i}{hhalf}")
                p.emit("vector", lambda e, b=b, hhalf=hhalf:
                       e.scalar_tensor_tensor(
                           out=oat_t[:, hhalf * 512:(hhalf + 1) * 512],
                           in0=cand[:, b * 512:(b + 1) * 512],
                           scalar=m2col[:],
                           in1=oat_t[:, hhalf * 512:(hhalf + 1) * 512],
                           op0=ALU.mult, op1=ALU.add),
                       reads=[(cand, b), m2col, (oat_t, hhalf)],
                       writes=[(oat_t, hhalf)], tag=f"{prefix}_sb{i}{hhalf}")
            for c in range(DC):
                slot = c % 4
                bank = trp[(c // 4) % 2]
                p.emit("tensor", lambda e, c=c, bank=bank, slot=slot:
                       e.transpose(bank[:, slot * 128:slot * 128 + 128],
                                   oat_t[:, c * 128:(c + 1) * 128],
                                   ctx.ident[:]),
                       reads=[(oat_t, 0), (oat_t, 1), ctx.ident],
                       writes=[(bank, "s")], tag=f"{prefix}_t{i}{c}")
                if slot == 3:
                    p.emit("vector", lambda e, c=c, bank=bank, i=i:
                           e.tensor_copy(
                               oat_fm.rearrange("p (c t) -> p c t", t=T)
                               [:, c - 3:c + 1, i * 128:(i + 1) * 128],
                               bank.rearrange("p (c t) -> p c t", t=128)
                               [:, 0:4, :]),
                           reads=[(bank, "s")],
                           writes=[(oat_fm, f"{c // 4}_{i}")],
                           tag=f"{prefix}_te{i}{c}")
        z_sb = ctx.sb(f"{prefix}_z", [128, 4 * D], F32, stack=sc)
        for i in range(NT):
            j = i % 4
            for half in range(2):
                bank = ps[(2 * i + half) % 4]
                pairs = [(oat_fm[:, cc * T + i * 128:cc * T + (i + 1) * 128],
                          ow[:, cc * D + half * 512:cc * D + half * 512 + 512],
                          [(oat_fm, f"{cc // 4}_{i}"), (ow, cc)])
                         for cc in range(DC)]
                mm_accum(p, bank, "s", pairs, 0, 512,
                         tag=f"{prefix}_mz{i}h{half}")
                p.emit("vector", lambda e, bank=bank, j=j, half=half:
                       e.tensor_add(
                           z_sb[:, j * D + half * 512:j * D + half * 512 + 512],
                           bank[:, 0:512],
                           ob_bc[:, half * 512:half * 512 + 512]),
                       reads=[(bank, "s"), (ob_bc, f"c{half * 512}")],
                       writes=[(z_sb, f"t{j}")], tag=f"{prefix}_ez{i}{half}")
            if j == 3:
                with ExitStack() as esc:
                    sandwich_epilogue(ctx, z_sb, [0, 1, 2, 3], x_buf,
                                      [i - 3, i - 2, i - 1, i],
                                      lambda t: f"t{t}",
                                      f"{prefix}_e{i // 4}", esc)


def emit_l3_attn_proj(ctx, W, x_tm, oat_dram, prefix="a3b"):
    nc, p = ctx.nc, ctx.p
    with ExitStack() as sc:
        ps = [ctx.ps_f32(f"{prefix}_ps{i}", stack=sc) for i in range(4)]
        trp = [ctx.ps_bf16(f"{prefix}_trp{i}", stack=sc) for i in range(2)]
        ow = ctx.sb(f"{prefix}_ow", [128, DC * D], BF16, stack=sc)
        owr = W["o_w"].rearrange("(c p) o -> c p o", p=128)
        for c in range(DC):
            p.dma("gpsimd", ow[:, c * D:(c + 1) * D], owr[c],
                  writes=[(ow, c)], tag=f"{prefix}_low{c}")
        obr = ctx.sb(f"{prefix}_obr", [1, D], F32, stack=sc)
        p.dma("sync", obr[:], W["o_b"], writes=[obr], tag=f"{prefix}_lob")
        ob_bc = ctx.broadcast_row(obr, D, f"{prefix}_obbc", [ps[3]],
                                  stack=sc, row_reads=[obr])
        oat_tm = ctx.sb(f"{prefix}_oat", [128, NT * D], BF16, stack=sc)
        oar = oat_dram.rearrange("(n p) d -> n p d", p=128)
        for i in range(NT):
            p.dma("sync", oat_tm[:, i * D:(i + 1) * D], oar[i],
                  writes=[(oat_tm, f"t{i}")], tag=f"{prefix}_lo{i}")
        oat_fm = ctx.sb(f"{prefix}_ofm", [128, DC * T], BF16, stack=sc)
        transpose_tm_to_fm(ctx, oat_tm, list(range(NT)), oat_fm, 0, T,
                           f"{prefix}_t", trp)
        z_sb = ctx.sb(f"{prefix}_z", [128, 4 * D], F32, stack=sc)
        for i in range(NT):
            j = i % 4
            for half in range(2):
                bank = ps[half]
                pairs = [(oat_fm[:, cc * T + i * 128:cc * T + (i + 1) * 128],
                          ow[:, cc * D + half * 512:cc * D + half * 512 + 512],
                          [(oat_fm, f"c{cc}g{(i // 4) * 512}"), (ow, cc)])
                         for cc in range(DC)]
                mm_accum(p, bank, "s", pairs, 0, 512,
                         tag=f"{prefix}_mz{i}h{half}")
                p.emit("vector", lambda e, bank=bank, j=j, half=half:
                       e.tensor_add(
                           z_sb[:, j * D + half * 512:j * D + half * 512 + 512],
                           bank[:, 0:512],
                           ob_bc[:, half * 512:half * 512 + 512]),
                       reads=[(bank, "s"), (ob_bc, f"c{half * 512}")],
                       writes=[(z_sb, f"t{j}")], tag=f"{prefix}_ez{i}{half}")
            if j == 3:
                with ExitStack() as esc:
                    sandwich_epilogue(ctx, z_sb, [0, 1, 2, 3], x_tm,
                                      [i - 3, i - 2, i - 1, i],
                                      lambda t: f"t{t}",
                                      f"{prefix}_e{i // 4}", esc)


_PROG_CACHE = {}


def build_tokenlocal_prog(weight_specs, emit_fn):
    nc = bass.Bass("TRN2", target_bir_lowering=False)
    x = nc.dram_tensor("x", [T, D], F32, kind="ExternalInput").ap()
    W = {n: nc.dram_tensor(n, shp, dt, kind="ExternalInput").ap()
         for n, shp, dt in weight_specs}
    xout = nc.dram_tensor("xout", [T, D], F32, kind="ExternalOutput").ap()
    stack = ExitStack()
    with stack:
        p = Prog(nc)
        ctx = Ctx(nc, p, stack)
        yb_fm = ctx.sb("yb_fm", [128, DC * T], BF16)
        x_tm = ctx.sb("x_tm", [128, NT * D], F32)
        xr = x.rearrange("(n p) d -> n p d", p=128)
        xor_ = xout.rearrange("(n p) d -> n p d", p=128)
        for i in range(NT):
            p.dma("sync", x_tm[:, i * D:(i + 1) * D], xr[i],
                  writes=[(x_tm, f"t{i}")], tag=f"ldx{i}")
        with ExitStack() as sc0:
            trps = [ctx.ps_bf16(f"trps{i}", stack=sc0) for i in range(2)]
            ln_prologue(ctx, x_tm, list(range(NT)), lambda i: f"t{i}",
                        yb_fm, 0, T, "p", trps, sc0)
        emit_fn(ctx, W, yb_fm, x_tm, list(range(NT)), lambda i: f"t{i}")
        for i in range(NT):
            p.dma("sync", xor_[i], x_tm[:, i * D:(i + 1) * D],
                  reads=[(x_tm, f"t{i}")], tag=f"stx{i}")
        p.finalize(stack)
    return nc


def _run_sharded(progkey, build_fn, x_flat, wmap):
    if progkey not in _PROG_CACHE:
        _PROG_CACHE[progkey] = build_fn()
    nc = _PROG_CACHE[progkey]
    in_maps = [dict(wmap, x=s) for s in _shard_tokens(x_flat)]
    res = run_bass_kernel_spmd(nc, in_maps, core_ids=list(range(N_CORES)))
    return np.concatenate([r["xout"] for r in res.results], axis=0)


def run_l0_ffn(x_flat, inputs):
    wmap = {
        "base_wT": np.ascontiguousarray(
            np.asarray(inputs["L0_ffn_base_w"], np.float32).T),
        "base_b": np.asarray(
            inputs["L0_ffn_base_b"], np.float32).reshape(1, D),
        "gate_wT": np.ascontiguousarray(
            np.asarray(inputs["L0_ffn_gate_w"], np.float32).T),
        "A_stackT": np.ascontiguousarray(
            np.asarray(inputs["L0_ffn_lora_A"], np.float32)
            .transpose(2, 0, 1).reshape(D, 128)),
        "B_stackT": np.ascontiguousarray(
            np.asarray(inputs["L0_ffn_lora_B"], np.float32)
            .transpose(0, 2, 1).reshape(128, D)),
    }
    specs = [("base_wT", [D, D], F32), ("base_b", [1, D], F32),
             ("gate_wT", [D, 10], F32), ("A_stackT", [D, 128], F32),
             ("B_stackT", [128, D], F32)]
    return _run_sharded(
        "l0_ffn",
        lambda: build_tokenlocal_prog(specs, lambda ctx, W, yb, xb, tl, kf:
                                      emit_l0_ffn(ctx, W, yb, xb, tl, kf)),
        x_flat, wmap)


def build_l0_attn_prog():
    nc = bass.Bass("TRN2", target_bir_lowering=False)
    x = nc.dram_tensor("x", [T, D], F32, kind="ExternalInput").ap()
    xn = nc.dram_tensor("xn", [T, D], F32, kind="ExternalInput").ap()
    specs = [("w_qkv", [D, 1536], F32), ("b_qkv", [1, 1536], F32),
             ("w_fc", [512, D], F32), ("b_fc", [1, D], F32),
             ("gamma", [1, D], F32), ("beta", [1, D], F32)]
    W = {n: nc.dram_tensor(n, shp, dt, kind="ExternalInput").ap()
         for n, shp, dt in specs}
    xout = nc.dram_tensor("xout", [T, D], F32, kind="ExternalOutput").ap()
    stack = ExitStack()
    with stack:
        p = Prog(nc)
        ctx = Ctx(nc, p, stack)
        x_tm = ctx.sb("x_tm", [128, NT * D], F32)
        xr = x.rearrange("(n p) d -> n p d", p=128)
        xor_ = xout.rearrange("(n p) d -> n p d", p=128)
        for i in range(NT):
            p.dma("sync", x_tm[:, i * D:(i + 1) * D], xr[i],
                  writes=[(x_tm, f"t{i}")], tag=f"ldx{i}")
        emit_l0_attn(ctx, W, x_tm, xn)
        for i in range(NT):
            p.dma("sync", xor_[i], x_tm[:, i * D:(i + 1) * D],
                  reads=[(x_tm, f"t{i}")], tag=f"stx{i}")
        p.finalize(stack)
    return nc


def run_l0_attn(x_flat, inputs):
    f32 = lambda k: np.asarray(inputs[k], np.float32)
    wmap = {
        "w_qkv": np.ascontiguousarray(f32("L0_attn_w_qkv")),
        "b_qkv": f32("L0_attn_b_qkv").reshape(1, 1536),
        "w_fc": np.ascontiguousarray(f32("L0_attn_w_fc")),
        "b_fc": f32("L0_attn_b_fc").reshape(1, D),
        "gamma": f32("L0_attn_gamma").reshape(1, D),
        "beta": f32("L0_attn_beta").reshape(1, D),
    }
    if "l0_attn" not in _PROG_CACHE:
        _PROG_CACHE["l0_attn"] = build_l0_attn_prog()
    nc = _PROG_CACHE["l0_attn"]
    shards = _shard_tokens(x_flat)
    in_maps = [dict(wmap, x=shards[c], xn=shards[c ^ 1])
               for c in range(N_CORES)]
    res = run_bass_kernel_spmd(nc, in_maps, core_ids=list(range(N_CORES)))
    return np.concatenate([r["xout"] for r in res.results], axis=0)


def build_l1_attn_prog():
    nc = bass.Bass("TRN2", target_bir_lowering=False)
    x = nc.dram_tensor("x", [T, D], F32, kind="ExternalInput").ap()
    xn = nc.dram_tensor("xn", [T, D], F32, kind="ExternalInput").ap()
    specs = [("cl", [D, 256], F32), ("cr", [D, 256], F32),
             ("thr", [1, 1], F32), ("gb", [1, 1], F32)]
    W = {n: nc.dram_tensor(n, shp, dt, kind="ExternalInput").ap()
         for n, shp, dt in specs}
    xout = nc.dram_tensor("xout", [T, D], F32, kind="ExternalOutput").ap()
    stack = ExitStack()
    with stack:
        p = Prog(nc)
        ctx = Ctx(nc, p, stack)
        x_tm = ctx.sb("x_tm", [128, NT * D], F32)
        xr = x.rearrange("(n p) d -> n p d", p=128)
        xor_ = xout.rearrange("(n p) d -> n p d", p=128)
        for i in range(NT):
            p.dma("sync", x_tm[:, i * D:(i + 1) * D], xr[i],
                  writes=[(x_tm, f"t{i}")], tag=f"ldx{i}")
        emit_l1_attn(ctx, W, x_tm, xn)
        for i in range(NT):
            p.dma("sync", xor_[i], x_tm[:, i * D:(i + 1) * D],
                  reads=[(x_tm, f"t{i}")], tag=f"stx{i}")
        p.finalize(stack)
    return nc


def run_l1_attn(x_flat, inputs):
    f32 = lambda k: np.asarray(inputs[k], np.float32)
    wmap = {
        "cl": np.ascontiguousarray(f32("L1_attn_compare_left")),
        "cr": np.ascontiguousarray(f32("L1_attn_compare_right")),
        "thr": f32("L1_attn_threshold").reshape(1, 1),
        "gb": f32("L1_attn_gate_bias").reshape(1, 1),
    }
    if "l1_attn" not in _PROG_CACHE:
        _PROG_CACHE["l1_attn"] = build_l1_attn_prog()
    nc = _PROG_CACHE["l1_attn"]
    shards = _shard_tokens(x_flat)
    in_maps = [dict(wmap, x=shards[c], xn=shards[c ^ 1])
               for c in range(N_CORES)]
    res = run_bass_kernel_spmd(nc, in_maps, core_ids=list(range(N_CORES)))
    return np.concatenate([r["xout"] for r in res.results], axis=0)


def build_l2_attn_prog():
    nc = bass.Bass("TRN2", target_bir_lowering=False)
    x = nc.dram_tensor("x", [T, D], F32, kind="ExternalInput").ap()
    specs = [("caT", [3072, D], F32), ("qw", [D, D], F32),
             ("kw", [D, D], F32), ("vw", [D, D], F32),
             ("cab", [1, 3072], F32), ("qb", [1, D], F32),
             ("kb", [1, D], F32), ("vb", [1, D], F32),
             ("o_w", [D, D], F32), ("o_b", [1, D], F32)]
    W = {n: nc.dram_tensor(n, shp, dt, kind="ExternalInput").ap()
         for n, shp, dt in specs}
    xout = nc.dram_tensor("xout", [T, D], F32, kind="ExternalOutput").ap()
    stack = ExitStack()
    with stack:
        p = Prog(nc)
        ctx = Ctx(nc, p, stack)
        yb_fm = ctx.sb("yb_fm", [128, DC * T], BF16)
        out_fm = ctx.sb("out_fm", [128, DC * T], BF16)
        xr = x.rearrange("(n p) d -> n p d", p=128)
        xor_ = xout.rearrange("(n p) d -> n p d", p=128)
        with ExitStack() as sc0:
            x_tm = ctx.sb("x_tm", [128, NT * D], F32, stack=sc0)
            for i in range(NT):
                p.dma("sync", x_tm[:, i * D:(i + 1) * D], xr[i],
                      writes=[(x_tm, f"t{i}")], tag=f"ldx{i}")
            trps = [ctx.ps_bf16(f"trps{i}", stack=sc0) for i in range(2)]
            ln_prologue(ctx, x_tm, list(range(NT)), lambda i: f"t{i}",
                        yb_fm, 0, T, "p", trps, sc0)
        with ExitStack() as sqa:
            emit_l2_attn(ctx, W, yb_fm, out_fm, sqa)
        with ExitStack() as sc:
            ps = [ctx.ps_f32(f"op{i}", stack=sc) for i in range(4)]
            ow = ctx.sb("ow", [128, DC * D], BF16, stack=sc)
            owr = W["o_w"].rearrange("(c p) o -> c p o", p=128)
            for c in range(DC):
                p.dma("gpsimd", ow[:, c * D:(c + 1) * D], owr[c],
                      writes=[(ow, c)], tag=f"low{c}")
            obr = ctx.sb("obr", [1, D], F32, stack=sc)
            p.dma("sync", obr[:], W["o_b"], writes=[obr], tag="lob")
            ob_bc = ctx.broadcast_row(obr, D, "obbc", [ps[3]],
                                      stack=sc, row_reads=[obr])
            z_sb = ctx.sb("z_sb", [128, 4 * D], F32, stack=sc)
            xg = ctx.sb("xg", [128, 4 * D], F32, stack=sc)
            for i in range(NT):
                j = i % 4
                for half in range(2):
                    bank = ps[half]
                    pairs = [(out_fm[:, cc * T + i * 128:
                                     cc * T + (i + 1) * 128],
                              ow[:, cc * D + half * 512:
                                 cc * D + half * 512 + 512],
                              [(out_fm, f"{cc}_{(i * 128) // 512}"),
                               (ow, cc)]) for cc in range(DC)]
                    mm_accum(p, bank, "s", pairs, 0, 512,
                             tag=f"mz{i}h{half}")
                    p.emit("vector", lambda e, bank=bank, j=j, half=half:
                           e.tensor_add(
                               z_sb[:, j * D + half * 512:
                                    j * D + half * 512 + 512],
                               bank[:, 0:512],
                               ob_bc[:, half * 512:half * 512 + 512]),
                           reads=[(bank, "s"), (ob_bc, f"c{half * 512}")],
                           writes=[(z_sb, f"t{j}")], tag=f"ez{i}{half}")
                if j == 3:
                    g = i // 4
                    with ExitStack() as esc:
                        for jj in range(4):
                            p.dma("sync", xg[:, jj * D:(jj + 1) * D],
                                  xr[g * 4 + jj],
                                  writes=[(xg, f"t{jj}")], tag=f"eldx{i}{jj}")
                        sandwich_epilogue(ctx, z_sb, [0, 1, 2, 3], xg,
                                          [0, 1, 2, 3], lambda t: f"t{t}",
                                          f"ep{g}", esc)
                        for jj in range(4):
                            p.dma("sync", xor_[g * 4 + jj],
                                  xg[:, jj * D:(jj + 1) * D],
                                  reads=[(xg, f"t{jj}")], tag=f"estx{i}{jj}")
        p.finalize(stack)
    return nc


def run_l2_attn(x_flat, inputs):
    f32 = lambda k: np.asarray(inputs[k], np.float32)
    wmap = {
        "caT": np.ascontiguousarray(f32("L2_attn_c_attn_w").T),
        "qw": np.ascontiguousarray(f32("L2_attn_q_w")),
        "kw": np.ascontiguousarray(f32("L2_attn_k_w")),
        "vw": np.ascontiguousarray(f32("L2_attn_v_w")),
        "cab": f32("L2_attn_c_attn_b").reshape(1, 3072),
        "qb": f32("L2_attn_q_b").reshape(1, D),
        "kb": f32("L2_attn_k_b").reshape(1, D),
        "vb": f32("L2_attn_v_b").reshape(1, D),
        "o_w": np.ascontiguousarray(f32("L2_attn_o_w")),
        "o_b": f32("L2_attn_o_b").reshape(1, D),
    }
    return _run_sharded("l2_attn", build_l2_attn_prog, x_flat, wmap)


def build_l3_attn_a_prog():
    nc = bass.Bass("TRN2", target_bir_lowering=False)
    x1 = nc.dram_tensor("x1", [T, D], F32, kind="ExternalInput").ap()
    x2 = nc.dram_tensor("x2", [T, D], F32, kind="ExternalInput").ap()
    specs = [("wq", [D, 512], F32), ("wk", [D, 512], F32),
             ("wv", [D, 512], F32), ("bq", [1, 512], F32),
             ("bk", [1, 512], F32), ("bv", [1, 512], F32)]
    W = {n: nc.dram_tensor(n, shp, dt, kind="ExternalInput").ap()
         for n, shp, dt in specs}
    oat = nc.dram_tensor("oat", [2 * T, 512], BF16,
                         kind="ExternalOutput").ap()
    stack = ExitStack()
    with stack:
        p = Prog(nc)
        ctx = Ctx(nc, p, stack)
        emit_l3_attn_core(ctx, W, x1, x2, oat)
        p.finalize(stack)
    return nc


def build_l3_attn_b_prog():
    nc = bass.Bass("TRN2", target_bir_lowering=False)
    x = nc.dram_tensor("x", [T, D], F32, kind="ExternalInput").ap()
    oat = nc.dram_tensor("oat", [T, D], BF16, kind="ExternalInput").ap()
    W = {"o_w": nc.dram_tensor("o_w", [D, D], F32, kind="ExternalInput").ap(),
         "o_b": nc.dram_tensor("o_b", [1, D], F32, kind="ExternalInput").ap()}
    xout = nc.dram_tensor("xout", [T, D], F32, kind="ExternalOutput").ap()
    stack = ExitStack()
    with stack:
        p = Prog(nc)
        ctx = Ctx(nc, p, stack)
        x_tm = ctx.sb("x_tm", [128, NT * D], F32)
        xr = x.rearrange("(n p) d -> n p d", p=128)
        xor_ = xout.rearrange("(n p) d -> n p d", p=128)
        for i in range(NT):
            p.dma("sync", x_tm[:, i * D:(i + 1) * D], xr[i],
                  writes=[(x_tm, f"t{i}")], tag=f"ldx{i}")
        emit_l3_attn_proj(ctx, W, x_tm, oat)
        for i in range(NT):
            p.dma("sync", xor_[i], x_tm[:, i * D:(i + 1) * D],
                  reads=[(x_tm, f"t{i}")], tag=f"stx{i}")
        p.finalize(stack)
    return nc


def run_l3_attn(x_flat, inputs):
    f32 = lambda k: np.asarray(inputs[k], np.float32)
    aw = f32("L3_attn_attn_W")
    ab = f32("L3_attn_attn_b")
    if "l3_attn_a" not in _PROG_CACHE:
        _PROG_CACHE["l3_attn_a"] = build_l3_attn_a_prog()
        _PROG_CACHE["l3_attn_b"] = build_l3_attn_b_prog()
    in_maps = []
    for c in range(N_CORES):
        heads = range(0, 8) if c % 2 == 0 else range(8, 16)
        qc = [h * 192 + s for h in heads for s in range(64)]
        kc = [h * 192 + 64 + s for h in heads for s in range(64)]
        vc = [h * 192 + 128 + s for h in heads for s in range(64)]
        b = c // 2
        in_maps.append({
            "x1": np.ascontiguousarray(x_flat[b * 4096:b * 4096 + T]),
            "x2": np.ascontiguousarray(x_flat[b * 4096 + T:(b + 1) * 4096]),
            "wq": np.ascontiguousarray(aw[:, qc]),
            "wk": np.ascontiguousarray(aw[:, kc]),
            "wv": np.ascontiguousarray(aw[:, vc]),
            "bq": np.ascontiguousarray(ab[qc]).reshape(1, 512),
            "bk": np.ascontiguousarray(ab[kc]).reshape(1, 512),
            "bv": np.ascontiguousarray(ab[vc]).reshape(1, 512),
        })
    res = run_bass_kernel_spmd(_PROG_CACHE["l3_attn_a"], in_maps,
                               core_ids=list(range(N_CORES)))
    oats = [r["oat"] for r in res.results]
    wmap = {"o_w": np.ascontiguousarray(f32("L3_attn_out_W")),
            "o_b": f32("L3_attn_out_b").reshape(1, D)}
    in_maps2 = []
    for c in range(N_CORES):
        b = c // 2
        r0 = 0 if c % 2 == 0 else T
        oat_own = np.concatenate(
            [np.asarray(oats[2 * b])[r0:r0 + T],
             np.asarray(oats[2 * b + 1])[r0:r0 + T]], axis=1)
        in_maps2.append(dict(
            wmap, x=np.ascontiguousarray(x_flat[c * T:(c + 1) * T]),
            oat=np.ascontiguousarray(oat_own)))
    res2 = run_bass_kernel_spmd(_PROG_CACHE["l3_attn_b"], in_maps2,
                                core_ids=list(range(N_CORES)))
    return np.concatenate([r["xout"] for r in res2.results], axis=0)


def build_l3_ffn_prog():
    nc = bass.Bass("TRN2", target_bir_lowering=False)
    x = nc.dram_tensor("x", [T, D], F32, kind="ExternalInput").ap()
    specs = [("gate_wT", [D, 16], F32), ("wgT", [16, D, D], F32),
             ("w1T", [16, D, D], F32), ("w2T", [16, D, D], F32),
             ("sgwT", [D, 2048], F32), ("s1wT", [D, 2048], F32),
             ("s2wT", [2048, D], F32), ("sggT", [D, 1], F32)]
    W = {n: nc.dram_tensor(n, shp, dt, kind="ExternalInput").ap()
         for n, shp, dt in specs}
    xout = nc.dram_tensor("xout", [T, D], F32, kind="ExternalOutput").ap()
    stack = ExitStack()
    with stack:
        p = Prog(nc)
        ctx = Ctx(nc, p, stack)
        yb_fm = ctx.sb("yb_fm", [128, DC * T], BF16)
        acc = ctx.sb("acc", [128, NT * D], F32)
        xr = x.rearrange("(n p) d -> n p d", p=128)
        xor_ = xout.rearrange("(n p) d -> n p d", p=128)
        with ExitStack() as sc0:
            x_tm = ctx.sb("x_tm", [128, NT * D], F32, stack=sc0)
            for i in range(NT):
                p.dma("sync", x_tm[:, i * D:(i + 1) * D], xr[i],
                      writes=[(x_tm, f"t{i}")], tag=f"ldx{i}")
            trps = [ctx.ps_bf16(f"trps{i}", stack=sc0) for i in range(2)]
            ln_prologue(ctx, x_tm, list(range(NT)), lambda i: f"t{i}",
                        yb_fm, 0, T, "p", trps, sc0)
        emit_l3_ffn(ctx, W, yb_fm, acc)
        for g in range(4):
            with ExitStack() as esc:
                xg = ctx.sb(f"xg{g}", [128, 4 * D], F32, stack=esc)
                for j in range(4):
                    i = g * 4 + j
                    p.dma("sync", xg[:, j * D:(j + 1) * D], xr[i],
                          writes=[(xg, f"t{j}")], tag=f"eldx{i}")
                sandwich_epilogue(ctx, acc, [g * 4 + j for j in range(4)],
                                  xg, [0, 1, 2, 3], lambda i: f"t{i}",
                                  f"ep{g}", esc)
                for j in range(4):
                    i = g * 4 + j
                    p.dma("sync", xor_[i], xg[:, j * D:(j + 1) * D],
                          reads=[(xg, f"t{j}")], tag=f"estx{i}")
        p.finalize(stack)
    return nc


def _shard_tokens(a):
    return [np.ascontiguousarray(a[i * T:(i + 1) * T]) for i in range(N_CORES)]


def run_l1_ffn(x_flat, inputs):
    wmap = {
        "wgT": np.ascontiguousarray(np.asarray(inputs["L1_ffn_w_gate"]).T),
        "wuT": np.ascontiguousarray(np.asarray(inputs["L1_ffn_w_up"]).T),
        "wdT": np.ascontiguousarray(np.asarray(inputs["L1_ffn_w_down"]).T),
    }
    specs = [("wgT", [D, 3072], I32), ("wuT", [D, 3072], I32),
             ("wdT", [3072, D], I32)]
    return _run_sharded(
        "l1_ffn",
        lambda: build_tokenlocal_prog(specs, lambda ctx, W, yb, xb, tl, kf:
                                      emit_l1_ffn(ctx, W, yb, xb, tl, kf)),
        x_flat, wmap)


def run_l2_ffn(x_flat, inputs):
    f32 = lambda k: np.asarray(inputs[k], np.float32)
    wmap = {
        "base_WT": np.ascontiguousarray(f32("L2_ffn_base_W").T),
        "base_b": f32("L2_ffn_base_b").reshape(1, D),
        "sA_T": np.ascontiguousarray(f32("L2_ffn_shared_A").T),
        "sB_T": np.ascontiguousarray(f32("L2_ffn_shared_B").T),
        "A_stackT": np.ascontiguousarray(
            f32("L2_ffn_routed_A").transpose(2, 0, 1).reshape(D, 96)),
        "B_stackT": np.ascontiguousarray(
            f32("L2_ffn_routed_B").transpose(0, 2, 1).reshape(96, D)),
        "rw_T": np.ascontiguousarray(f32("L2_ffn_router_W").T),
        "router_b": f32("L2_ffn_router_b").reshape(1, 6),
    }
    specs = [("base_WT", [D, D], F32), ("base_b", [1, D], F32),
             ("sA_T", [D, 16], F32), ("sB_T", [16, D], F32),
             ("A_stackT", [D, 96], F32), ("B_stackT", [96, D], F32),
             ("rw_T", [D, 6], F32), ("router_b", [1, 6], F32)]
    return _run_sharded(
        "l2_ffn",
        lambda: build_tokenlocal_prog(specs, lambda ctx, W, yb, xb, tl, kf:
                                      emit_l2_ffn(ctx, W, yb, xb, tl, kf)),
        x_flat, wmap)


def run_l3_ffn(x_flat, inputs):
    f32 = lambda k: np.asarray(inputs[k], np.float32)
    wmap = {
        "gate_wT": np.ascontiguousarray(f32("L3_ffn_gate_w").T),
        "wgT": np.ascontiguousarray(f32("L3_ffn_w_gate").transpose(0, 2, 1)),
        "w1T": np.ascontiguousarray(f32("L3_ffn_w1").transpose(0, 2, 1)),
        "w2T": np.ascontiguousarray(f32("L3_ffn_w2").transpose(0, 2, 1)),
        "sgwT": np.ascontiguousarray(f32("L3_ffn_shared_w_gate").T),
        "s1wT": np.ascontiguousarray(f32("L3_ffn_shared_w1").T),
        "s2wT": np.ascontiguousarray(f32("L3_ffn_shared_w2").T),
        "sggT": np.ascontiguousarray(f32("L3_ffn_shared_gate_w").T),
    }
    return _run_sharded("l3_ffn", build_l3_ffn_prog, x_flat, wmap)



FUSED_NBLOCKS = 8


def _fused_weight_specs():
    s = []
    s += [("a0_" + n, shp, F32) for n, shp in
          [("w_qkv", [D, 1536]), ("b_qkv", [1, 1536]), ("w_fc", [512, D]),
           ("b_fc", [1, D]), ("gamma", [1, D]), ("beta", [1, D])]]
    s += [("f0_" + n, shp, F32) for n, shp in
          [("base_wT", [D, D]), ("base_b", [1, D]), ("gate_wT", [D, 10]),
           ("A_stackT", [D, 128]), ("B_stackT", [128, D])]]
    s += [("a1_" + n, shp, F32) for n, shp in
          [("cl", [D, 256]), ("cr", [D, 256]), ("thr", [1, 1]),
           ("gb", [1, 1])]]
    s += [("f1_" + n, shp, I32) for n, shp in
          [("wgT", [D, 3072]), ("wuT", [D, 3072]), ("wdT", [3072, D])]]
    s += [("a2_" + n, shp, F32) for n, shp in
          [("caT", [3072, D]), ("qw", [D, D]), ("kw", [D, D]),
           ("vw", [D, D]), ("cab", [1, 3072]), ("qb", [1, D]),
           ("kb", [1, D]), ("vb", [1, D]), ("o_w", [D, D]), ("o_b", [1, D])]]
    s += [("f2_" + n, shp, F32) for n, shp in
          [("base_WT", [D, D]), ("base_b", [1, D]), ("sA_T", [D, 16]),
           ("sB_T", [16, D]), ("A_stackT", [D, 96]), ("B_stackT", [96, D]),
           ("rw_T", [D, 6]), ("router_b", [1, 6])]]
    s += [("a3_" + n, shp, F32) for n, shp in
          [("wq", [D, 512]), ("wk", [D, 512]), ("wv", [D, 512]),
           ("bq", [1, 512]), ("bk", [1, 512]), ("bv", [1, 512]),
           ("o_w", [D, D]), ("o_b", [1, D]), ("m", [128, 1]),
           ("m2", [128, 1])]]
    s += [("f3_" + n, shp, F32) for n, shp in
          [("gate_wT", [D, 16]), ("wgT", [16, D, D]), ("w1T", [16, D, D]),
           ("w2T", [16, D, D]), ("sgwT", [D, 2048]), ("s1wT", [D, 2048]),
           ("s2wT", [2048, D]), ("sggT", [D, 1])]]
    return s


def build_fused_prog(nblocks=8):
    nc = bass.Bass("TRN2", target_bir_lowering=False, num_swdge_queues=4)
    x = nc.dram_tensor("x", [T, D], F32, kind="ExternalInput").ap()
    W = {}
    for n, shp, dt in _fused_weight_specs():
        W[n] = nc.dram_tensor(n, shp, dt, kind="ExternalInput").ap()
    xout = nc.dram_tensor("xout", [T, D], F32, kind="ExternalOutput").ap()
    Wp = lambda pre: {k[len(pre):]: v for k, v in W.items()
                      if k.startswith(pre)}
    ex0i = nc.dram_tensor("ex0i", [T, D], BF16, kind="Internal")
    ex0o = nc.dram_tensor("ex0o", [2 * T, D], BF16, kind="Internal")
    ex1i = nc.dram_tensor("ex1i", [T, D], BF16, kind="Internal")
    ex1o = nc.dram_tensor("ex1o", [2 * T, D], BF16, kind="Internal")
    ex2i = nc.dram_tensor("ex2i", [T, D], BF16, kind="Internal")
    ex2o = nc.dram_tensor("ex2o", [2 * T, D], BF16, kind="Internal")
    ex3i = nc.dram_tensor("ex3i", [2 * T, 512], BF16, kind="Internal")
    ex3o = nc.dram_tensor("ex3o", [4 * T, 512], BF16, kind="Internal")
    xsp = nc.dram_tensor("xsp", [T, D], F32, kind="Internal")
    xspr = xsp.ap().rearrange("(n p) d -> n p d", p=128)
    stack = ExitStack()
    with stack:
        p = Prog(nc)
        ctx = Ctx(nc, p, stack)
        slab = ctx.sb("slab", [128, NT * D], F32)
        qk = slab.bitcast(BF16)
        keyf = lambda i: f"t{i}"
        xr = x.rearrange("(n p) d -> n p d", p=128)
        xor_ = xout.rearrange("(n p) d -> n p d", p=128)
        for i in range(NT):
            p.dma("sync", slab[:, i * D:(i + 1) * D], xr[i],
                  writes=[(slab, f"t{i}")], tag=f"ldx{i}")

        def spill():
            for i in range(NT):
                p.dma("sync", xspr[i], slab[:, i * D:(i + 1) * D],
                      reads=[(slab, f"t{i}")], writes=[(xsp, i)],
                      tag=f"sp{i}")

        def reload():
            for i in range(NT):
                p.dma("sync", slab[:, i * D:(i + 1) * D], xspr[i],
                      reads=[(xsp, i)], writes=[(slab, f"t{i}")],
                      tag=f"rl{i}")

        def store_out():
            for i in range(NT):
                p.dma("sync", xor_[i], slab[:, i * D:(i + 1) * D],
                      reads=[(slab, f"t{i}")], tag=f"stx{i}")

        nb = 0
        with ExitStack() as b:
            emit_l0_attn(ctx, Wp("a0_"), slab, None, prefix="a0", scope=b,
                         co=ex0o, exch=(ex0i, ex0o))
        nb += 1
        if nblocks <= nb:
            store_out()
            p.finalize(stack)
            return nc
        with ExitStack() as b:
            yb = ctx.sb("f0_yb", [128, DC * T], BF16, stack=b)
            with ExitStack() as sc0:
                trps = [ctx.ps_bf16(f"f0_tr{i}", stack=sc0)
                        for i in range(2)]
                ln_prologue(ctx, slab, list(range(NT)), keyf, yb, 0, T,
                            "f0p", trps, sc0)
            emit_l0_ffn(ctx, Wp("f0_"), yb, slab, list(range(NT)), keyf)
        nb += 1
        if nblocks <= nb:
            store_out()
            p.finalize(stack)
            return nc
        with ExitStack() as b:
            emit_l1_attn(ctx, Wp("a1_"), slab, None, prefix="a1", scope=b,
                         co=ex1o, exch=(ex1i, ex1o))
        nb += 1
        if nblocks <= nb:
            store_out()
            p.finalize(stack)
            return nc
        with ExitStack() as b:
            yb = ctx.sb("f1_yb", [128, DC * T], BF16, stack=b)
            with ExitStack() as sc0:
                trps = [ctx.ps_bf16(f"f1_tr{i}", stack=sc0)
                        for i in range(2)]
                ln_prologue(ctx, slab, list(range(NT)), keyf, yb, 0, T,
                            "f1p", trps, sc0)
            emit_l1_ffn(ctx, Wp("f1_"), yb, slab, list(range(NT)), keyf)
        nb += 1
        if nblocks <= nb:
            store_out()
            p.finalize(stack)
            return nc
        with ExitStack() as b:
            out2 = ctx.sb("a2_out", [128, DC * T], BF16, stack=b)
            with ExitStack() as by:
                yb = ctx.sb("a2_yb", [128, DC * T], BF16, stack=by)
                with ExitStack() as sc0:
                    trps = [ctx.ps_bf16(f"a2_tr{i}", stack=sc0)
                            for i in range(2)]
                    ln_prologue(ctx, slab, list(range(NT)), keyf, yb, 0, T,
                                "a2p", trps, sc0)
                spill()
                q_ap = qk[:, 0:DC * T]
                k_ap = qk[:, DC * T:2 * DC * T]
                p.transition([q_ap, k_ap], [slab])
                emit_l2_attn(ctx, Wp("a2_"), yb, out2, by, prefix="a2",
                             q_fm=q_ap, k_fm=k_ap)
            p.transition([slab], [q_ap, k_ap])
            reload()
            ps_ = [ctx.ps_f32(f"a2o_ps{i}", stack=b) for i in range(4)]
            ow = ctx.sb("a2o_ow", [128, DC * D], BF16, stack=b)
            owr = W["a2_o_w"].rearrange("(c p) o -> c p o", p=128)
            for c in range(DC):
                p.dma("gpsimd", ow[:, c * D:(c + 1) * D], owr[c],
                      writes=[(ow, c)], tag=f"a2o_low{c}")
            obr = ctx.sb("a2o_obr", [1, D], F32, stack=b)
            p.dma("sync", obr[:], W["a2_o_b"], writes=[obr], tag="a2o_lob")
            ob_bc = ctx.broadcast_row(obr, D, "a2o_obbc", [ps_[3]],
                                      stack=b, row_reads=[obr])
            z_sb = ctx.sb("a2o_z", [128, 4 * D], F32, stack=b)
            for i in range(NT):
                j = i % 4
                for half in range(2):
                    bank = ps_[(2 * i + half) % 4]
                    pairs = [(out2[:, cc * T + i * 128:cc * T + (i + 1) * 128],
                              ow[:, cc * D + half * 512:
                                 cc * D + half * 512 + 512],
                              [(out2, f"{cc}_{(i * 128) // 512}"), (ow, cc)])
                             for cc in range(DC)]
                    mm_accum(p, bank, "s", pairs, 0, 512,
                             tag=f"a2o_mz{i}h{half}")
                    p.emit("vector", lambda e, bank=bank, j=j, half=half:
                           e.tensor_add(
                               z_sb[:, j * D + half * 512:
                                    j * D + half * 512 + 512],
                               bank[:, 0:512],
                               ob_bc[:, half * 512:half * 512 + 512]),
                           reads=[(bank, "s"), (ob_bc, f"c{half * 512}")],
                           writes=[(z_sb, f"t{j}")], tag=f"a2o_ez{i}{half}")
                if j == 3:
                    with ExitStack() as esc:
                        sandwich_epilogue(ctx, z_sb, [0, 1, 2, 3], slab,
                                          [i - 3, i - 2, i - 1, i], keyf,
                                          f"a2o_e{i // 4}", esc)
        nb += 1
        if nblocks <= nb:
            store_out()
            p.finalize(stack)
            return nc
        with ExitStack() as b:
            yb = ctx.sb("f2_yb", [128, DC * T], BF16, stack=b)
            with ExitStack() as sc0:
                trps = [ctx.ps_bf16(f"f2_tr{i}", stack=sc0)
                        for i in range(2)]
                ln_prologue(ctx, slab, list(range(NT)), keyf, yb, 0, T,
                            "f2p", trps, sc0)
            emit_l2_ffn(ctx, Wp("f2_"), yb, slab, list(range(NT)), keyf)
        nb += 1
        if nblocks <= nb:
            store_out()
            p.finalize(stack)
            return nc
        with ExitStack() as b:
            a3b_ow = ctx.sb("a3b_ow", [128, DC * D], BF16, stack=b)
            a3b_owr = W["a3_o_w"].rearrange("(c p) o -> c p o", p=128)
            for c in range(DC):
                p.dma("gpsimd", a3b_ow[:, c * D:(c + 1) * D], a3b_owr[c],
                      writes=[(a3b_ow, c)], tag=f"a3b_low{c}")
            a3b_obr = ctx.sb("a3b_obr", [1, D], F32, stack=b)
            p.dma("sync", a3b_obr[:], W["a3_o_b"], writes=[a3b_obr],
                  tag="a3b_lob")
            a3b_mc = ctx.sb("a3b_mc", [128, 1], F32, stack=b)
            a3b_mc2 = ctx.sb("a3b_mc2", [128, 1], F32, stack=b)
            p.dma("sync", a3b_mc[:], W["a3_m"], writes=[a3b_mc],
                  tag="a3b_lm")
            p.dma("sync", a3b_mc2[:], W["a3_m2"], writes=[a3b_mc2],
                  tag="a3b_lm2")
            with ExitStack() as sc0:
                ybo = ctx.sb("a3_ybo", [128, NT * D], BF16, stack=sc0)
                st3 = ctx.sb("a3_st", [128, 2 * NT], F32, stack=sc0)
                ln_stats(ctx, slab, list(range(NT)), keyf, st3, 1e-5,
                         "a3_s0", sc0)
                ln_apply_bf16(ctx, slab, list(range(NT)), keyf, st3, ybo,
                              list(range(NT)), "a3p")
                emit_exchange_ag(ctx, ybo, NT, ex2i, ex2o, "a3_ex")
            spill()
            p.transition([qk], [slab])
            with ExitStack() as bcore:
                emit_l3_attn_core_fused(ctx, Wp("a3_"), ex2o, ex3i, ex3o, qk,
                                        scope=bcore)
            p.transition([slab], [qk])
            reload()
            emit_l3_attn_proj_fused(ctx, Wp("a3_"), slab, ex3o,
                                    None, None, b, pre=(a3b_ow, a3b_obr,
                                                        a3b_mc, a3b_mc2))
        nb += 1
        if nblocks <= nb:
            store_out()
            p.finalize(stack)
            return nc
        with ExitStack() as b:
            yb = ctx.sb("f3_yb", [128, DC * T], BF16, stack=b)
            with ExitStack() as sc0:
                trps = [ctx.ps_bf16(f"f3_tr{i}", stack=sc0) for i in range(2)]
                ln_prologue(ctx, slab, list(range(NT)), keyf, yb, 0, T,
                            "f3p", trps, sc0)
            spill()
            p.transition([slab], [])
            emit_l3_ffn(ctx, Wp("f3_"), yb, slab)
            for g in range(4):
                with ExitStack() as esc:
                    xg = ctx.sb(f"f3_xg{g}", [128, 4 * D], F32, stack=esc)
                    for j in range(4):
                        i = g * 4 + j
                        p.dma("sync", xg[:, j * D:(j + 1) * D], xspr[i],
                              reads=[(xsp, i)], writes=[(xg, f"t{j}")],
                              tag=f"f3_eldx{i}")
                    sandwich_epilogue(ctx, slab, [g * 4 + j for j in range(4)],
                                      xg, [0, 1, 2, 3], lambda t: f"t{t}",
                                      f"f3_ep{g}", esc)
                    for j in range(4):
                        i = g * 4 + j
                        p.dma("sync", xor_[i], xg[:, j * D:(j + 1) * D],
                              reads=[(xg, f"t{j}")], tag=f"f3_estx{i}")
        p.finalize(stack)
    return nc


def _prep_fused_inmaps(inputs):
    f32 = lambda k: np.asarray(inputs[k], np.float32)
    com = {}
    com.update({"a0_" + k: v for k, v in {
        "w_qkv": np.ascontiguousarray(f32("L0_attn_w_qkv")),
        "b_qkv": f32("L0_attn_b_qkv").reshape(1, 1536),
        "w_fc": np.ascontiguousarray(f32("L0_attn_w_fc")),
        "b_fc": f32("L0_attn_b_fc").reshape(1, D),
        "gamma": f32("L0_attn_gamma").reshape(1, D),
        "beta": f32("L0_attn_beta").reshape(1, D)}.items()})
    com.update({"f0_" + k: v for k, v in {
        "base_wT": np.ascontiguousarray(f32("L0_ffn_base_w").T),
        "base_b": f32("L0_ffn_base_b").reshape(1, D),
        "gate_wT": np.ascontiguousarray(f32("L0_ffn_gate_w").T),
        "A_stackT": np.ascontiguousarray(
            f32("L0_ffn_lora_A").transpose(2, 0, 1).reshape(D, 128)),
        "B_stackT": np.ascontiguousarray(
            f32("L0_ffn_lora_B").transpose(0, 2, 1).reshape(128, D))}.items()})
    com.update({"a1_" + k: v for k, v in {
        "cl": np.ascontiguousarray(f32("L1_attn_compare_left")),
        "cr": np.ascontiguousarray(f32("L1_attn_compare_right")),
        "thr": f32("L1_attn_threshold").reshape(1, 1),
        "gb": f32("L1_attn_gate_bias").reshape(1, 1)}.items()})
    com.update({"f1_" + k: v for k, v in {
        "wgT": np.ascontiguousarray(np.asarray(inputs["L1_ffn_w_gate"]).T),
        "wuT": np.ascontiguousarray(np.asarray(inputs["L1_ffn_w_up"]).T),
        "wdT": np.ascontiguousarray(
            np.asarray(inputs["L1_ffn_w_down"]).T)}.items()})
    com.update({"a2_" + k: v for k, v in {
        "caT": np.ascontiguousarray(f32("L2_attn_c_attn_w").T),
        "qw": np.ascontiguousarray(f32("L2_attn_q_w")),
        "kw": np.ascontiguousarray(f32("L2_attn_k_w")),
        "vw": np.ascontiguousarray(f32("L2_attn_v_w")),
        "cab": f32("L2_attn_c_attn_b").reshape(1, 3072),
        "qb": f32("L2_attn_q_b").reshape(1, D),
        "kb": f32("L2_attn_k_b").reshape(1, D),
        "vb": f32("L2_attn_v_b").reshape(1, D),
        "o_w": np.ascontiguousarray(f32("L2_attn_o_w")),
        "o_b": f32("L2_attn_o_b").reshape(1, D)}.items()})
    com.update({"f2_" + k: v for k, v in {
        "base_WT": np.ascontiguousarray(f32("L2_ffn_base_W").T),
        "base_b": f32("L2_ffn_base_b").reshape(1, D),
        "sA_T": np.ascontiguousarray(f32("L2_ffn_shared_A").T),
        "sB_T": np.ascontiguousarray(f32("L2_ffn_shared_B").T),
        "A_stackT": np.ascontiguousarray(
            f32("L2_ffn_routed_A").transpose(2, 0, 1).reshape(D, 96)),
        "B_stackT": np.ascontiguousarray(
            f32("L2_ffn_routed_B").transpose(0, 2, 1).reshape(96, D)),
        "rw_T": np.ascontiguousarray(f32("L2_ffn_router_W").T),
        "router_b": f32("L2_ffn_router_b").reshape(1, 6)}.items()})
    com.update({"f3_" + k: v for k, v in {
        "gate_wT": np.ascontiguousarray(f32("L3_ffn_gate_w").T),
        "wgT": np.ascontiguousarray(f32("L3_ffn_w_gate").transpose(0, 2, 1)),
        "w1T": np.ascontiguousarray(f32("L3_ffn_w1").transpose(0, 2, 1)),
        "w2T": np.ascontiguousarray(f32("L3_ffn_w2").transpose(0, 2, 1)),
        "sgwT": np.ascontiguousarray(f32("L3_ffn_shared_w_gate").T),
        "s1wT": np.ascontiguousarray(f32("L3_ffn_shared_w1").T),
        "s2wT": np.ascontiguousarray(f32("L3_ffn_shared_w2").T),
        "sggT": np.ascontiguousarray(
            f32("L3_ffn_shared_gate_w").T)}.items()})
    aw = f32("L3_attn_attn_W")
    ab = f32("L3_attn_attn_b")
    x_flat = np.ascontiguousarray(
        np.asarray(inputs["L0_attn_x"], np.float32).reshape(-1, D))
    in_maps = []
    for c in range(N_CORES):
        heads = range(0, 8) if c % 2 == 0 else range(8, 16)
        qc = [h * 192 + s for h in heads for s in range(64)]
        kc = [h * 192 + 64 + s for h in heads for s in range(64)]
        vc = [h * 192 + 128 + s for h in heads for s in range(64)]
        m = 1.0 if c % 2 == 0 else 0.0
        im = dict(com)
        im.update({
            "x": np.ascontiguousarray(x_flat[c * T:(c + 1) * T]),
            "a3_wq": np.ascontiguousarray(aw[:, qc]),
            "a3_wk": np.ascontiguousarray(aw[:, kc]),
            "a3_wv": np.ascontiguousarray(aw[:, vc]),
            "a3_bq": np.ascontiguousarray(ab[qc]).reshape(1, 512),
            "a3_bk": np.ascontiguousarray(ab[kc]).reshape(1, 512),
            "a3_bv": np.ascontiguousarray(ab[vc]).reshape(1, 512),
            "a3_o_w": np.ascontiguousarray(f32("L3_attn_out_W")),
            "a3_o_b": f32("L3_attn_out_b").reshape(1, D),
            "a3_m": np.full((128, 1), m, np.float32),
            "a3_m2": np.full((128, 1), 1.0 - m, np.float32),
        })
        in_maps.append(im)
    return in_maps


def run_fused(inputs, nblocks=8):
    key = f"fused{nblocks}"
    if key not in _PROG_CACHE:
        _PROG_CACHE[key] = build_fused_prog(nblocks)
    nc = _PROG_CACHE[key]
    in_maps = _prep_fused_inmaps(inputs)
    res = run_bass_kernel_spmd(nc, in_maps, core_ids=list(range(N_CORES)))
    return np.concatenate([r["xout"] for r in res.results], axis=0)


def kernel(**inputs):
    x = run_fused(inputs)
    return np.ascontiguousarray(
        np.asarray(x, np.float32).reshape(4, 4096, D))
